# version 22
# baseline (speedup 1.0000x reference)
import contextlib
import math

import numpy as np
import ml_dtypes

import concourse.bass as bass
import concourse.mybir as mybir
from concourse.bass_utils import run_bass_kernel_spmd

F32 = mybir.dt.float32
BF16 = mybir.dt.bfloat16
I32 = mybir.dt.int32
AF = mybir.ActivationFunctionType
ALU = mybir.AluOpType
AX = mybir.AxisListType
BFDT = ml_dtypes.bfloat16

B, S, H = 4, 4096, 1024
HALF = S // 2
FF = 4096
NT_K = S // 128
NT_Q = HALF // 128
INV_SCALE = 1.0 / math.sqrt(512.0)
EPS = 1e-5


class Sched:

    ENGINES = ("t", "v", "s", "g")
    STREAM_OF_Q = {"y": "y", "gd": "g"}

    def __init__(self, nc, es):
        self.nc = nc
        self.es = es
        self.sems = {}
        self.sem_policy = {}
        self.cnt = {}
        for e in self.ENGINES:
            self._mk(e, "ord")
        self.pending = []
        self.last_writer = {}
        self.readers = {}
        self.hw = {}
        self.bar_vals = None
        self.bar_seen = set()

    def _mk(self, name, policy):
        if name not in self.sems:
            self.sems[name] = self.es.enter_context(self.nc.semaphore("sem_" + name))
            self.sem_policy[name] = policy
            self.cnt[name] = 0

    def dma_sem(self, name, policy="ord"):
        self._mk(name, policy)

    def op(self, eng, emit, reads=(), writes=(), tag="", sem=None, stream=None,
           small=True):
        if eng in ("y", "gd"):
            assert sem is not None, tag
            self._mk(sem, self.sem_policy.get(sem, "ord"))
            stream = self.STREAM_OF_Q[eng]
            step = 16
        else:
            sem = eng
            stream = eng
            step = 1
        i = len(self.pending)
        deps = set()
        for r in reads:
            w = self.last_writer.get(r)
            if w is not None:
                deps.add(w)
            self.readers.setdefault(r, []).append(i)
        for w in writes:
            pw = self.last_writer.get(w)
            if pw is not None:
                deps.add(pw)
            for rd in self.readers.get(w, ()):
                deps.add(rd)
            self.last_writer[w] = i
            self.readers[w] = []
        self.pending.append(dict(eng=eng, sem=sem, stream=stream, step=step,
                                 emit=emit, deps=deps, rawdeps=set(deps),
                                 tag=tag, small=small,
                                 need_inc=(eng in ("y", "gd")), inc_ord=None))

    def finalize_phase(self, block, final=False):
        ops = self.pending
        for i, o in enumerate(ops):
            kept = set()
            same_eng_deps = []
            for dd in o["deps"]:
                po = ops[dd]
                if po["eng"] == o["eng"] and po["eng"] not in ("y", "gd"):
                    same_eng_deps.append(dd)
                    continue
                if po["eng"] == o["eng"] and po["sem"] == o["sem"]:
                    continue
                kept.add(dd)
            o["deps"] = kept
            o["same_eng_deps"] = same_eng_deps
            for dd in kept:
                ops[dd]["need_inc"] = True
        last_flush = {}
        for i, o in enumerate(ops):
            o["pre_drain"] = False
            if o["eng"] in ("v", "s", "g"):
                e = o["eng"]
                for dd in o["same_eng_deps"]:
                    if ops[dd]["small"] and dd > last_flush.get(e, -1):
                        o["pre_drain"] = True
                        last_flush[e] = i - 1 if i else -1
                        break
                if o["need_inc"]:
                    last_flush[e] = i
        last_of = {}
        for i, o in enumerate(ops):
            last_of[o["eng"] if o["eng"] in self.ENGINES else o["sem"]] = i
        for i in last_of.values():
            ops[i]["need_inc"] = True
        totals_at = [None] * len(ops)
        run_tot = dict(self.cnt)
        for i, o in enumerate(ops):
            totals_at[i] = None
            if o["need_inc"]:
                run_tot[o["sem"]] = run_tot.get(o["sem"], 0) + 1
                o["inc_ord"] = run_tot[o["sem"]]
        pre_tot = {}
        run2 = dict(self.cnt)
        pref = []
        for i, o in enumerate(ops):
            pref.append(dict(run2))
            if o["need_inc"]:
                run2[o["sem"]] = run2.get(o["sem"], 0) + 1
        waits = [[] for _ in ops]
        for i, o in enumerate(ops):
            st = o["stream"]
            need = {}
            if self.bar_vals is not None and st not in self.bar_seen:
                self.bar_seen.add(st)
                for pe, v in self.bar_vals.items():
                    if v > 0:
                        need[pe] = v
            for dd in o["deps"]:
                po = ops[dd]
                assert po["inc_ord"] is not None, (po["tag"], o["tag"])
                if self.sem_policy.get(po["sem"], "ord") == "total":
                    v = pref[i].get(po["sem"], 0) * po["step"]
                    v = max(v, po["inc_ord"] * po["step"])
                else:
                    v = po["inc_ord"] * po["step"]
                if v > need.get(po["sem"], 0):
                    need[po["sem"]] = v
            for pe, v in need.items():
                k = (st, pe)
                if v > self.hw.get(k, 0):
                    self.hw[k] = v
                    waits[i].append((pe, v))
        per_stream = {"t": [], "v": [], "s": [], "g": [], "y": []}
        for i, o in enumerate(ops):
            per_stream[o["stream"]].append(i)
        sems = self.sems
        self.cnt = run_tot
        fin_vals = {nm: self.cnt.get(nm, 0) * (16 if self.sem_policy[nm] != "ord"
                    or nm not in self.ENGINES else 1) for nm in self.sems}
        fin_vals = {nm: self.cnt.get(nm, 0) * (1 if nm in self.ENGINES else 16)
                    for nm in self.sems}

        def run_stream(engh, stream):
            for i in per_stream[stream]:
                o = ops[i]
                for (pe, v) in waits[i]:
                    engh.wait_ge(sems[pe], v)
                if o.get("pre_drain"):
                    engh.drain()
                instr = o["emit"](engh)
                if o["need_inc"]:
                    if o["eng"] in ("v", "s", "g"):
                        engh.drain().then_inc(sems[o["sem"]],
                                                          o["step"])
                    else:
                        assert instr is not None, o["tag"]
                        instr.then_inc(sems[o["sem"]], o["step"])
            if final:
                for pe, v in fin_vals.items():
                    k = (stream, pe)
                    if v > self.hw.get(k, 0) and v > 0:
                        engh.wait_ge(sems[pe], v)

        @block.sync
        def _(eng):
            run_stream(eng, "y")

        @block.tensor
        def _(eng):
            run_stream(eng, "t")

        @block.vector
        def _(eng):
            run_stream(eng, "v")

        @block.scalar
        def _(eng):
            run_stream(eng, "s")

        @block.gpsimd
        def _(eng):
            run_stream(eng, "g")

        self.bar_vals = fin_vals
        self.bar_seen = set()
        self.pending = []
        self.last_writer = {}
        self.readers = {}


def build_nc(debug=False, n_ct=NT_Q):
    nc = bass.Bass("TRN2", target_bir_lowering=False, detect_race_conditions=False)

    d = {}
    for name, shape, dt in [
            ("x_full", [S, H], F32), ("x_own", [HALF, H], F32),
            ("wqT", [H, H], F32), ("wkT", [H, H], F32), ("wvT", [H, H], F32),
            ("bq2", [1, H], F32), ("bk2", [1, H], F32), ("bv2", [1, H], F32),
            ("w1T", [H, FF], F32), ("w2T", [FF, H], F32),
            ("b1r", [4, FF // 4], F32), ("b2r", [4, H // 4], F32),
            ("gamma_bc", [128, H], F32), ("beta_bc", [128, H], F32),
            ("sck_hi", [S, 512], F32),
            ("scq_hi", [HALF, 512], F32),
            ("ident_b", [128, 128], BF16), ("ident_f", [128, 128], F32)]:
        d[name] = nc.dram_tensor(name, shape, dt, kind="ExternalInput").ap()
    out_d = nc.dram_tensor("out", [HALF, H], F32, kind="ExternalOutput").ap()
    dbg = {}
    if debug:
        for nm, sh, dt_ in [("d_ssb", [128, S], F32), ("d_qro", [128, H], BF16),
                            ("d_lp2", [128, 8], F32),
                            ("d_xnC", [128, H], F32), ("d_ln2b", [128, H], F32),
                            ("d_z", [128, 1024], F32), ("d_nxq", [128, H], BF16),
                            ("d_cstats", [128, 64], F32),
                            ("d_xin0", [128, H], F32), ("d_xn0", [128, H], BF16),
                            ("d_kro0", [128, H], BF16), ("d_ps0", [128, 512], F32),
                            ("d_big0", [128, 8 * S], BF16), ("d_big1", [128, NT_K * H], BF16),
                            ("d_knorm2", [128, NT_K], F32), ("d_stats", [128, 64], F32),
                            ("d_partial", [HALF, H], F32), ("d_lparts", [128, 8], F32),
                            ("d_wq_mix", [128, 8 * H], BF16), ("d_hbuf", [128, FF], F32),
                            ("d_xnfmA", [128, H], BF16), ("d_w1q", [128, 8 * FF], BF16)]:
            dbg[nm] = nc.dram_tensor(nm, sh, dt_, kind="ExternalOutput").ap()
    part_d = nc.dram_tensor("partial", [HALF, H], F32).ap()
    w1q_d = nc.dram_tensor("w1q_sp", [128, 8 * FF], BF16).ap()
    w2q_d = nc.dram_tensor("w2q_sp", [128, 32 * H], BF16).ap()
    w1v = d["w1T"].rearrange("(a p) n -> p a n", p=128)
    w2v = d["w2T"].rearrange("(a p) n -> p a n", p=128)

    es = contextlib.ExitStack()
    with es:
        sch = Sched(nc, es)
        for nm, pol in (("ys", "total"), ("yr", "total"), ("bm", "total"),
                        ("gm", "total")):
            sch.dma_sem(nm, pol)

        def sb(stk, name, shape, dt):
            return stk.enter_context(nc.sbuf_tensor(name, shape, dt))

        big0 = sb(es, "big0", [128, 8 * S], BF16)
        big1 = sb(es, "big1", [128, NT_K * H], BF16)
        identb = sb(es, "identb", [128, 128], BF16)
        identf = sb(es, "identf", [128, 128], F32)
        onesf = sb(es, "onesf", [128, 128], F32)
        ones2 = sb(es, "ones2", [2, 128], BF16)
        stats = sb(es, "stats", [128, 64], F32)
        st2 = sb(es, "st2", [128, 16], F32)
        knorm2 = sb(es, "knorm2", [128, NT_K], F32)
        lparts = sb(es, "lparts", [128, 8], F32)
        wsums = sb(es, "wsums", [128, 68], F32)
        rowB = sb(es, "rowB", [1, 128], F32)
        ps_ = [es.enter_context(nc.psum_tensor(f"ps{i}", [128, 512], F32))
               for i in range(4)]
        psT = [es.enter_context(nc.psum_tensor(f"psT{i}", [128, 1024], BF16))
               for i in range(4)]

        C_KMAX2, C_MW1, C_MW2, C_SW1, C_SW2 = 16, 17, 18, 19, 20

        def stc(i):
            return stats[:, i:i + 1]

        y_dma = lambda dst, src, reads, writes, tag="", sem="ys": sch.op(
            "y", lambda e, a=dst, b=src: e.dma_start(out=a, in_=b), reads, writes,
            tag, sem=sem)
        g_dma = lambda dst, src, reads, writes, tag="", sem="gm": sch.op(
            "gd", lambda e, a=dst, b=src: e.dma_start(out=a, in_=b), reads, writes,
            tag, sem=sem)
        v_op = lambda fn, reads, writes, tag="": sch.op("v", fn, reads, writes, tag)
        s_op = lambda fn, reads, writes, tag="": sch.op("s", fn, reads, writes, tag)
        t_op = lambda fn, reads, writes, tag="": sch.op("t", fn, reads, writes, tag)
        g_op = lambda fn, reads, writes, tag="": sch.op("g", fn, reads, writes, tag)

        def emit_ln(xin, xtok, xn, xntok, scratch, sctok, c0, xnsum_col=None,
                    pre=""):
            tk = [f"stat{c0 + j}" for j in range(7)]
            v_op(lambda e: e.reduce_sum(stc(c0), xin[:], axis=AX.X),
                 [xtok], [tk[0]], pre + "ln.sum")
            v_op(lambda e: e.tensor_scalar(out=stc(c0 + 1), in0=stc(c0),
                                           scalar1=-1.0 / H, scalar2=None,
                                           op0=ALU.mult),
                 [tk[0]], [tk[1]], pre + "ln.neg")
            s_op(lambda e: e.activation(out=scratch[:], in_=xin[:], func=AF.Square,
                                        bias=stc(c0 + 1), scale=1.0,
                                        accum_out=stc(c0 + 2)),
                 [xtok, tk[1]], [sctok, tk[2]], pre + "ln.var")
            v_op(lambda e: e.tensor_scalar(out=stc(c0 + 3), in0=stc(c0 + 2),
                                           scalar1=1.0 / H, scalar2=EPS,
                                           op0=ALU.mult, op1=ALU.add),
                 [tk[2]], [tk[3]], pre + "ln.vps")
            v_op(lambda e: e.reciprocal(stc(c0 + 4), stc(c0 + 3)),
                 [tk[3]], [tk[4]], pre + "ln.rin")
            s_op(lambda e: e.activation(out=stc(c0 + 5), in_=stc(c0 + 4),
                                        func=AF.Sqrt),
                 [tk[4]], [tk[5]], pre + "ln.rsd")
            v_op(lambda e: e.tensor_tensor(out=stc(c0 + 6), in0=stc(c0 + 1),
                                           in1=stc(c0 + 5), op=ALU.mult),
                 [tk[1], tk[5]], [tk[6]], pre + "ln.nmr")
            acc = None if xnsum_col is None else stc(xnsum_col)
            if acc is None:
                s_op(lambda e: e.activation(out=xn[:], in_=xin[:], func=AF.Identity,
                                            scale=stc(c0 + 5), bias=stc(c0 + 6)),
                     [xtok, tk[5], tk[6]], [xntok], pre + "ln.xn")
            else:
                s_op(lambda e: e.activation(out=xn[:], in_=xin[:], func=AF.Identity,
                                            scale=stc(c0 + 5), bias=stc(c0 + 6),
                                            accum_out=acc),
                     [xtok, tk[5], tk[6]], [xntok, "xnsum7"], pre + "ln.xn")

        def emit_tr(src, stok, nchunks, dst, dtok, colf, tag, src_colf=None,
                    per_chunk=False, ev_alt=False):
            for j in range(nchunks):
                bi = j % 4
                sc = src_colf(j) if src_colf else j * 128
                t_op(lambda e, bi=bi, sc=sc: e.transpose(
                        psT[bi][:, 0:128], src[:, sc:sc + 128], identb[:]),
                     [stok, "identb"], [f"psT{bi}"], f"{tag}.T{j}")
                c0 = colf(j)
                wtok = f"{dtok}#{c0 // 128}" if per_chunk else dtok
                if ev_alt and j % 2 == 1:
                    v_op(lambda e, bi=bi, c0=c0: e.tensor_copy(
                            dst[:, c0:c0 + 128], psT[bi][:, 0:128]),
                         [f"psT{bi}"], [wtok], f"{tag}.E{j}")
                else:
                    s_op(lambda e, bi=bi, c0=c0: e.copy(dst[:, c0:c0 + 128],
                                                        psT[bi][:, 0:128]),
                         [f"psT{bi}"], [wtok], f"{tag}.E{j}")

        es_ab = contextlib.ExitStack()
        with es_ab:
            wq_mix = sb(es_ab, "wq_mix", [128, 8 * H], BF16)
            bq_rows = sb(es_ab, "bq_rows", [2, H], BF16)

            es_a = contextlib.ExitStack()
            with es_a:
                wk_mix = sb(es_a, "wk_mix", [128, 8 * H], BF16)
                wv_bf = sb(es_a, "wv_bf", [128, 8 * H], BF16)
                xinA = [sb(es_a, f"xinA{i}", [128, H], F32) for i in range(2)]
                schi = sb(es_a, "schi", [128, 512], F32)
                xnA = [sb(es_a, f"xnA{i}", [128, H], BF16)
                       for i in range(2)]
                xnfmA = sb(es_a, "xnfmA", [128, H], BF16)
                kro = sb(es_a, "kro", [128, H], BF16)
                wraw = sb(es_a, "wraw", [128, H], F32)
                brow = wraw[0:1, :]
                bmixf2 = xinA[0][0:1, :]
                brow2 = xnA[0][0:1, :]
                bk_rows = sb(es_a, "bk_rows", [2, H], BF16)
                bv_rows = sb(es_a, "bv_rows", [2, H], BF16)

                blkA = es_a.enter_context(nc.Block("phaseA"))

                y_dma(identb[:], d["ident_b"], [], ["identb"], "ld.idb")
                y_dma(identf[:], d["ident_f"], [], ["identf"], "ld.idf")
                g_op(lambda e: e.memset(onesf[:], 1.0), [], ["onesf"], "ms.1f")
                if debug:
                    g_op(lambda e: e.memset(stats[:], 0.0), [], [], "ms.dbg1")
                    g_op(lambda e: e.memset(st2[:], 0.0), [], [], "ms.dbg2")
                    g_op(lambda e: e.memset(wsums[:], 0.0), [], [], "ms.dbg3")
                    g_op(lambda e: e.memset(lparts[:], 0.0), [], [], "ms.dbg4")
                    g_op(lambda e: e.memset(knorm2[:], 0.0), [], [], "ms.dbg5")
                g_op(lambda e: e.memset(ones2[:], 1.0), [], ["ones2"], "ms.12")

                def emit_wprep(src, dst, dtok, mixed, tag):
                    for dj in range(8):
                        y_dma(wraw[:], src[dj * 128:(dj + 1) * 128, :],
                              ["wraw"], ["wraw"], f"{tag}.ld{dj}", sem="yw")
                        if mixed:
                            v_op(lambda e, dj=dj: e.tensor_tensor(
                                    out=dst[:, dj * H:dj * H + 512],
                                    in0=wraw[:, 0:512], in1=wraw[:, 512:H],
                                    op=ALU.subtract),
                                 ["wraw"], [dtok], f"{tag}.s{dj}")
                            v_op(lambda e, dj=dj: e.tensor_tensor(
                                    out=dst[:, dj * H + 512:(dj + 1) * H],
                                    in0=wraw[:, 0:512], in1=wraw[:, 512:H],
                                    op=ALU.add),
                                 ["wraw"], [dtok], f"{tag}.a{dj}")
                        else:
                            v_op(lambda e, dj=dj: e.tensor_copy(
                                    dst[:, dj * H:(dj + 1) * H], wraw[:]),
                                 ["wraw"], [dtok], f"{tag}.c{dj}")

                def emit_brows(src, rows, rtok, mixed, tag):
                    y_dma(brow, src, ["wraw"], ["wraw"], f"{tag}.ld", sem="yw")
                    if mixed:
                        v_op(lambda e: e.tensor_tensor(out=bmixf2[:, 0:512],
                                                       in0=brow[:, 0:512],
                                                       in1=brow[:, 512:H],
                                                       op=ALU.subtract),
                             ["wraw"], ["xinA0"], f"{tag}.s")
                        v_op(lambda e: e.tensor_tensor(out=bmixf2[:, 512:H],
                                                       in0=brow[:, 0:512],
                                                       in1=brow[:, 512:H],
                                                       op=ALU.add),
                             ["wraw"], ["xinA0"], f"{tag}.a")
                        srcb, srct = bmixf2, "xinA0"
                    else:
                        srcb, srct = brow, "wraw"
                    v_op(lambda e, srcb=srcb: e.tensor_copy(rows[0:1, :], srcb),
                         [srct], [rtok], f"{tag}.hi")
                    v_op(lambda e, srcb=srcb: e.scalar_tensor_tensor(
                            out=brow2, in0=srcb, scalar=1.0,
                            in1=rows[0:1, :], op0=ALU.mult, op1=ALU.subtract),
                         [srct, rtok], ["xnA0"], f"{tag}.lo")
                    g_dma(rows[1:2, :], brow2, ["xnA0"], [rtok], f"{tag}.mv",
                          sem="bm")

                emit_wprep(d["wkT"], wk_mix, "wk_mix", True, "wk")
                emit_wprep(d["wvT"], wv_bf, "wv_bf", False, "wv")
                emit_wprep(d["wqT"], wq_mix, "wq_mix", True, "wq")
                emit_brows(d["bk2"][:], bk_rows, "bk_rows", True, "bk")
                emit_brows(d["bv2"][:], bv_rows, "bv_rows", False, "bv")
                emit_brows(d["bq2"][:], bq_rows, "bq_rows", True, "bq")

                def a_xload(kt):
                    xin, xtok = xinA[kt % 2], f"xinA{kt % 2}"
                    y_dma(xin[:], d["x_full"][kt * 128:(kt + 1) * 128, :],
                          [xtok], [xtok], f"A{kt}.x", sem=f"yx{kt % 2}")

                def a_lnops(kt):
                    xin, xtok = xinA[kt % 2], f"xinA{kt % 2}"
                    emit_ln(xin, xtok, xnA[kt % 2], f"xnA{kt % 2}",
                            wraw[:, 0:H], "wraw", 0, pre=f"A{kt}.")

                def a_pe(kt):
                    xn, xntok = xnA[kt % 2], f"xnA{kt % 2}"
                    y_dma(schi[:], d["sck_hi"][kt * 128:(kt + 1) * 128, :],
                          ["schi"], ["schi"], f"A{kt}.h", sem="ysh")
                    if debug and kt == 0:
                        g_dma(dbg["d_xin0"], xinA[0][:], ["xinA0"], [], "dbg.xin0")
                        g_dma(dbg["d_xn0"], xn[:], [xntok], [], "dbg.xn0")
                    emit_tr(xn, xntok, 8, xnfmA, "xnfmA", lambda j: j * 128,
                            f"A{kt}.xn", per_chunk=True, ev_alt=True)
                    for wi, (bks, wsb, wtok, btok, brows) in enumerate((
                            ((0, 1), wk_mix, "wk_mix", "bk_rows", bk_rows),
                            ((2, 3), wv_bf, "wv_bf", "bv_rows", bv_rows))):
                        for half, bk_i in enumerate(bks):
                            for dj in range(8):
                                t_op(lambda e, dj=dj, bk_i=bk_i, wsb=wsb, half=half:
                                     e.matmul(ps_[bk_i][:],
                                              xnfmA[:, dj * 128:(dj + 1) * 128],
                                              wsb[:, dj * H + half * 512:
                                                  dj * H + half * 512 + 512],
                                              start=(dj == 0), stop=False),
                                     [f"xnfmA#{dj}", wtok], [f"ps{bk_i}"],
                                     f"A{kt}.mm{wtok}{half}.{dj}")
                            t_op(lambda e, bk_i=bk_i, brows=brows, half=half:
                                 e.matmul(ps_[bk_i][:], ones2[:],
                                          brows[:, half * 512:half * 512 + 512],
                                          start=False, stop=True),
                                 ["ones2", btok], [f"ps{bk_i}"],
                                 f"A{kt}.b{wtok}{half}")
                        if wi == 0:
                            if kt + 2 < NT_K:
                                a_xload(kt + 2)
                            if kt + 1 < NT_K:
                                a_lnops(kt + 1)
                    v_op(lambda e: e.tensor_tensor(out=kro[:, 0:512], in0=ps_[0][:],
                                                   in1=schi[:], op=ALU.mult),
                         ["ps0", "schi"], ["kro"], f"A{kt}.r0")
                    v_op(lambda e: e.tensor_tensor(out=kro[:, 512:H], in0=ps_[1][:],
                                                   in1=schi[:], op=ALU.mult),
                         ["ps1", "schi"], ["kro"], f"A{kt}.r1")
                    v_op(lambda e, kt=kt: e.tensor_copy(
                            big1[:, kt * H:kt * H + 512], ps_[2][:]),
                         ["ps2"], ["big1"], f"A{kt}.v0")
                    v_op(lambda e, kt=kt: e.tensor_copy(
                            big1[:, kt * H + 512:(kt + 1) * H], ps_[3][:]),
                         ["ps3"], ["big1"], f"A{kt}.v1")
                    if debug and kt == 0:
                        g_dma(dbg["d_kro0"], kro[:], ["kro"], [], "dbg.kro0")
                        g_dma(dbg["d_ps0"], None, [], [], "")
                        sch.pending.pop()
                    emit_tr(kro, "kro", 8, big0, "big0",
                            lambda j, kt=kt: j * S + kt * 128, f"A{kt}.k",
                            ev_alt=True)

                a_xload(0)
                a_xload(1)
                a_lnops(0)
                for kt in range(NT_K):
                    a_pe(kt)

                sch.finalize_phase(blkA)

            es_b = contextlib.ExitStack()
            with es_b:
                xinB = [sb(es_b, f"xinB{i}", [128, H], F32) for i in range(2)]
                schiB = sb(es_b, "schiB", [128, 512], F32)
                xnB = sb(es_b, "xnB", [128, H], BF16)
                xnfmB = sb(es_b, "xnfmB", [128, H], BF16)
                qro = sb(es_b, "qro", [128, H], BF16)
                qfm = [sb(es_b, f"qfm{i}", [128, H], BF16) for i in range(2)]
                probs = [sb(es_b, f"probs{i}", [128, 512], BF16) for i in range(4)]
                pT = [sb(es_b, f"pT{i}", [128, 128], BF16) for i in range(4)]
                outB = [sb(es_b, "outB0", [128, H], F32)] * 2
                wchd = [sb(es_b, f"wch{i}", [128, 1024], F32) for i in range(2)]
                ssb = sb(es_b, "ssb", [128, S], F32)
                wst = [sb(es_b, "wst0", [128, 1024], BF16)] * 2
                wm1 = sb(es_b, "wm1", [128, 1024], BF16)
                wm2 = sb(es_b, "wm2", [128, 1024], BF16)

                blkB = es_b.enter_context(nc.Block("phaseB"))

                def emit_abs_chunk(src, col, width, tag, p):
                    w, wt = wchd[p], f"wch{p}"
                    g_dma(w[:, 0:width], src, [wt], [wt], f"{tag}.ld",
                          sem=f"gw{p}")
                    v_op(lambda e, w=w, col=col, width=width: e.reduce_sum(
                            wsums[:, col:col + 1], w[:, 0:width], axis=AX.X,
                            apply_absolute_value=True),
                         [wt], [f"ws{col}"], f"{tag}.abs")

                def emit_scales(first_col, n_col, tot_col, denom, out_col, tag):
                    v_op(lambda e, first_col=first_col, n_col=n_col, tot_col=tot_col:
                         e.reduce_sum(st2[:, tot_col:tot_col + 1],
                                      wsums[:, first_col:first_col + n_col],
                                      axis=AX.X),
                         [f"ws{c}" for c in range(first_col, first_col + n_col)],
                         [f"tot{tag}"], f"{tag}.tot")
                    t_op(lambda e, tot_col=tot_col: e.matmul(
                            ps_[2][:, 0:1], onesf[:, 0:128],
                            st2[:, tot_col:tot_col + 1], start=True, stop=True),
                         [f"tot{tag}", "onesf"], ["ps2"], f"{tag}.bc")
                    v_op(lambda e, denom=denom, out_col=out_col: e.tensor_scalar(
                            out=stc(out_col), in0=ps_[2][:, 0:1],
                            scalar1=1.0 / denom, scalar2=EPS,
                            op0=ALU.mult, op1=ALU.max),
                         ["ps2"], [f"mwc{tag}"], f"{tag}.mwc")
                    v_op(lambda e, out_col=out_col: e.reciprocal(
                            stc(out_col + 2), stc(out_col)),
                         [f"mwc{tag}"], [f"sw{tag}"], f"{tag}.sw")

                def emit_wq_load(src, parity, tag):
                    w, wt = wchd[parity], f"wch{parity}"
                    g_dma(w[:, 0:1024], src, [wt], [wt], f"{tag}.ld",
                          sem=f"gw{parity}")

                def emit_wquant_chunk(src, dst_dram, parity, mwtag, tag):
                    w, wt = wchd[parity], f"wch{parity}"
                    ws, wst_t = wst[parity], "wst0"
                    v_op(lambda e, w=w, mwtag=mwtag: e.tensor_scalar(
                            out=wm1[:], in0=w[:, 0:1024],
                            scalar1=stc(C_SW1 if mwtag == "w1" else C_SW2),
                            scalar2=0.5, op0=ALU.mult, op1=ALU.is_ge),
                         [wt, f"sw{mwtag}"], ["wm1"], f"{tag}.m1")
                    v_op(lambda e, w=w, mwtag=mwtag: e.tensor_scalar(
                            out=wm2[:], in0=w[:, 0:1024],
                            scalar1=stc(C_SW1 if mwtag == "w1" else C_SW2),
                            scalar2=-0.5, op0=ALU.mult, op1=ALU.is_le),
                         [wt, f"sw{mwtag}"], ["wm2"], f"{tag}.m2")
                    v_op(lambda e, ws=ws: e.tensor_tensor(
                            out=ws[:], in0=wm2[:], in1=wm1[:],
                            op=ALU.subtract),
                         ["wm1", "wm2"], [wst_t], f"{tag}.nq")
                    g_dma(dst_dram, ws[:], [wst_t],
                          [wst_t], f"{tag}.sp", sem=f"gs{parity}")

                abs_jobs = []
                for a in range(8):
                    for q4 in range(4):
                        abs_jobs.append((w1v[:, a, q4 * 1024:(q4 + 1) * 1024],
                                         a * 4 + q4, 1024, f"w1a{a}.{q4}"))
                for a in range(32):
                    abs_jobs.append((w2v[:, a, :], 32 + a, 1024, f"w2a{a}"))

                quant_jobs = []
                for a in range(8):
                    for q4 in range(4):
                        quant_jobs.append((w1v[:, a, q4 * 1024:(q4 + 1) * 1024],
                                           w1q_d[:, (a * 4 + q4) * 1024:
                                                 (a * 4 + q4 + 1) * 1024],
                                           "w1", f"qw1.{a}.{q4}"))
                for a in range(32):
                    quant_jobs.append((w2v[:, a, :],
                                       w2q_d[:, a * 1024:(a + 1) * 1024],
                                       "w2", f"qw2.{a}"))

                abs_i = [0]
                quant_i = [0]
                quant_ld_i = [0]

                def drip_weight_work(n_abs, n_quant):
                    for _ in range(n_abs):
                        if abs_i[0] < len(abs_jobs):
                            srcv, col, width, tag = abs_jobs[abs_i[0]]
                            emit_abs_chunk(srcv, col, width, tag, abs_i[0] % 2)
                            abs_i[0] += 1
                            if abs_i[0] == 32:
                                emit_scales(0, 32, 1, float(H) * FF, C_MW1, "w1")
                            if abs_i[0] == 64:
                                emit_scales(32, 32, 2, float(H) * FF, C_MW2, "w2")
                    for _ in range(n_quant):
                        if abs_i[0] >= 64 and quant_i[0] < len(quant_jobs):
                            i = quant_i[0]
                            while quant_ld_i[0] <= min(i + 1,
                                                       len(quant_jobs) - 1):
                                srcv, dst, mwtag, tag = \
                                    quant_jobs[quant_ld_i[0]]
                                emit_wq_load(srcv, quant_ld_i[0] % 2, tag)
                                quant_ld_i[0] += 1
                            srcv, dst, mwtag, tag = quant_jobs[i]
                            emit_wquant_chunk(srcv, dst, i % 2, mwtag, tag)
                            quant_i[0] += 1

                def stage1_ln(qt):
                    xin, xtok = xinB[qt % 2], f"xinB{qt % 2}"
                    y_dma(xin[:], d["x_own"][qt * 128:(qt + 1) * 128, :],
                          [xtok], [xtok], f"B{qt}.x", sem=f"yb{qt % 2}")
                    y_dma(schiB[:], d["scq_hi"][qt * 128:(qt + 1) * 128, :],
                          ["schiB"], ["schiB"], f"B{qt}.h", sem="ybh")
                    emit_ln(xin, xtok, xnB, "xnB", qro, "qro", 0, pre=f"B{qt}.")

                def stage1_pe(qt):
                    qf, qftok = qfm[qt % 2], f"qfm{qt % 2}"
                    emit_tr(xnB, "xnB", 8, xnfmB, "xnfmB", lambda j: j * 128,
                            f"B{qt}.xn", per_chunk=True, ev_alt=True)
                    for half, bk_i in enumerate((0, 1)):
                        for dj in range(8):
                            t_op(lambda e, dj=dj, bk_i=bk_i, half=half: e.matmul(
                                    ps_[bk_i][:], xnfmB[:, dj * 128:(dj + 1) * 128],
                                    wq_mix[:, dj * H + half * 512:
                                           dj * H + half * 512 + 512],
                                    start=(dj == 0), stop=False),
                                 [f"xnfmB#{dj}", "wq_mix"], [f"ps{bk_i}"],
                                 f"B{qt}.q{half}.{dj}")
                        t_op(lambda e, bk_i=bk_i, half=half: e.matmul(
                                ps_[bk_i][:], ones2[:],
                                bq_rows[:, half * 512:half * 512 + 512],
                                start=False, stop=True),
                             ["ones2", "bq_rows"], [f"ps{bk_i}"], f"B{qt}.qb{half}")
                    v_op(lambda e: e.tensor_tensor(out=qro[:, 0:512], in0=ps_[0][:],
                                                   in1=schiB[:], op=ALU.mult),
                         ["ps0", "schiB"], ["qro"], f"B{qt}.r0")
                    v_op(lambda e: e.tensor_tensor(out=qro[:, 512:H], in0=ps_[1][:],
                                                   in1=schiB[:], op=ALU.mult),
                         ["ps1", "schiB"], ["qro"], f"B{qt}.r1")
                    emit_tr(qro, "qro", 8, qf, qftok, lambda j: j * 128,
                            f"B{qt}.q", per_chunk=True, ev_alt=True)

                def emit_scores(qt, ln_cb=None):
                    qf, qftok = qfm[qt % 2], f"qfm{qt % 2}"
                    for kc in range(8):
                        bk_i = kc % 4
                        for ej in range(8):
                            t_op(lambda e, ej=ej, kc=kc, bk_i=bk_i, qf=qf: e.matmul(
                                    ps_[bk_i][:], qf[:, ej * 128:(ej + 1) * 128],
                                    big0[:, ej * S + kc * 512:ej * S + kc * 512 + 512],
                                    start=(ej == 0), stop=(ej == 7)),
                                 [f"{qftok}#{ej}", "big0"], [f"ps{bk_i}"],
                                 f"B{qt}.s{kc}.{ej}")
                        s_op(lambda e, kc=kc, bk_i=bk_i: e.copy(
                                ssb[:, kc * 512:(kc + 1) * 512], ps_[bk_i][:]),
                             [f"ps{bk_i}"], [f"ssb{kc}"], f"B{qt}.sev{kc}")
                        v_op(lambda e, kc=kc: e.reduce_max(
                                st2[:, 8 + kc:9 + kc],
                                ssb[:, kc * 512:(kc + 1) * 512], axis=AX.X),
                             [f"ssb{kc}"], [f"smx{kc}"], f"B{qt}.smx{kc}")
                        if kc == 2 and ln_cb is not None:
                            ln_cb()
                        drip_weight_work(1, 0)

                def stage2(qt):
                    xin, xtok = xinB[qt % 2], f"xinB{qt % 2}"
                    v_op(lambda e: e.reduce_max(stc(10), st2[:, 8:16],
                                                axis=AX.X),
                         [f"smx{kc}" for kc in range(8)], ["qM"],
                         f"B{qt}.rmax")
                    s_op(lambda e: e.activation(out=stc(11), in_=stc(10),
                                                func=AF.Identity,
                                                scale=-INV_SCALE),
                         ["qM"], ["ebias"], f"B{qt}.eb")
                    for kc in range(8):
                        bk_i = kc % 4
                        s_op(lambda e, kc=kc, bk_i=bk_i: e.activation(
                                out=probs[bk_i][:],
                                in_=ssb[:, kc * 512:(kc + 1) * 512], func=AF.Exp,
                                scale=INV_SCALE, bias=stc(11),
                                accum_out=lparts[:, kc:kc + 1]),
                             [f"ssb{kc}", "ebias"], [f"probs{bk_i}", "lparts"],
                             f"B{qt}.e{kc}")
                        for sub in range(4):
                            t_op(lambda e, sub=sub, bk_i=bk_i: e.transpose(
                                    psT[sub][:, 0:128],
                                    probs[bk_i][:, sub * 128:(sub + 1) * 128],
                                    identb[:]),
                                 [f"probs{bk_i}", "identb"], [f"psT{sub}"],
                                 f"B{qt}.pT{kc}.{sub}")
                            s_op(lambda e, sub=sub: e.copy(
                                    pT[sub][:], psT[sub][:, 0:128]),
                                 [f"psT{sub}"], [f"pT{sub}"],
                                 f"B{qt}.pE{kc}.{sub}")
                        for sub in range(4):
                            kt64 = kc * 4 + sub
                            for half, bk2 in enumerate((2, 3)):
                                t_op(lambda e, sub=sub, kt64=kt64, half=half,
                                     bk2=bk2, first=(kt64 == 0):
                                     e.matmul(ps_[bk2][:], pT[sub][:],
                                              big1[:, kt64 * H + half * 512:
                                                   kt64 * H + half * 512 + 512],
                                              start=first, stop=(kt64 == 31)),
                                     [f"pT{sub}", "big1"], [f"ps{bk2}"],
                                     f"B{qt}.pv{kc}.{sub}.{half}")
                        drip_weight_work(0, 1)
                    v_op(lambda e: e.reduce_sum(stc(12), lparts[:, 0:8], axis=AX.X),
                         ["lparts"], ["lsum"], f"B{qt}.l")
                    v_op(lambda e: e.reciprocal(stc(13), stc(12)),
                         ["lsum"], ["rl"], f"B{qt}.rl")
                    ob, obt = outB[0], "outB0"
                    v_op(lambda e, ob=ob, xin=xin: e.scalar_tensor_tensor(
                            out=ob[:, 0:512], in0=ps_[2][:], scalar=stc(13),
                            in1=xin[:, 0:512], op0=ALU.mult, op1=ALU.add),
                         ["ps2", "rl", xtok], [obt], f"B{qt}.o0")
                    v_op(lambda e, ob=ob, xin=xin: e.scalar_tensor_tensor(
                            out=ob[:, 512:H], in0=ps_[3][:], scalar=stc(13),
                            in1=xin[:, 512:H], op0=ALU.mult, op1=ALU.add),
                         ["ps3", "rl", xtok], [obt], f"B{qt}.o1")
                    g_dma(part_d[qt * 128:(qt + 1) * 128, :], ob[:], [obt], [obt],
                          f"B{qt}.st", sem=f"go{qt % 2}")

                stage1_ln(0)
                stage1_pe(0)
                for qt in range(NT_Q):
                    emit_scores(qt, ln_cb=(
                        (lambda q=qt: stage1_ln(q + 1))
                        if qt + 1 < NT_Q else None))
                    if qt + 1 < NT_Q:
                        stage1_pe(qt + 1)
                    stage2(qt)
                drip_weight_work(64, 64)

                if debug:
                    g_dma(dbg["d_ssb"], ssb[:], ["ssb"], [], "dbgBs")
                    g_dma(dbg["d_qro"], qro[:], ["qro"], [], "dbgBq")
                    g_dma(dbg["d_lp2"], lparts[:], ["lparts"], [], "dbgBl")
                    g_dma(dbg["d_stats"], stats[:], ["ebias"], [], "dbgB0")
                    g_dma(dbg["d_lparts"], lparts[:], ["lparts"], [], "dbgB1")
                    g_dma(dbg["d_w1q"], w1q_d[:], [], [], "dbgB2")
                sch.finalize_phase(blkB)

        es_c = contextlib.ExitStack()
        with es_c:
            gamma = sb(es_c, "gamma", [128, H], F32)
            beta = sb(es_c, "beta", [128, H], F32)
            xinC = [sb(es_c, f"xinC{i}", [128, H], F32) for i in range(3)]
            xpart = xinC
            xnC = sb(es_c, "xnC", [128, H], F32)
            ln2b = sb(es_c, "ln2b", [128, H], F32)
            zbuf = sb(es_c, "zbuf", [128, 1024], F32)
            cibuf = sb(es_c, "cibuf", [128, 1024], I32)
            m1b = sb(es_c, "m1b", [128, 1024], BF16)
            nxq = sb(es_c, "nxq", [128, H], BF16)
            xqfm = sb(es_c, "xqfm", [128, H], BF16)
            hbuf = sb(es_c, "hbuf", [128, FF], F32)
            nhqd = [sb(es_c, f"nhq{i}", [128, 1024], BF16) for i in range(2)]
            hqfm = sb(es_c, "hqfm", [128, 2048], BF16)
            b1rows = sb(es_c, "b1rows", [4, FF], BF16)
            b2rows = sb(es_c, "b2rows", [4, H], BF16)
            blhs1 = sb(es_c, "blhs1", [4, 128], BF16)
            blhs2 = sb(es_c, "blhs2", [4, 128], BF16)
            stg4 = zbuf[0:4, :]
            stg4h = m1b[0:4, :]
            stg4l = nhqd[0][0:4, :]

            blkC = es_c.enter_context(nc.Block("phaseC"))

            for j in range(4):
                y_dma(big0[:, j * 8192:(j + 1) * 8192],
                      w1q_d[:, j * 8192:(j + 1) * 8192],
                      ["big0"], ["big0"], f"C.w1q{j}", sem="yr")
            for j in range(4):
                y_dma(big1[:, j * 8192:(j + 1) * 8192],
                      w2q_d[:, j * 8192:(j + 1) * 8192],
                      ["big1"], ["big1"], f"C.w2q{j}", sem="yr")
            y_dma(gamma[:], d["gamma_bc"], [], ["gamma"], "C.g", sem="yr")
            y_dma(beta[:], d["beta_bc"], [], ["beta"], "C.b", sem="yr")

            def emit_bsplit(src, nq, rows, rtok, tag):
                w = nq
                y_dma(stg4[:, 0:w], src, ["zbuf"], ["zbuf"], f"{tag}.ld", sem="yr")
                v_op(lambda e, w=w: e.tensor_copy(stg4h[:, 0:w], stg4[:, 0:w]),
                     ["zbuf"], ["m1b"], f"{tag}.hi")
                v_op(lambda e, w=w: e.scalar_tensor_tensor(
                        out=stg4l[:, 0:w], in0=stg4[:, 0:w], scalar=1.0,
                        in1=stg4h[:, 0:w], op0=ALU.mult, op1=ALU.subtract),
                     ["zbuf", "m1b"], ["nhq0"], f"{tag}.lo")
                full = 4 * w
                for r, srct, srcb in ((0, "m1b", stg4h), (1, "m1b", stg4h),
                                      (2, "nhq0", stg4l), (3, "nhq0", stg4l)):
                    for p in range(4):
                        g_dma(rows[r:r + 1, p * w:(p + 1) * w],
                              srcb[p:p + 1, 0:w],
                              [srct], [rtok], f"{tag}.mv{r}.{p}")

            emit_bsplit(d["b1r"], FF // 4, b1rows, "b1rows", "b1")
            emit_bsplit(d["b2r"], H // 4, b2rows, "b2rows", "b2")

            CQ = 21

            st2b = sb(es_c, "st2b", [128, 8], BF16)

            def emit_aquant(src, stoks, sx_tok, sx_col, width, off, nxq_out,
                            nxq_tok, tag):
                sl = slice(off, off + width)
                v_op(lambda e: e.scalar_tensor_tensor(
                        out=zbuf[:, 0:width], in0=src[:, sl],
                        scalar=stc(sx_col), in1=src[:, sl],
                        op0=ALU.mult, op1=ALU.bypass),
                     list(stoks) + [sx_tok], ["zbuf"], f"{tag}.z")
                v_op(lambda e: e.tensor_copy(cibuf[:, 0:width], zbuf[:, 0:width]),
                     ["zbuf"], ["cibuf"], f"{tag}.ci")
                v_op(lambda e: e.scalar_tensor_tensor(
                        out=zbuf[:, 0:width], in0=zbuf[:, 0:width], scalar=1.0,
                        in1=cibuf[:, 0:width], op0=ALU.mult, op1=ALU.subtract),
                     ["zbuf", "cibuf"], ["zbuf"], f"{tag}.d")
                v_op(lambda e: e.tensor_scalar(
                        out=m1b[:, 0:width], in0=zbuf[:, 0:width],
                        scalar1=0.5, scalar2=None, op0=ALU.is_ge),
                     ["zbuf"], ["m1b"], f"{tag}.m1")
                v_op(lambda e: e.scalar_tensor_tensor(
                        out=m1b[:, 0:width], in0=zbuf[:, 0:width], scalar=-0.5,
                        in1=m1b[:, 0:width], op0=ALU.is_le, op1=ALU.subtract),
                     ["zbuf", "m1b"], ["m1b"], f"{tag}.mc")
                v_op(lambda e: e.scalar_tensor_tensor(
                        out=nxq_out, in0=cibuf[:, 0:width], scalar=-1.0,
                        in1=m1b[:, 0:width], op0=ALU.mult, op1=ALU.add),
                     ["cibuf", "m1b"], [nxq_tok], f"{tag}.nx")

            def emit_split(src_col, srctok, c0, pre):
                s_op(lambda e: e.copy(st2b[:, c0:c0 + 1], stc(src_col)),
                     [srctok], [f"{pre}hb"], f"{pre}.hb")
                s_op(lambda e: e.copy(st2b[:, c0 + 2:c0 + 3],
                                      st2b[:, c0:c0 + 1]),
                     [f"{pre}hb"], [f"{pre}hb2"], f"{pre}.hb2")
                s_op(lambda e: e.copy(st2[:, 5:6], st2b[:, c0:c0 + 1]),
                     [f"{pre}hb"], [f"{pre}hf"], f"{pre}.hf")
                s_op(lambda e: e.activation(out=st2[:, 6:7], in_=st2[:, 5:6],
                                            func=AF.Identity, scale=-1.0,
                                            bias=stc(src_col)),
                     [f"{pre}hf", srctok], [f"{pre}lf"], f"{pre}.lf")
                s_op(lambda e: e.copy(st2b[:, c0 + 1:c0 + 2], st2[:, 6:7]),
                     [f"{pre}lf"], [f"{pre}lb"], f"{pre}.lb")
                s_op(lambda e: e.copy(st2b[:, c0 + 3:c0 + 4], st2[:, 6:7]),
                     [f"{pre}lf"], [f"{pre}lb2"], f"{pre}.lb2")

            def emit_blhs(c0, pre, rows, rtok, tag):
                t_op(lambda e: e.transpose(psT[0][0:4, 0:128],
                                           st2b[:, c0:c0 + 4], identb[:]),
                     [f"{pre}hb", f"{pre}hb2", f"{pre}lb", f"{pre}lb2",
                      "identb"], ["psT0"], f"{tag}.T")
                s_op(lambda e: e.copy(rows[0:4, :], psT[0][0:4, 0:128]),
                     ["psT0"], [rtok], f"{tag}.E")

            def c_front_vs(tt):
                xin, xtok = xinC[tt % 3], f"xinC{tt % 3}"
                y_dma(xin[:], d["x_own"][tt * 128:(tt + 1) * 128, :],
                      [xtok], [xtok], f"C{tt}.x", sem=f"yc{tt % 3}")
                emit_ln(xin, xtok, xnC, "xnC", zbuf[:, 0:H], "zbuf", 0,
                        xnsum_col=7, pre=f"C{tt}.")
                v_op(lambda e: e.tensor_scalar(out=stc(CQ), in0=stc(7),
                                               scalar1=-1.0 / H, scalar2=None,
                                               op0=ALU.mult),
                     ["xnsum7"], ["nmu2"], f"C{tt}.mu2")
                s_op(lambda e: e.activation(out=zbuf[:, 0:H], in_=xnC[:],
                                            func=AF.Square, bias=stc(CQ),
                                            accum_out=stc(CQ + 1)),
                     ["xnC", "nmu2"], ["zbuf", "v2s"], f"C{tt}.v2")
                v_op(lambda e: e.tensor_scalar(out=stc(CQ + 2), in0=stc(CQ + 1),
                                               scalar1=1.0 / H, scalar2=EPS,
                                               op0=ALU.mult, op1=ALU.add),
                     ["v2s"], ["v2p"], f"C{tt}.vp2")
                v_op(lambda e: e.reciprocal(stc(CQ + 3), stc(CQ + 2)),
                     ["v2p"], ["ri2"], f"C{tt}.ri2")
                s_op(lambda e: e.activation(out=stc(CQ + 4), in_=stc(CQ + 3),
                                            func=AF.Sqrt),
                     ["ri2"], ["rs2"], f"C{tt}.rs2")
                v_op(lambda e: e.tensor_tensor(out=stc(CQ + 5), in0=stc(CQ),
                                               in1=stc(CQ + 4), op=ALU.mult),
                     ["nmu2", "rs2"], ["nm2r"], f"C{tt}.nm2r")
                s_op(lambda e: e.activation(out=ln2b[:], in_=xnC[:], func=AF.Identity,
                                            scale=stc(CQ + 4), bias=stc(CQ + 5)),
                     ["xnC", "rs2", "nm2r"], ["ln2b"], f"C{tt}.l2n")
                v_op(lambda e: e.tensor_tensor(out=ln2b[:], in0=ln2b[:],
                                               in1=gamma[:], op=ALU.mult),
                     ["ln2b", "gamma"], ["ln2b"], f"C{tt}.gam")
                v_op(lambda e: e.tensor_tensor(out=ln2b[:], in0=ln2b[:],
                                               in1=beta[:], op=ALU.add),
                     ["ln2b", "beta"], ["ln2b"], f"C{tt}.bet")
                v_op(lambda e: e.reduce_max(stc(CQ + 6), ln2b[:], axis=AX.X,
                                            apply_absolute_value=True),
                     ["ln2b"], ["mx1"], f"C{tt}.mx1")
                v_op(lambda e: e.tensor_scalar(out=stc(CQ + 7), in0=stc(CQ + 6),
                                               scalar1=EPS, scalar2=None,
                                               op0=ALU.max),
                     ["mx1"], ["den1"], f"C{tt}.den1")
                v_op(lambda e: e.reciprocal(stc(CQ + 8), stc(CQ + 7)),
                     ["den1"], ["rd1"], f"C{tt}.rd1")
                v_op(lambda e: e.tensor_scalar(out=stc(CQ + 9), in0=stc(CQ + 8),
                                               scalar1=127.0, scalar2=None,
                                               op0=ALU.mult),
                     ["rd1"], ["sx1"], f"C{tt}.sx1")
                rqc = 40 + (tt % 2)
                v_op(lambda e, rqc=rqc: e.tensor_tensor(
                        out=stc(rqc), in0=stc(CQ + 7),
                        in1=stc(C_MW1), op=ALU.mult),
                     ["den1"], ["rqt1"], f"C{tt}.rt1")
                v_op(lambda e, rqc=rqc: e.tensor_scalar(
                        out=stc(rqc), in0=stc(rqc),
                        scalar1=1.0 / 127.0, scalar2=None,
                        op0=ALU.mult),
                     ["rqt1"], [f"rq1{tt % 2}"], f"C{tt}.rq1")
                v_op(lambda e, rqc=rqc: e.reciprocal(stc(CQ + 11), stc(rqc)),
                     [f"rq1{tt % 2}"], ["sr1"], f"C{tt}.sr1")

            def c_front_vs2(tt):
                emit_split(CQ + 11, "sr1", 0, "s1")
                emit_aquant(ln2b, ["ln2b"], "sx1", CQ + 9, H, 0, nxq[:], "nxq",
                            f"C{tt}.xq")

            def c_front_pe(tt):
                emit_tr(nxq, "nxq", 8, xqfm, "xqfm", lambda j: j * 128,
                        f"C{tt}.xf", per_chunk=True)

            def c_mid(tt, cb1=None, cb2=None):
                for hc in range(8):
                    bk_i = hc % 4
                    if hc == 0 and cb1 is not None:
                        cb1()
                    if hc == 4 and cb2 is not None:
                        cb2()
                    for dj in range(8):
                        t_op(lambda e, dj=dj, hc=hc, bk_i=bk_i: e.matmul(
                                ps_[bk_i][:], xqfm[:, dj * 128:(dj + 1) * 128],
                                big0[:, dj * FF + hc * 512:dj * FF + hc * 512 + 512],
                                start=(dj == 0), stop=False),
                             [f"xqfm#{dj}", "big0"], [f"ps{bk_i}"],
                             f"C{tt}.m1.{hc}.{dj}")
                    t_op(lambda e, hc=hc, bk_i=bk_i: e.matmul(
                            ps_[bk_i][:], blhs1[:],
                            b1rows[:, hc * 512:hc * 512 + 512],
                            start=False, stop=True),
                         ["blhs1", "b1rows"], [f"ps{bk_i}"], f"C{tt}.bb1.{hc}")
                    s_op(lambda e, hc=hc, bk_i=bk_i, tt=tt: e.activation(
                            out=hbuf[:, hc * 512:(hc + 1) * 512], in_=ps_[bk_i][:],
                            func=AF.Gelu, scale=stc(40 + (tt % 2))),
                         [f"ps{bk_i}", f"rq1{tt % 2}"], [f"hbuf#{hc}"],
                         f"C{tt}.g{hc}")

            def c_tail(tt):
                for hc in range(8):
                    v_op(lambda e, hc=hc: e.reduce_max(
                            st2[:, 8 + hc:9 + hc], hbuf[:, hc * 512:(hc + 1) * 512],
                            axis=AX.X, apply_absolute_value=True),
                         [f"hbuf#{hc}"], [f"hm{hc}"], f"C{tt}.hm{hc}")
                v_op(lambda e: e.reduce_max(stc(CQ + 15), st2[:, 8:16], axis=AX.X),
                     [f"hm{h}" for h in range(8)], ["mx2"], f"C{tt}.mx2")
                v_op(lambda e: e.tensor_scalar(out=stc(CQ + 16), in0=stc(CQ + 15),
                                               scalar1=EPS, scalar2=None,
                                               op0=ALU.max),
                     ["mx2"], ["den2"], f"C{tt}.den2")
                v_op(lambda e: e.reciprocal(stc(CQ + 17), stc(CQ + 16)),
                     ["den2"], ["rd2"], f"C{tt}.rd2")
                v_op(lambda e: e.tensor_scalar(out=stc(CQ + 12), in0=stc(CQ + 17),
                                               scalar1=127.0, scalar2=None,
                                               op0=ALU.mult),
                     ["rd2"], ["sx2"], f"C{tt}.sx2")
                v_op(lambda e: e.tensor_tensor(out=stc(CQ + 13), in0=stc(CQ + 16),
                                               in1=stc(C_MW2), op=ALU.mult),
                     ["den2"], ["rqt2"], f"C{tt}.rt2")
                v_op(lambda e: e.tensor_scalar(out=stc(CQ + 13), in0=stc(CQ + 13),
                                               scalar1=1.0 / 127.0, scalar2=None,
                                               op0=ALU.mult),
                     ["rqt2"], ["rq2"], f"C{tt}.rq2")
                v_op(lambda e: e.reciprocal(stc(CQ + 14), stc(CQ + 13)),
                     ["rq2"], ["sr2"], f"C{tt}.sr2")
                emit_split(CQ + 14, "sr2", 4, "s2")

                def emit_aq_q(q):
                    emit_aquant(hbuf, [f"hbuf#{2 * q}", f"hbuf#{2 * q + 1}"],
                                "sx2", CQ + 12, 1024, q * 1024,
                                nhqd[q % 2][:], f"nhq{q % 2}",
                                f"C{tt}.hq{q}")

                def emit_hqT(q):
                    emit_tr(nhqd[q % 2], f"nhq{q % 2}", 8, hqfm, "hqfm",
                            lambda j, q=q: (q % 2) * 1024 + j * 128,
                            f"C{tt}.hf{q}", per_chunk=True)

                def emit_mm2(hf, rng=range(16)):
                    for hj16 in rng:
                        hj = hf * 16 + hj16
                        for half, bk2 in enumerate((2, 3)):
                            t_op(lambda e, hj16=hj16, hj=hj, half=half, bk2=bk2:
                                 e.matmul(ps_[bk2][:],
                                          hqfm[:, hj16 * 128:(hj16 + 1) * 128],
                                          big1[:, hj * H + half * 512:
                                               hj * H + half * 512 + 512],
                                          start=(hj == 0), stop=False),
                                 [f"hqfm#{hj16}", "big1"], [f"ps{bk2}"],
                                 f"C{tt}.m2.{hj}.{half}")

                emit_aq_q(0)
                emit_aq_q(1)
                emit_hqT(0)
                emit_aq_q(2)
                emit_hqT(1)
                emit_aq_q(3)
                emit_mm2(0)
                emit_hqT(2)
                emit_mm2(1, range(8))
                emit_hqT(3)
                emit_blhs(4, "s2", blhs2, "blhs2", f"C{tt}.b2")
                emit_mm2(1, range(8, 16))
                for half, bk2 in enumerate((2, 3)):
                    t_op(lambda e, half=half, bk2=bk2: e.matmul(
                            ps_[bk2][:], blhs2[:],
                            b2rows[:, half * 512:half * 512 + 512],
                            start=False, stop=True),
                         ["blhs2", "b2rows"], [f"ps{bk2}"], f"C{tt}.bb2.{half}")
                if tt + 1 < NT_Q:
                    emit_blhs(0, "s1", blhs1, "blhs1", f"C{tt + 1}.b1")

            def c_pload(tt):
                y_dma(xpart[tt % 3][:], part_d[tt * 128:(tt + 1) * 128, :],
                      [f"xinC{tt % 3}"], [f"xinC{tt % 3}"], f"C{tt}.p",
                      sem=f"yc{tt % 3}")

            def c_out(tt):
                xp, xptok = xpart[tt % 3], f"xinC{tt % 3}"
                v_op(lambda e, xp=xp: e.scalar_tensor_tensor(
                        out=xp[:, 0:512], in0=ps_[2][:], scalar=stc(CQ + 13),
                        in1=xp[:, 0:512], op0=ALU.mult, op1=ALU.add),
                     ["ps2", "rq2", xptok], [xptok], f"C{tt}.o0")
                v_op(lambda e, xp=xp: e.scalar_tensor_tensor(
                        out=xp[:, 512:H], in0=ps_[3][:], scalar=stc(CQ + 13),
                        in1=xp[:, 512:H], op0=ALU.mult, op1=ALU.add),
                     ["ps3", "rq2", xptok], [xptok], f"C{tt}.o1")
                g_dma(out_d[tt * 128:(tt + 1) * 128, :], xp[:], [xptok],
                      [xptok], f"C{tt}.st", sem=f"gc{tt % 2}")

            c_front_vs(0)
            c_front_vs2(0)
            emit_blhs(0, "s1", blhs1, "blhs1", "C0.b1")
            c_front_pe(0)
            for tt in range(NT_Q):
                if tt > 0:
                    c_out(tt - 1)
                c_mid(tt,
                      cb1=((lambda t=tt: c_front_vs(t + 1))
                           if tt + 1 < NT_Q else None),
                      cb2=((lambda t=tt: c_front_vs2(t + 1))
                           if tt + 1 < NT_Q else None))
                if tt + 1 < NT_Q:
                    c_front_pe(tt + 1)
                c_pload(tt)
                c_tail(tt)
                if debug and tt == 0:
                    g_dma(dbg["d_xnC"], xnC[:], ["xnC"], [], "dbg.xnC")
                    g_dma(dbg["d_ln2b"], ln2b[:], ["ln2b"], [], "dbg.ln2b")
                    g_dma(dbg["d_z"], zbuf[:], ["zbuf"], [], "dbg.z")
                    g_dma(dbg["d_nxq"], nxq[:], ["nxq"], [], "dbg.nxq")
                    g_dma(dbg["d_cstats"], stats[:], ["sr1"], [], "dbg.cst")
            c_out(NT_Q - 1)
            sch.finalize_phase(blkC, final=True)

    return nc


_NC_CACHE = []


def kernel(**inputs):
    X = np.ascontiguousarray(np.asarray(inputs["L0_attn_hidden_states"],
                                        dtype=np.float32))
    Wq = np.asarray(inputs["L0_attn_Wq"], dtype=np.float32)
    Wk = np.asarray(inputs["L0_attn_Wk"], dtype=np.float32)
    Wv = np.asarray(inputs["L0_attn_Wv"], dtype=np.float32)
    bq = np.asarray(inputs["L0_attn_bq"], dtype=np.float32)
    bk = np.asarray(inputs["L0_attn_bk"], dtype=np.float32)
    bv = np.asarray(inputs["L0_attn_bv"], dtype=np.float32)
    w1 = np.asarray(inputs["L0_ffn_w1"], dtype=np.float32)
    b1 = np.asarray(inputs["L0_ffn_b1"], dtype=np.float32)
    w2 = np.asarray(inputs["L0_ffn_w2"], dtype=np.float32)
    b2 = np.asarray(inputs["L0_ffn_b2"], dtype=np.float32)
    gamma = np.asarray(inputs["L0_ffn_gamma"], dtype=np.float32)
    beta = np.asarray(inputs["L0_ffn_beta"], dtype=np.float32)

    perm = np.concatenate([np.arange(0, H, 2), np.arange(1, H, 2)])
    wqT = np.ascontiguousarray(Wq.T[:, perm])
    wkT = np.ascontiguousarray(Wk.T[:, perm])
    wvT = np.ascontiguousarray(Wv.T)
    w1T = np.ascontiguousarray(w1.T)
    w2T = np.ascontiguousarray(w2.T)

    pos = np.arange(S, dtype=np.float32)
    inv = (10000.0 ** (np.arange(0, 512, 2, dtype=np.float64) / 512.0)).astype(np.float32)
    rad = (pos[:, None] / inv).astype(np.float32)
    sc_hi = np.stack([np.sin(rad.astype(np.float64)),
                      np.cos(rad.astype(np.float64))],
                     axis=-1).reshape(S, 512).astype(np.float32)

    identb = np.eye(128, dtype=BFDT)
    identf = np.eye(128, dtype=np.float32)

    in_maps = []
    for c in range(8):
        b, hh = c // 2, c % 2
        m = {
            "x_full": np.ascontiguousarray(X[b]),
            "x_own": np.ascontiguousarray(X[b, hh * HALF:(hh + 1) * HALF]),
            "wqT": wqT, "wkT": wkT, "wvT": wvT,
            "bq2": bq[perm][None, :], "bk2": bk[perm][None, :], "bv2": bv[None, :],
            "w1T": w1T, "w2T": w2T,
            "b1r": np.ascontiguousarray(b1.reshape(4, FF // 4)),
            "b2r": np.ascontiguousarray(b2.reshape(4, H // 4)),
            "gamma_bc": np.ascontiguousarray(np.broadcast_to(gamma, (128, H))),
            "beta_bc": np.ascontiguousarray(np.broadcast_to(beta, (128, H))),
            "sck_hi": sc_hi,
            "scq_hi": np.ascontiguousarray(sc_hi[hh * HALF:(hh + 1) * HALF]),
            "ident_b": identb, "ident_f": identf,
        }
        in_maps.append(m)

    if not _NC_CACHE:
        _NC_CACHE.append(build_nc())
    nc = _NC_CACHE[0]
    res = run_bass_kernel_spmd(nc, in_maps, core_ids=list(range(8)))
    out = np.empty((B, S, H), dtype=np.float32)
    for c in range(8):
        b, hh = c // 2, c % 2
        out[b, hh * HALF:(hh + 1) * HALF] = res.results[c]["out"]
    return out


# version 23
# speedup vs baseline: 1.0438x; 1.0438x over previous
import contextlib
import math

import numpy as np
import ml_dtypes

import concourse.bass as bass
import concourse.mybir as mybir
from concourse.bass_utils import run_bass_kernel_spmd

F32 = mybir.dt.float32
BF16 = mybir.dt.bfloat16
I32 = mybir.dt.int32
AF = mybir.ActivationFunctionType
ALU = mybir.AluOpType
AX = mybir.AxisListType
BFDT = ml_dtypes.bfloat16

B, S, H = 4, 4096, 1024
HALF = S // 2
FF = 4096
NT_K = S // 128
NT_Q = HALF // 128
INV_SCALE = 1.0 / math.sqrt(512.0)
EPS = 1e-5


class Sched:

    ENGINES = ("t", "v", "s", "g")
    STREAM_OF_Q = {"y": "y", "gd": "g"}

    def __init__(self, nc, es):
        self.nc = nc
        self.es = es
        self.sems = {}
        self.sem_policy = {}
        self.cnt = {}
        for e in self.ENGINES:
            self._mk(e, "ord")
        self.pending = []
        self.last_writer = {}
        self.readers = {}
        self.hw = {}
        self.bar_vals = None
        self.bar_seen = set()

    def _mk(self, name, policy):
        if name not in self.sems:
            self.sems[name] = self.es.enter_context(self.nc.semaphore("sem_" + name))
            self.sem_policy[name] = policy
            self.cnt[name] = 0

    def dma_sem(self, name, policy="ord"):
        self._mk(name, policy)

    def op(self, eng, emit, reads=(), writes=(), tag="", sem=None, stream=None,
           small=True):
        if eng in ("y", "gd"):
            assert sem is not None, tag
            self._mk(sem, self.sem_policy.get(sem, "ord"))
            stream = self.STREAM_OF_Q[eng]
            step = 16
        else:
            sem = eng
            stream = eng
            step = 1
        i = len(self.pending)
        deps = set()
        for r in reads:
            w = self.last_writer.get(r)
            if w is not None:
                deps.add(w)
            self.readers.setdefault(r, []).append(i)
        for w in writes:
            pw = self.last_writer.get(w)
            if pw is not None:
                deps.add(pw)
            for rd in self.readers.get(w, ()):
                deps.add(rd)
            self.last_writer[w] = i
            self.readers[w] = []
        self.pending.append(dict(eng=eng, sem=sem, stream=stream, step=step,
                                 emit=emit, deps=deps, rawdeps=set(deps),
                                 tag=tag, small=small,
                                 need_inc=(eng in ("y", "gd")), inc_ord=None))

    def finalize_phase(self, block, final=False):
        ops = self.pending
        for i, o in enumerate(ops):
            kept = set()
            same_eng_deps = []
            for dd in o["deps"]:
                po = ops[dd]
                if po["eng"] == o["eng"] and po["eng"] not in ("y", "gd"):
                    same_eng_deps.append(dd)
                    continue
                if po["eng"] == o["eng"] and po["sem"] == o["sem"]:
                    continue
                kept.add(dd)
            o["deps"] = kept
            o["same_eng_deps"] = same_eng_deps
            for dd in kept:
                ops[dd]["need_inc"] = True
        last_flush = {}
        for i, o in enumerate(ops):
            o["pre_drain"] = False
            if o["eng"] in ("v", "s", "g"):
                e = o["eng"]
                for dd in o["same_eng_deps"]:
                    if ops[dd]["small"] and dd > last_flush.get(e, -1):
                        o["pre_drain"] = True
                        last_flush[e] = i - 1 if i else -1
                        break
                if o["need_inc"]:
                    last_flush[e] = i
        last_of = {}
        for i, o in enumerate(ops):
            last_of[o["eng"] if o["eng"] in self.ENGINES else o["sem"]] = i
        for i in last_of.values():
            ops[i]["need_inc"] = True
        totals_at = [None] * len(ops)
        run_tot = dict(self.cnt)
        for i, o in enumerate(ops):
            totals_at[i] = None
            if o["need_inc"]:
                run_tot[o["sem"]] = run_tot.get(o["sem"], 0) + 1
                o["inc_ord"] = run_tot[o["sem"]]
        pre_tot = {}
        run2 = dict(self.cnt)
        pref = []
        for i, o in enumerate(ops):
            pref.append(dict(run2))
            if o["need_inc"]:
                run2[o["sem"]] = run2.get(o["sem"], 0) + 1
        waits = [[] for _ in ops]
        for i, o in enumerate(ops):
            st = o["stream"]
            need = {}
            if self.bar_vals is not None and st not in self.bar_seen:
                self.bar_seen.add(st)
                for pe, v in self.bar_vals.items():
                    if v > 0:
                        need[pe] = v
            for dd in o["deps"]:
                po = ops[dd]
                assert po["inc_ord"] is not None, (po["tag"], o["tag"])
                if self.sem_policy.get(po["sem"], "ord") == "total":
                    v = pref[i].get(po["sem"], 0) * po["step"]
                    v = max(v, po["inc_ord"] * po["step"])
                else:
                    v = po["inc_ord"] * po["step"]
                if v > need.get(po["sem"], 0):
                    need[po["sem"]] = v
            for pe, v in need.items():
                k = (st, pe)
                if v > self.hw.get(k, 0):
                    self.hw[k] = v
                    waits[i].append((pe, v))
        per_stream = {"t": [], "v": [], "s": [], "g": [], "y": []}
        for i, o in enumerate(ops):
            per_stream[o["stream"]].append(i)
        sems = self.sems
        self.cnt = run_tot
        fin_vals = {nm: self.cnt.get(nm, 0) * (16 if self.sem_policy[nm] != "ord"
                    or nm not in self.ENGINES else 1) for nm in self.sems}
        fin_vals = {nm: self.cnt.get(nm, 0) * (1 if nm in self.ENGINES else 16)
                    for nm in self.sems}

        def run_stream(engh, stream):
            for i in per_stream[stream]:
                o = ops[i]
                for (pe, v) in waits[i]:
                    engh.wait_ge(sems[pe], v)
                if o.get("pre_drain"):
                    engh.drain()
                instr = o["emit"](engh)
                if o["need_inc"]:
                    if o["eng"] in ("v", "s", "g"):
                        engh.drain().then_inc(sems[o["sem"]],
                                                          o["step"])
                    else:
                        assert instr is not None, o["tag"]
                        instr.then_inc(sems[o["sem"]], o["step"])
            if final:
                for pe, v in fin_vals.items():
                    k = (stream, pe)
                    if v > self.hw.get(k, 0) and v > 0:
                        engh.wait_ge(sems[pe], v)

        @block.sync
        def _(eng):
            run_stream(eng, "y")

        @block.tensor
        def _(eng):
            run_stream(eng, "t")

        @block.vector
        def _(eng):
            run_stream(eng, "v")

        @block.scalar
        def _(eng):
            run_stream(eng, "s")

        @block.gpsimd
        def _(eng):
            run_stream(eng, "g")

        self.bar_vals = fin_vals
        self.bar_seen = set()
        self.pending = []
        self.last_writer = {}
        self.readers = {}


def build_nc(debug=False, n_ct=NT_Q):
    nc = bass.Bass("TRN2", target_bir_lowering=False, detect_race_conditions=False)

    d = {}
    for name, shape, dt in [
            ("x_full", [S, H], F32), ("x_own", [HALF, H], F32),
            ("wqT", [H, H], F32), ("wkT", [H, H], F32), ("wvT", [H, H], F32),
            ("bq2", [1, H], F32), ("bk2", [1, H], F32), ("bv2", [1, H], F32),
            ("w1T", [H, FF], F32), ("w2T", [FF, H], F32),
            ("b1r", [4, FF // 4], F32), ("b2r", [4, H // 4], F32),
            ("gamma_bc", [128, H], F32), ("beta_bc", [128, H], F32),
            ("sck_hi", [S, 512], F32),
            ("scq_hi", [HALF, 512], F32),
            ("ident_b", [128, 128], BF16), ("ident_f", [128, 128], F32)]:
        d[name] = nc.dram_tensor(name, shape, dt, kind="ExternalInput").ap()
    out_d = nc.dram_tensor("out", [HALF, H], F32, kind="ExternalOutput").ap()
    dbg = {}
    if debug:
        for nm, sh, dt_ in [("d_ssb", [128, S], F32), ("d_qro", [128, H], BF16),
                            ("d_lp2", [128, 8], F32),
                            ("d_xnC", [128, H], F32), ("d_ln2b", [128, H], F32),
                            ("d_z", [128, 1024], F32), ("d_nxq", [128, H], BF16),
                            ("d_cstats", [128, 64], F32),
                            ("d_xin0", [128, H], F32), ("d_xn0", [128, H], BF16),
                            ("d_kro0", [128, H], BF16), ("d_ps0", [128, 512], F32),
                            ("d_big0", [128, 8 * S], BF16), ("d_big1", [128, NT_K * H], BF16),
                            ("d_knorm2", [128, NT_K], F32), ("d_stats", [128, 64], F32),
                            ("d_partial", [HALF, H], F32), ("d_lparts", [128, 8], F32),
                            ("d_wq_mix", [128, 8 * H], BF16), ("d_hbuf", [128, FF], F32),
                            ("d_xnfmA", [128, H], BF16), ("d_w1q", [128, 8 * FF], BF16)]:
            dbg[nm] = nc.dram_tensor(nm, sh, dt_, kind="ExternalOutput").ap()
    part_d = nc.dram_tensor("partial", [HALF, H], F32).ap()
    w1q_d = nc.dram_tensor("w1q_sp", [128, 8 * FF], BF16).ap()
    w2q_d = nc.dram_tensor("w2q_sp", [128, 32 * H], BF16).ap()
    w1v = d["w1T"].rearrange("(a p) n -> p a n", p=128)
    w2v = d["w2T"].rearrange("(a p) n -> p a n", p=128)

    es = contextlib.ExitStack()
    with es:
        sch = Sched(nc, es)
        for nm, pol in (("ys", "total"), ("yr", "total"), ("bm", "total"),
                        ("gm", "total")):
            sch.dma_sem(nm, pol)

        def sb(stk, name, shape, dt):
            return stk.enter_context(nc.sbuf_tensor(name, shape, dt))

        big0 = sb(es, "big0", [128, 8 * S], BF16)
        big1 = sb(es, "big1", [128, NT_K * H], BF16)
        identb = sb(es, "identb", [128, 128], BF16)
        identf = sb(es, "identf", [128, 128], F32)
        onesf = sb(es, "onesf", [128, 128], F32)
        ones2 = sb(es, "ones2", [2, 128], BF16)
        stats = sb(es, "stats", [128, 64], F32)
        st2 = sb(es, "st2", [128, 16], F32)
        knorm2 = sb(es, "knorm2", [128, NT_K], F32)
        lparts = sb(es, "lparts", [128, 8], F32)
        wsums = sb(es, "wsums", [128, 68], F32)
        rowB = sb(es, "rowB", [1, 128], F32)
        ps_ = [es.enter_context(nc.psum_tensor(f"ps{i}", [128, 512], F32))
               for i in range(4)]
        psT = [es.enter_context(nc.psum_tensor(f"psT{i}", [128, 1024], BF16))
               for i in range(4)]

        C_KMAX2, C_MW1, C_MW2, C_SW1, C_SW2 = 16, 17, 18, 19, 20

        def stc(i):
            return stats[:, i:i + 1]

        y_dma = lambda dst, src, reads, writes, tag="", sem="ys": sch.op(
            "y", lambda e, a=dst, b=src: e.dma_start(out=a, in_=b), reads, writes,
            tag, sem=sem)
        g_dma = lambda dst, src, reads, writes, tag="", sem="gm": sch.op(
            "gd", lambda e, a=dst, b=src: e.dma_start(out=a, in_=b), reads, writes,
            tag, sem=sem)
        v_op = lambda fn, reads, writes, tag="": sch.op("v", fn, reads, writes, tag)
        s_op = lambda fn, reads, writes, tag="": sch.op("s", fn, reads, writes, tag)
        t_op = lambda fn, reads, writes, tag="": sch.op("t", fn, reads, writes, tag)
        g_op = lambda fn, reads, writes, tag="": sch.op("g", fn, reads, writes, tag)

        def emit_ln(xin, xtok, xn, xntok, scratch, sctok, c0, xnsum_col=None,
                    pre=""):
            tk = [f"stat{c0 + j}" for j in range(7)]
            v_op(lambda e: e.reduce_sum(stc(c0), xin[:], axis=AX.X),
                 [xtok], [tk[0]], pre + "ln.sum")
            v_op(lambda e: e.tensor_scalar(out=stc(c0 + 1), in0=stc(c0),
                                           scalar1=-1.0 / H, scalar2=None,
                                           op0=ALU.mult),
                 [tk[0]], [tk[1]], pre + "ln.neg")
            s_op(lambda e: e.activation(out=scratch[:], in_=xin[:], func=AF.Square,
                                        bias=stc(c0 + 1), scale=1.0,
                                        accum_out=stc(c0 + 2)),
                 [xtok, tk[1]], [sctok, tk[2]], pre + "ln.var")
            v_op(lambda e: e.tensor_scalar(out=stc(c0 + 3), in0=stc(c0 + 2),
                                           scalar1=1.0 / H, scalar2=EPS,
                                           op0=ALU.mult, op1=ALU.add),
                 [tk[2]], [tk[3]], pre + "ln.vps")
            v_op(lambda e: e.reciprocal(stc(c0 + 4), stc(c0 + 3)),
                 [tk[3]], [tk[4]], pre + "ln.rin")
            s_op(lambda e: e.activation(out=stc(c0 + 5), in_=stc(c0 + 4),
                                        func=AF.Sqrt),
                 [tk[4]], [tk[5]], pre + "ln.rsd")
            v_op(lambda e: e.tensor_tensor(out=stc(c0 + 6), in0=stc(c0 + 1),
                                           in1=stc(c0 + 5), op=ALU.mult),
                 [tk[1], tk[5]], [tk[6]], pre + "ln.nmr")
            acc = None if xnsum_col is None else stc(xnsum_col)
            if acc is None:
                s_op(lambda e: e.activation(out=xn[:], in_=xin[:], func=AF.Identity,
                                            scale=stc(c0 + 5), bias=stc(c0 + 6)),
                     [xtok, tk[5], tk[6]], [xntok], pre + "ln.xn")
            else:
                s_op(lambda e: e.activation(out=xn[:], in_=xin[:], func=AF.Identity,
                                            scale=stc(c0 + 5), bias=stc(c0 + 6),
                                            accum_out=acc),
                     [xtok, tk[5], tk[6]], [xntok, "xnsum7"], pre + "ln.xn")

        def emit_tr(src, stok, nchunks, dst, dtok, colf, tag, src_colf=None,
                    per_chunk=False, ev_alt=False):
            for j in range(nchunks):
                bi = j % 4
                sc = src_colf(j) if src_colf else j * 128
                t_op(lambda e, bi=bi, sc=sc: e.transpose(
                        psT[bi][:, 0:128], src[:, sc:sc + 128], identb[:]),
                     [stok, "identb"], [f"psT{bi}"], f"{tag}.T{j}")
                c0 = colf(j)
                wtok = f"{dtok}#{c0 // 128}" if per_chunk else dtok
                if ev_alt and j % 2 == 1:
                    v_op(lambda e, bi=bi, c0=c0: e.tensor_copy(
                            dst[:, c0:c0 + 128], psT[bi][:, 0:128]),
                         [f"psT{bi}"], [wtok], f"{tag}.E{j}")
                else:
                    s_op(lambda e, bi=bi, c0=c0: e.copy(dst[:, c0:c0 + 128],
                                                        psT[bi][:, 0:128]),
                         [f"psT{bi}"], [wtok], f"{tag}.E{j}")

        es_ab = contextlib.ExitStack()
        with es_ab:
            wq_mix = sb(es_ab, "wq_mix", [128, 8 * H], BF16)
            bq_rows = sb(es_ab, "bq_rows", [2, H], BF16)

            es_a = contextlib.ExitStack()
            with es_a:
                wk_mix = sb(es_a, "wk_mix", [128, 8 * H], BF16)
                wv_bf = sb(es_a, "wv_bf", [128, 8 * H], BF16)
                xinA = [sb(es_a, f"xinA{i}", [128, H], F32) for i in range(2)]
                schi = sb(es_a, "schi", [128, 512], F32)
                xnA = [sb(es_a, f"xnA{i}", [128, H], BF16)
                       for i in range(2)]
                xnfmA = sb(es_a, "xnfmA", [128, H], BF16)
                kro = sb(es_a, "kro", [128, H], BF16)
                wraw = sb(es_a, "wraw", [128, H], F32)
                brow = wraw[0:1, :]
                bmixf2 = xinA[0][0:1, :]
                brow2 = xnA[0][0:1, :]
                bk_rows = sb(es_a, "bk_rows", [2, H], BF16)
                bv_rows = sb(es_a, "bv_rows", [2, H], BF16)

                blkA = es_a.enter_context(nc.Block("phaseA"))

                y_dma(identb[:], d["ident_b"], [], ["identb"], "ld.idb")
                y_dma(identf[:], d["ident_f"], [], ["identf"], "ld.idf")
                g_op(lambda e: e.memset(onesf[:], 1.0), [], ["onesf"], "ms.1f")
                if debug:
                    g_op(lambda e: e.memset(stats[:], 0.0), [], [], "ms.dbg1")
                    g_op(lambda e: e.memset(st2[:], 0.0), [], [], "ms.dbg2")
                    g_op(lambda e: e.memset(wsums[:], 0.0), [], [], "ms.dbg3")
                    g_op(lambda e: e.memset(lparts[:], 0.0), [], [], "ms.dbg4")
                    g_op(lambda e: e.memset(knorm2[:], 0.0), [], [], "ms.dbg5")
                g_op(lambda e: e.memset(ones2[:], 1.0), [], ["ones2"], "ms.12")

                def emit_wprep(src, dst, dtok, mixed, tag):
                    for dj in range(8):
                        y_dma(wraw[:], src[dj * 128:(dj + 1) * 128, :],
                              ["wraw"], ["wraw"], f"{tag}.ld{dj}", sem="yw")
                        if mixed:
                            v_op(lambda e, dj=dj: e.tensor_tensor(
                                    out=dst[:, dj * H:dj * H + 512],
                                    in0=wraw[:, 0:512], in1=wraw[:, 512:H],
                                    op=ALU.subtract),
                                 ["wraw"], [dtok], f"{tag}.s{dj}")
                            v_op(lambda e, dj=dj: e.tensor_tensor(
                                    out=dst[:, dj * H + 512:(dj + 1) * H],
                                    in0=wraw[:, 0:512], in1=wraw[:, 512:H],
                                    op=ALU.add),
                                 ["wraw"], [dtok], f"{tag}.a{dj}")
                        else:
                            v_op(lambda e, dj=dj: e.tensor_copy(
                                    dst[:, dj * H:(dj + 1) * H], wraw[:]),
                                 ["wraw"], [dtok], f"{tag}.c{dj}")

                def emit_brows(src, rows, rtok, mixed, tag):
                    y_dma(brow, src, ["wraw"], ["wraw"], f"{tag}.ld", sem="yw")
                    if mixed:
                        v_op(lambda e: e.tensor_tensor(out=bmixf2[:, 0:512],
                                                       in0=brow[:, 0:512],
                                                       in1=brow[:, 512:H],
                                                       op=ALU.subtract),
                             ["wraw"], ["xinA0"], f"{tag}.s")
                        v_op(lambda e: e.tensor_tensor(out=bmixf2[:, 512:H],
                                                       in0=brow[:, 0:512],
                                                       in1=brow[:, 512:H],
                                                       op=ALU.add),
                             ["wraw"], ["xinA0"], f"{tag}.a")
                        srcb, srct = bmixf2, "xinA0"
                    else:
                        srcb, srct = brow, "wraw"
                    v_op(lambda e, srcb=srcb: e.tensor_copy(rows[0:1, :], srcb),
                         [srct], [rtok], f"{tag}.hi")
                    v_op(lambda e, srcb=srcb: e.scalar_tensor_tensor(
                            out=brow2, in0=srcb, scalar=1.0,
                            in1=rows[0:1, :], op0=ALU.mult, op1=ALU.subtract),
                         [srct, rtok], ["xnA0"], f"{tag}.lo")
                    g_dma(rows[1:2, :], brow2, ["xnA0"], [rtok], f"{tag}.mv",
                          sem="bm")

                emit_wprep(d["wkT"], wk_mix, "wk_mix", True, "wk")
                emit_wprep(d["wvT"], wv_bf, "wv_bf", False, "wv")
                emit_wprep(d["wqT"], wq_mix, "wq_mix", True, "wq")
                emit_brows(d["bk2"][:], bk_rows, "bk_rows", True, "bk")
                emit_brows(d["bv2"][:], bv_rows, "bv_rows", False, "bv")
                emit_brows(d["bq2"][:], bq_rows, "bq_rows", True, "bq")

                def a_xload(kt):
                    xin, xtok = xinA[kt % 2], f"xinA{kt % 2}"
                    y_dma(xin[:], d["x_full"][kt * 128:(kt + 1) * 128, :],
                          [xtok], [xtok], f"A{kt}.x", sem=f"yx{kt % 2}")

                def a_lnops(kt):
                    xin, xtok = xinA[kt % 2], f"xinA{kt % 2}"
                    emit_ln(xin, xtok, xnA[kt % 2], f"xnA{kt % 2}",
                            wraw[:, 0:H], "wraw", 0, pre=f"A{kt}.")

                def a_pe(kt):
                    xn, xntok = xnA[kt % 2], f"xnA{kt % 2}"
                    y_dma(schi[:], d["sck_hi"][kt * 128:(kt + 1) * 128, :],
                          ["schi"], ["schi"], f"A{kt}.h", sem="ysh")
                    if debug and kt == 0:
                        g_dma(dbg["d_xin0"], xinA[0][:], ["xinA0"], [], "dbg.xin0")
                        g_dma(dbg["d_xn0"], xn[:], [xntok], [], "dbg.xn0")
                    emit_tr(xn, xntok, 8, xnfmA, "xnfmA", lambda j: j * 128,
                            f"A{kt}.xn", per_chunk=True, ev_alt=True)
                    for wi, (bks, wsb, wtok, btok, brows) in enumerate((
                            ((0, 1), wk_mix, "wk_mix", "bk_rows", bk_rows),
                            ((2, 3), wv_bf, "wv_bf", "bv_rows", bv_rows))):
                        for half, bk_i in enumerate(bks):
                            for dj in range(8):
                                t_op(lambda e, dj=dj, bk_i=bk_i, wsb=wsb, half=half:
                                     e.matmul(ps_[bk_i][:],
                                              xnfmA[:, dj * 128:(dj + 1) * 128],
                                              wsb[:, dj * H + half * 512:
                                                  dj * H + half * 512 + 512],
                                              start=(dj == 0), stop=False),
                                     [f"xnfmA#{dj}", wtok], [f"ps{bk_i}"],
                                     f"A{kt}.mm{wtok}{half}.{dj}")
                            t_op(lambda e, bk_i=bk_i, brows=brows, half=half:
                                 e.matmul(ps_[bk_i][:], ones2[:],
                                          brows[:, half * 512:half * 512 + 512],
                                          start=False, stop=True),
                                 ["ones2", btok], [f"ps{bk_i}"],
                                 f"A{kt}.b{wtok}{half}")
                        if wi == 0:
                            if kt + 2 < NT_K:
                                a_xload(kt + 2)
                            if kt + 1 < NT_K:
                                a_lnops(kt + 1)
                    v_op(lambda e: e.tensor_tensor(out=kro[:, 0:512], in0=ps_[0][:],
                                                   in1=schi[:], op=ALU.mult),
                         ["ps0", "schi"], ["kro"], f"A{kt}.r0")
                    v_op(lambda e: e.tensor_tensor(out=kro[:, 512:H], in0=ps_[1][:],
                                                   in1=schi[:], op=ALU.mult),
                         ["ps1", "schi"], ["kro"], f"A{kt}.r1")
                    v_op(lambda e, kt=kt: e.tensor_copy(
                            big1[:, kt * H:kt * H + 512], ps_[2][:]),
                         ["ps2"], ["big1"], f"A{kt}.v0")
                    v_op(lambda e, kt=kt: e.tensor_copy(
                            big1[:, kt * H + 512:(kt + 1) * H], ps_[3][:]),
                         ["ps3"], ["big1"], f"A{kt}.v1")
                    if debug and kt == 0:
                        g_dma(dbg["d_kro0"], kro[:], ["kro"], [], "dbg.kro0")
                        g_dma(dbg["d_ps0"], None, [], [], "")
                        sch.pending.pop()
                    emit_tr(kro, "kro", 8, big0, "big0",
                            lambda j, kt=kt: j * S + kt * 128, f"A{kt}.k",
                            ev_alt=True)

                a_xload(0)
                a_xload(1)
                a_lnops(0)
                for kt in range(NT_K):
                    a_pe(kt)

                sch.finalize_phase(blkA)

            es_b = contextlib.ExitStack()
            with es_b:
                xinB = [sb(es_b, f"xinB{i}", [128, H], F32) for i in range(2)]
                schiB = sb(es_b, "schiB", [128, 512], F32)
                xnB = sb(es_b, "xnB", [128, H], BF16)
                xnfmB = sb(es_b, "xnfmB", [128, H], BF16)
                qro = sb(es_b, "qro", [128, H], BF16)
                qfm = [sb(es_b, f"qfm{i}", [128, H], BF16) for i in range(2)]
                probs = [sb(es_b, f"probs{i}", [128, 512], BF16) for i in range(4)]
                pT = [sb(es_b, f"pT{i}", [128, 128], BF16) for i in range(4)]
                outB = [sb(es_b, "outB0", [128, H], F32)] * 2
                wchd = [sb(es_b, f"wch{i}", [128, 1024], F32) for i in range(2)]
                ssb = sb(es_b, "ssb", [128, S], F32)
                wst = [sb(es_b, "wst0", [128, 1024], BF16)] * 2
                wm1 = sb(es_b, "wm1", [128, 1024], BF16)
                wm2 = sb(es_b, "wm2", [128, 1024], BF16)

                blkB = es_b.enter_context(nc.Block("phaseB"))

                def emit_abs_chunk(src, col, width, tag, p):
                    w, wt = wchd[p], f"wch{p}"
                    g_dma(w[:, 0:width], src, [wt], [wt], f"{tag}.ld",
                          sem=f"gw{p}")
                    v_op(lambda e, w=w, col=col, width=width: e.reduce_sum(
                            wsums[:, col:col + 1], w[:, 0:width], axis=AX.X,
                            apply_absolute_value=True),
                         [wt], [f"ws{col}"], f"{tag}.abs")

                def emit_scales(first_col, n_col, tot_col, denom, out_col, tag):
                    v_op(lambda e, first_col=first_col, n_col=n_col, tot_col=tot_col:
                         e.reduce_sum(st2[:, tot_col:tot_col + 1],
                                      wsums[:, first_col:first_col + n_col],
                                      axis=AX.X),
                         [f"ws{c}" for c in range(first_col, first_col + n_col)],
                         [f"tot{tag}"], f"{tag}.tot")
                    t_op(lambda e, tot_col=tot_col: e.matmul(
                            ps_[2][:, 0:1], onesf[:, 0:128],
                            st2[:, tot_col:tot_col + 1], start=True, stop=True),
                         [f"tot{tag}", "onesf"], ["ps2"], f"{tag}.bc")
                    v_op(lambda e, denom=denom, out_col=out_col: e.tensor_scalar(
                            out=stc(out_col), in0=ps_[2][:, 0:1],
                            scalar1=1.0 / denom, scalar2=EPS,
                            op0=ALU.mult, op1=ALU.max),
                         ["ps2"], [f"mwc{tag}"], f"{tag}.mwc")
                    v_op(lambda e, out_col=out_col: e.reciprocal(
                            stc(out_col + 2), stc(out_col)),
                         [f"mwc{tag}"], [f"sw{tag}"], f"{tag}.sw")

                def emit_wq_load(src, parity, tag):
                    w, wt = wchd[parity], f"wch{parity}"
                    g_dma(w[:, 0:1024], src, [wt], [wt], f"{tag}.ld",
                          sem=f"gw{parity}")

                def emit_wquant_chunk(src, dst_dram, parity, mwtag, tag):
                    w, wt = wchd[parity], f"wch{parity}"
                    ws, wst_t = wst[parity], "wst0"
                    v_op(lambda e, w=w, mwtag=mwtag: e.tensor_scalar(
                            out=wm1[:], in0=w[:, 0:1024],
                            scalar1=stc(C_SW1 if mwtag == "w1" else C_SW2),
                            scalar2=0.5, op0=ALU.mult, op1=ALU.is_ge),
                         [wt, f"sw{mwtag}"], ["wm1"], f"{tag}.m1")
                    v_op(lambda e, w=w, mwtag=mwtag: e.tensor_scalar(
                            out=wm2[:], in0=w[:, 0:1024],
                            scalar1=stc(C_SW1 if mwtag == "w1" else C_SW2),
                            scalar2=-0.5, op0=ALU.mult, op1=ALU.is_le),
                         [wt, f"sw{mwtag}"], ["wm2"], f"{tag}.m2")
                    v_op(lambda e, ws=ws: e.tensor_tensor(
                            out=ws[:], in0=wm2[:], in1=wm1[:],
                            op=ALU.subtract),
                         ["wm1", "wm2"], [wst_t], f"{tag}.nq")
                    g_dma(dst_dram, ws[:], [wst_t],
                          [wst_t], f"{tag}.sp", sem=f"gs{parity}")

                abs_jobs = []
                for a in range(8):
                    for q4 in range(4):
                        abs_jobs.append((w1v[:, a, q4 * 1024:(q4 + 1) * 1024],
                                         a * 4 + q4, 1024, f"w1a{a}.{q4}"))
                for a in range(32):
                    abs_jobs.append((w2v[:, a, :], 32 + a, 1024, f"w2a{a}"))

                quant_jobs = []
                for a in range(8):
                    for q4 in range(4):
                        quant_jobs.append((w1v[:, a, q4 * 1024:(q4 + 1) * 1024],
                                           w1q_d[:, (a * 4 + q4) * 1024:
                                                 (a * 4 + q4 + 1) * 1024],
                                           "w1", f"qw1.{a}.{q4}"))
                for a in range(32):
                    quant_jobs.append((w2v[:, a, :],
                                       w2q_d[:, a * 1024:(a + 1) * 1024],
                                       "w2", f"qw2.{a}"))

                abs_i = [0]
                quant_i = [0]
                quant_ld_i = [0]

                def drip_weight_work(n_abs, n_quant):
                    for _ in range(n_abs):
                        if abs_i[0] < len(abs_jobs):
                            srcv, col, width, tag = abs_jobs[abs_i[0]]
                            emit_abs_chunk(srcv, col, width, tag, abs_i[0] % 2)
                            abs_i[0] += 1
                            if abs_i[0] == 32:
                                emit_scales(0, 32, 1, float(H) * FF, C_MW1, "w1")
                            if abs_i[0] == 64:
                                emit_scales(32, 32, 2, float(H) * FF, C_MW2, "w2")
                    for _ in range(n_quant):
                        if abs_i[0] >= 64 and quant_i[0] < len(quant_jobs):
                            i = quant_i[0]
                            while quant_ld_i[0] <= min(i + 1,
                                                       len(quant_jobs) - 1):
                                srcv, dst, mwtag, tag = \
                                    quant_jobs[quant_ld_i[0]]
                                emit_wq_load(srcv, quant_ld_i[0] % 2, tag)
                                quant_ld_i[0] += 1
                            srcv, dst, mwtag, tag = quant_jobs[i]
                            emit_wquant_chunk(srcv, dst, i % 2, mwtag, tag)
                            quant_i[0] += 1

                def stage1_ln(qt):
                    xin, xtok = xinB[qt % 2], f"xinB{qt % 2}"
                    y_dma(xin[:], d["x_own"][qt * 128:(qt + 1) * 128, :],
                          [xtok], [xtok], f"B{qt}.x", sem=f"yb{qt % 2}")
                    y_dma(schiB[:], d["scq_hi"][qt * 128:(qt + 1) * 128, :],
                          ["schiB"], ["schiB"], f"B{qt}.h", sem="ybh")
                    emit_ln(xin, xtok, xnB, "xnB", qro, "qro", 0, pre=f"B{qt}.")

                def stage1_pe(qt):
                    qf, qftok = qfm[qt % 2], f"qfm{qt % 2}"
                    emit_tr(xnB, "xnB", 8, xnfmB, "xnfmB", lambda j: j * 128,
                            f"B{qt}.xn", per_chunk=True)
                    for half, bk_i in enumerate((0, 1)):
                        for dj in range(8):
                            t_op(lambda e, dj=dj, bk_i=bk_i, half=half: e.matmul(
                                    ps_[bk_i][:], xnfmB[:, dj * 128:(dj + 1) * 128],
                                    wq_mix[:, dj * H + half * 512:
                                           dj * H + half * 512 + 512],
                                    start=(dj == 0), stop=False),
                                 [f"xnfmB#{dj}", "wq_mix"], [f"ps{bk_i}"],
                                 f"B{qt}.q{half}.{dj}")
                        t_op(lambda e, bk_i=bk_i, half=half: e.matmul(
                                ps_[bk_i][:], ones2[:],
                                bq_rows[:, half * 512:half * 512 + 512],
                                start=False, stop=True),
                             ["ones2", "bq_rows"], [f"ps{bk_i}"], f"B{qt}.qb{half}")
                    v_op(lambda e: e.tensor_tensor(out=qro[:, 0:512], in0=ps_[0][:],
                                                   in1=schiB[:], op=ALU.mult),
                         ["ps0", "schiB"], ["qro"], f"B{qt}.r0")
                    v_op(lambda e: e.tensor_tensor(out=qro[:, 512:H], in0=ps_[1][:],
                                                   in1=schiB[:], op=ALU.mult),
                         ["ps1", "schiB"], ["qro"], f"B{qt}.r1")

                def stage1_pe2(qt):
                    qf, qftok = qfm[qt % 2], f"qfm{qt % 2}"
                    emit_tr(qro, "qro", 8, qf, qftok, lambda j: j * 128,
                            f"B{qt}.q", per_chunk=True)

                def emit_scores(qt, ln_cb=None):
                    qf, qftok = qfm[qt % 2], f"qfm{qt % 2}"
                    for kc in range(8):
                        bk_i = kc % 4
                        for ej in range(8):
                            t_op(lambda e, ej=ej, kc=kc, bk_i=bk_i, qf=qf: e.matmul(
                                    ps_[bk_i][:], qf[:, ej * 128:(ej + 1) * 128],
                                    big0[:, ej * S + kc * 512:ej * S + kc * 512 + 512],
                                    start=(ej == 0), stop=(ej == 7)),
                                 [f"{qftok}#{ej}", "big0"], [f"ps{bk_i}"],
                                 f"B{qt}.s{kc}.{ej}")
                        s_op(lambda e, kc=kc, bk_i=bk_i: e.copy(
                                ssb[:, kc * 512:(kc + 1) * 512], ps_[bk_i][:]),
                             [f"ps{bk_i}"], [f"ssb{kc}"], f"B{qt}.sev{kc}")
                        v_op(lambda e, kc=kc: e.reduce_max(
                                st2[:, 8 + kc:9 + kc],
                                ssb[:, kc * 512:(kc + 1) * 512], axis=AX.X),
                             [f"ssb{kc}"], [f"smx{kc}"], f"B{qt}.smx{kc}")
                        if kc == 2 and ln_cb is not None:
                            ln_cb()
                        drip_weight_work(1, 0)

                def stage2(qt):
                    xin, xtok = xinB[qt % 2], f"xinB{qt % 2}"
                    v_op(lambda e: e.reduce_max(stc(10), st2[:, 8:16],
                                                axis=AX.X),
                         [f"smx{kc}" for kc in range(8)], ["qM"],
                         f"B{qt}.rmax")
                    s_op(lambda e: e.activation(out=stc(11), in_=stc(10),
                                                func=AF.Identity,
                                                scale=-INV_SCALE),
                         ["qM"], ["ebias"], f"B{qt}.eb")
                    for kc in range(8):
                        bk_i = kc % 4
                        s_op(lambda e, kc=kc, bk_i=bk_i: e.activation(
                                out=probs[bk_i][:],
                                in_=ssb[:, kc * 512:(kc + 1) * 512], func=AF.Exp,
                                scale=INV_SCALE, bias=stc(11),
                                accum_out=lparts[:, kc:kc + 1]),
                             [f"ssb{kc}", "ebias"], [f"probs{bk_i}", "lparts"],
                             f"B{qt}.e{kc}")
                        for sub in range(4):
                            t_op(lambda e, sub=sub, bk_i=bk_i: e.transpose(
                                    psT[sub][:, 0:128],
                                    probs[bk_i][:, sub * 128:(sub + 1) * 128],
                                    identb[:]),
                                 [f"probs{bk_i}", "identb"], [f"psT{sub}"],
                                 f"B{qt}.pT{kc}.{sub}")
                            s_op(lambda e, sub=sub: e.copy(
                                    pT[sub][:], psT[sub][:, 0:128]),
                                 [f"psT{sub}"], [f"pT{sub}"],
                                 f"B{qt}.pE{kc}.{sub}")
                        for sub in range(4):
                            kt64 = kc * 4 + sub
                            for half, bk2 in enumerate((2, 3)):
                                t_op(lambda e, sub=sub, kt64=kt64, half=half,
                                     bk2=bk2, first=(kt64 == 0):
                                     e.matmul(ps_[bk2][:], pT[sub][:],
                                              big1[:, kt64 * H + half * 512:
                                                   kt64 * H + half * 512 + 512],
                                              start=first, stop=(kt64 == 31)),
                                     [f"pT{sub}", "big1"], [f"ps{bk2}"],
                                     f"B{qt}.pv{kc}.{sub}.{half}")
                        drip_weight_work(0, 1)
                    v_op(lambda e: e.reduce_sum(stc(12), lparts[:, 0:8], axis=AX.X),
                         ["lparts"], ["lsum"], f"B{qt}.l")
                    v_op(lambda e: e.reciprocal(stc(13), stc(12)),
                         ["lsum"], ["rl"], f"B{qt}.rl")
                    ob, obt = outB[0], "outB0"
                    v_op(lambda e, ob=ob, xin=xin: e.scalar_tensor_tensor(
                            out=ob[:, 0:512], in0=ps_[2][:], scalar=stc(13),
                            in1=xin[:, 0:512], op0=ALU.mult, op1=ALU.add),
                         ["ps2", "rl", xtok], [obt], f"B{qt}.o0")
                    v_op(lambda e, ob=ob, xin=xin: e.scalar_tensor_tensor(
                            out=ob[:, 512:H], in0=ps_[3][:], scalar=stc(13),
                            in1=xin[:, 512:H], op0=ALU.mult, op1=ALU.add),
                         ["ps3", "rl", xtok], [obt], f"B{qt}.o1")
                    g_dma(part_d[qt * 128:(qt + 1) * 128, :], ob[:], [obt], [obt],
                          f"B{qt}.st", sem=f"go{qt % 2}")

                stage1_ln(0)
                stage1_pe(0)
                stage1_pe2(0)
                for qt in range(NT_Q):
                    emit_scores(qt, ln_cb=(
                        (lambda q=qt: stage1_ln(q + 1))
                        if qt + 1 < NT_Q else None))
                    if qt + 1 < NT_Q:
                        stage1_pe(qt + 1)
                    stage2(qt)
                    if qt + 1 < NT_Q:
                        stage1_pe2(qt + 1)
                drip_weight_work(64, 64)

                if debug:
                    g_dma(dbg["d_ssb"], ssb[:], ["ssb"], [], "dbgBs")
                    g_dma(dbg["d_qro"], qro[:], ["qro"], [], "dbgBq")
                    g_dma(dbg["d_lp2"], lparts[:], ["lparts"], [], "dbgBl")
                    g_dma(dbg["d_stats"], stats[:], ["ebias"], [], "dbgB0")
                    g_dma(dbg["d_lparts"], lparts[:], ["lparts"], [], "dbgB1")
                    g_dma(dbg["d_w1q"], w1q_d[:], [], [], "dbgB2")
                sch.finalize_phase(blkB)

        es_c = contextlib.ExitStack()
        with es_c:
            gamma = sb(es_c, "gamma", [128, H], F32)
            beta = sb(es_c, "beta", [128, H], F32)
            xinC = [sb(es_c, f"xinC{i}", [128, H], F32) for i in range(3)]
            xpart = xinC
            xnC = sb(es_c, "xnC", [128, H], F32)
            ln2b = sb(es_c, "ln2b", [128, H], F32)
            zbuf = sb(es_c, "zbuf", [128, 1024], F32)
            cibuf = sb(es_c, "cibuf", [128, 1024], I32)
            m1b = sb(es_c, "m1b", [128, 1024], BF16)
            nxq = sb(es_c, "nxq", [128, H], BF16)
            xqfm = sb(es_c, "xqfm", [128, H], BF16)
            hbuf = sb(es_c, "hbuf", [128, FF], F32)
            nhqd = [sb(es_c, f"nhq{i}", [128, 1024], BF16) for i in range(2)]
            hqfm = sb(es_c, "hqfm", [128, 2048], BF16)
            b1rows = sb(es_c, "b1rows", [4, FF], BF16)
            b2rows = sb(es_c, "b2rows", [4, H], BF16)
            blhs1 = sb(es_c, "blhs1", [4, 128], BF16)
            blhs2 = sb(es_c, "blhs2", [4, 128], BF16)
            stg4 = zbuf[0:4, :]
            stg4h = m1b[0:4, :]
            stg4l = nhqd[0][0:4, :]

            blkC = es_c.enter_context(nc.Block("phaseC"))

            for j in range(4):
                y_dma(big0[:, j * 8192:(j + 1) * 8192],
                      w1q_d[:, j * 8192:(j + 1) * 8192],
                      ["big0"], ["big0"], f"C.w1q{j}", sem="yr")
            for j in range(4):
                y_dma(big1[:, j * 8192:(j + 1) * 8192],
                      w2q_d[:, j * 8192:(j + 1) * 8192],
                      ["big1"], ["big1"], f"C.w2q{j}", sem="yr")
            y_dma(gamma[:], d["gamma_bc"], [], ["gamma"], "C.g", sem="yr")
            y_dma(beta[:], d["beta_bc"], [], ["beta"], "C.b", sem="yr")

            def emit_bsplit(src, nq, rows, rtok, tag):
                w = nq
                y_dma(stg4[:, 0:w], src, ["zbuf"], ["zbuf"], f"{tag}.ld", sem="yr")
                v_op(lambda e, w=w: e.tensor_copy(stg4h[:, 0:w], stg4[:, 0:w]),
                     ["zbuf"], ["m1b"], f"{tag}.hi")
                v_op(lambda e, w=w: e.scalar_tensor_tensor(
                        out=stg4l[:, 0:w], in0=stg4[:, 0:w], scalar=1.0,
                        in1=stg4h[:, 0:w], op0=ALU.mult, op1=ALU.subtract),
                     ["zbuf", "m1b"], ["nhq0"], f"{tag}.lo")
                full = 4 * w
                for r, srct, srcb in ((0, "m1b", stg4h), (1, "m1b", stg4h),
                                      (2, "nhq0", stg4l), (3, "nhq0", stg4l)):
                    for p in range(4):
                        g_dma(rows[r:r + 1, p * w:(p + 1) * w],
                              srcb[p:p + 1, 0:w],
                              [srct], [rtok], f"{tag}.mv{r}.{p}")

            emit_bsplit(d["b1r"], FF // 4, b1rows, "b1rows", "b1")
            emit_bsplit(d["b2r"], H // 4, b2rows, "b2rows", "b2")

            CQ = 21

            st2b = sb(es_c, "st2b", [128, 8], BF16)

            def emit_aquant(src, stoks, sx_tok, sx_col, width, off, nxq_out,
                            nxq_tok, tag):
                sl = slice(off, off + width)
                v_op(lambda e: e.scalar_tensor_tensor(
                        out=zbuf[:, 0:width], in0=src[:, sl],
                        scalar=stc(sx_col), in1=src[:, sl],
                        op0=ALU.mult, op1=ALU.bypass),
                     list(stoks) + [sx_tok], ["zbuf"], f"{tag}.z")
                v_op(lambda e: e.tensor_copy(cibuf[:, 0:width], zbuf[:, 0:width]),
                     ["zbuf"], ["cibuf"], f"{tag}.ci")
                v_op(lambda e: e.scalar_tensor_tensor(
                        out=zbuf[:, 0:width], in0=zbuf[:, 0:width], scalar=1.0,
                        in1=cibuf[:, 0:width], op0=ALU.mult, op1=ALU.subtract),
                     ["zbuf", "cibuf"], ["zbuf"], f"{tag}.d")
                v_op(lambda e: e.tensor_scalar(
                        out=m1b[:, 0:width], in0=zbuf[:, 0:width],
                        scalar1=0.5, scalar2=None, op0=ALU.is_ge),
                     ["zbuf"], ["m1b"], f"{tag}.m1")
                v_op(lambda e: e.scalar_tensor_tensor(
                        out=m1b[:, 0:width], in0=zbuf[:, 0:width], scalar=-0.5,
                        in1=m1b[:, 0:width], op0=ALU.is_le, op1=ALU.subtract),
                     ["zbuf", "m1b"], ["m1b"], f"{tag}.mc")
                v_op(lambda e: e.scalar_tensor_tensor(
                        out=nxq_out, in0=cibuf[:, 0:width], scalar=-1.0,
                        in1=m1b[:, 0:width], op0=ALU.mult, op1=ALU.add),
                     ["cibuf", "m1b"], [nxq_tok], f"{tag}.nx")

            def emit_split(src_col, srctok, c0, pre):
                s_op(lambda e: e.copy(st2b[:, c0:c0 + 1], stc(src_col)),
                     [srctok], [f"{pre}hb"], f"{pre}.hb")
                s_op(lambda e: e.copy(st2b[:, c0 + 2:c0 + 3],
                                      st2b[:, c0:c0 + 1]),
                     [f"{pre}hb"], [f"{pre}hb2"], f"{pre}.hb2")
                s_op(lambda e: e.copy(st2[:, 5:6], st2b[:, c0:c0 + 1]),
                     [f"{pre}hb"], [f"{pre}hf"], f"{pre}.hf")
                s_op(lambda e: e.activation(out=st2[:, 6:7], in_=st2[:, 5:6],
                                            func=AF.Identity, scale=-1.0,
                                            bias=stc(src_col)),
                     [f"{pre}hf", srctok], [f"{pre}lf"], f"{pre}.lf")
                s_op(lambda e: e.copy(st2b[:, c0 + 1:c0 + 2], st2[:, 6:7]),
                     [f"{pre}lf"], [f"{pre}lb"], f"{pre}.lb")
                s_op(lambda e: e.copy(st2b[:, c0 + 3:c0 + 4], st2[:, 6:7]),
                     [f"{pre}lf"], [f"{pre}lb2"], f"{pre}.lb2")

            def emit_blhs(c0, pre, rows, rtok, tag):
                t_op(lambda e: e.transpose(psT[0][0:4, 0:128],
                                           st2b[:, c0:c0 + 4], identb[:]),
                     [f"{pre}hb", f"{pre}hb2", f"{pre}lb", f"{pre}lb2",
                      "identb"], ["psT0"], f"{tag}.T")
                s_op(lambda e: e.copy(rows[0:4, :], psT[0][0:4, 0:128]),
                     ["psT0"], [rtok], f"{tag}.E")

            def c_front_vs(tt):
                xin, xtok = xinC[tt % 3], f"xinC{tt % 3}"
                y_dma(xin[:], d["x_own"][tt * 128:(tt + 1) * 128, :],
                      [xtok], [xtok], f"C{tt}.x", sem=f"yc{tt % 3}")
                emit_ln(xin, xtok, xnC, "xnC", zbuf[:, 0:H], "zbuf", 0,
                        xnsum_col=7, pre=f"C{tt}.")
                v_op(lambda e: e.tensor_scalar(out=stc(CQ), in0=stc(7),
                                               scalar1=-1.0 / H, scalar2=None,
                                               op0=ALU.mult),
                     ["xnsum7"], ["nmu2"], f"C{tt}.mu2")
                s_op(lambda e: e.activation(out=zbuf[:, 0:H], in_=xnC[:],
                                            func=AF.Square, bias=stc(CQ),
                                            accum_out=stc(CQ + 1)),
                     ["xnC", "nmu2"], ["zbuf", "v2s"], f"C{tt}.v2")
                v_op(lambda e: e.tensor_scalar(out=stc(CQ + 2), in0=stc(CQ + 1),
                                               scalar1=1.0 / H, scalar2=EPS,
                                               op0=ALU.mult, op1=ALU.add),
                     ["v2s"], ["v2p"], f"C{tt}.vp2")
                v_op(lambda e: e.reciprocal(stc(CQ + 3), stc(CQ + 2)),
                     ["v2p"], ["ri2"], f"C{tt}.ri2")
                s_op(lambda e: e.activation(out=stc(CQ + 4), in_=stc(CQ + 3),
                                            func=AF.Sqrt),
                     ["ri2"], ["rs2"], f"C{tt}.rs2")
                v_op(lambda e: e.tensor_tensor(out=stc(CQ + 5), in0=stc(CQ),
                                               in1=stc(CQ + 4), op=ALU.mult),
                     ["nmu2", "rs2"], ["nm2r"], f"C{tt}.nm2r")
                s_op(lambda e: e.activation(out=ln2b[:], in_=xnC[:], func=AF.Identity,
                                            scale=stc(CQ + 4), bias=stc(CQ + 5)),
                     ["xnC", "rs2", "nm2r"], ["ln2b"], f"C{tt}.l2n")
                v_op(lambda e: e.tensor_tensor(out=ln2b[:], in0=ln2b[:],
                                               in1=gamma[:], op=ALU.mult),
                     ["ln2b", "gamma"], ["ln2b"], f"C{tt}.gam")
                v_op(lambda e: e.tensor_tensor(out=ln2b[:], in0=ln2b[:],
                                               in1=beta[:], op=ALU.add),
                     ["ln2b", "beta"], ["ln2b"], f"C{tt}.bet")
                v_op(lambda e: e.reduce_max(stc(CQ + 6), ln2b[:], axis=AX.X,
                                            apply_absolute_value=True),
                     ["ln2b"], ["mx1"], f"C{tt}.mx1")
                v_op(lambda e: e.tensor_scalar(out=stc(CQ + 7), in0=stc(CQ + 6),
                                               scalar1=EPS, scalar2=None,
                                               op0=ALU.max),
                     ["mx1"], ["den1"], f"C{tt}.den1")
                v_op(lambda e: e.reciprocal(stc(CQ + 8), stc(CQ + 7)),
                     ["den1"], ["rd1"], f"C{tt}.rd1")
                v_op(lambda e: e.tensor_scalar(out=stc(CQ + 9), in0=stc(CQ + 8),
                                               scalar1=127.0, scalar2=None,
                                               op0=ALU.mult),
                     ["rd1"], ["sx1"], f"C{tt}.sx1")
                rqc = 40 + (tt % 2)
                v_op(lambda e, rqc=rqc: e.tensor_tensor(
                        out=stc(rqc), in0=stc(CQ + 7),
                        in1=stc(C_MW1), op=ALU.mult),
                     ["den1"], ["rqt1"], f"C{tt}.rt1")
                v_op(lambda e, rqc=rqc: e.tensor_scalar(
                        out=stc(rqc), in0=stc(rqc),
                        scalar1=1.0 / 127.0, scalar2=None,
                        op0=ALU.mult),
                     ["rqt1"], [f"rq1{tt % 2}"], f"C{tt}.rq1")
                v_op(lambda e, rqc=rqc: e.reciprocal(stc(CQ + 11), stc(rqc)),
                     [f"rq1{tt % 2}"], ["sr1"], f"C{tt}.sr1")

            def c_front_vs2(tt):
                emit_split(CQ + 11, "sr1", 0, "s1")
                emit_aquant(ln2b, ["ln2b"], "sx1", CQ + 9, H, 0, nxq[:], "nxq",
                            f"C{tt}.xq")

            def c_front_pe(tt):
                emit_tr(nxq, "nxq", 8, xqfm, "xqfm", lambda j: j * 128,
                        f"C{tt}.xf", per_chunk=True)

            def c_mid(tt, cb1=None, cb2=None):
                for hc in range(8):
                    bk_i = hc % 4
                    if hc == 0 and cb1 is not None:
                        cb1()
                    if hc == 4 and cb2 is not None:
                        cb2()
                    for dj in range(8):
                        t_op(lambda e, dj=dj, hc=hc, bk_i=bk_i: e.matmul(
                                ps_[bk_i][:], xqfm[:, dj * 128:(dj + 1) * 128],
                                big0[:, dj * FF + hc * 512:dj * FF + hc * 512 + 512],
                                start=(dj == 0), stop=False),
                             [f"xqfm#{dj}", "big0"], [f"ps{bk_i}"],
                             f"C{tt}.m1.{hc}.{dj}")
                    t_op(lambda e, hc=hc, bk_i=bk_i: e.matmul(
                            ps_[bk_i][:], blhs1[:],
                            b1rows[:, hc * 512:hc * 512 + 512],
                            start=False, stop=True),
                         ["blhs1", "b1rows"], [f"ps{bk_i}"], f"C{tt}.bb1.{hc}")
                    s_op(lambda e, hc=hc, bk_i=bk_i, tt=tt: e.activation(
                            out=hbuf[:, hc * 512:(hc + 1) * 512], in_=ps_[bk_i][:],
                            func=AF.Gelu, scale=stc(40 + (tt % 2))),
                         [f"ps{bk_i}", f"rq1{tt % 2}"], [f"hbuf#{hc}"],
                         f"C{tt}.g{hc}")

            def c_tail(tt):
                for hc in range(8):
                    v_op(lambda e, hc=hc: e.reduce_max(
                            st2[:, 8 + hc:9 + hc], hbuf[:, hc * 512:(hc + 1) * 512],
                            axis=AX.X, apply_absolute_value=True),
                         [f"hbuf#{hc}"], [f"hm{hc}"], f"C{tt}.hm{hc}")
                v_op(lambda e: e.reduce_max(stc(CQ + 15), st2[:, 8:16], axis=AX.X),
                     [f"hm{h}" for h in range(8)], ["mx2"], f"C{tt}.mx2")
                v_op(lambda e: e.tensor_scalar(out=stc(CQ + 16), in0=stc(CQ + 15),
                                               scalar1=EPS, scalar2=None,
                                               op0=ALU.max),
                     ["mx2"], ["den2"], f"C{tt}.den2")
                v_op(lambda e: e.reciprocal(stc(CQ + 17), stc(CQ + 16)),
                     ["den2"], ["rd2"], f"C{tt}.rd2")
                v_op(lambda e: e.tensor_scalar(out=stc(CQ + 12), in0=stc(CQ + 17),
                                               scalar1=127.0, scalar2=None,
                                               op0=ALU.mult),
                     ["rd2"], ["sx2"], f"C{tt}.sx2")
                v_op(lambda e: e.tensor_tensor(out=stc(CQ + 13), in0=stc(CQ + 16),
                                               in1=stc(C_MW2), op=ALU.mult),
                     ["den2"], ["rqt2"], f"C{tt}.rt2")
                v_op(lambda e: e.tensor_scalar(out=stc(CQ + 13), in0=stc(CQ + 13),
                                               scalar1=1.0 / 127.0, scalar2=None,
                                               op0=ALU.mult),
                     ["rqt2"], ["rq2"], f"C{tt}.rq2")
                v_op(lambda e: e.reciprocal(stc(CQ + 14), stc(CQ + 13)),
                     ["rq2"], ["sr2"], f"C{tt}.sr2")
                emit_split(CQ + 14, "sr2", 4, "s2")

                def emit_aq_q(q):
                    emit_aquant(hbuf, [f"hbuf#{2 * q}", f"hbuf#{2 * q + 1}"],
                                "sx2", CQ + 12, 1024, q * 1024,
                                nhqd[q % 2][:], f"nhq{q % 2}",
                                f"C{tt}.hq{q}")

                def emit_hqT(q):
                    emit_tr(nhqd[q % 2], f"nhq{q % 2}", 8, hqfm, "hqfm",
                            lambda j, q=q: (q % 2) * 1024 + j * 128,
                            f"C{tt}.hf{q}", per_chunk=True)

                def emit_mm2(hf, rng=range(16)):
                    for hj16 in rng:
                        hj = hf * 16 + hj16
                        for half, bk2 in enumerate((2, 3)):
                            t_op(lambda e, hj16=hj16, hj=hj, half=half, bk2=bk2:
                                 e.matmul(ps_[bk2][:],
                                          hqfm[:, hj16 * 128:(hj16 + 1) * 128],
                                          big1[:, hj * H + half * 512:
                                               hj * H + half * 512 + 512],
                                          start=(hj == 0), stop=False),
                                 [f"hqfm#{hj16}", "big1"], [f"ps{bk2}"],
                                 f"C{tt}.m2.{hj}.{half}")

                emit_aq_q(0)
                emit_aq_q(1)
                emit_hqT(0)
                emit_aq_q(2)
                emit_hqT(1)
                emit_aq_q(3)
                emit_mm2(0)
                emit_hqT(2)
                emit_mm2(1, range(8))
                emit_hqT(3)
                emit_blhs(4, "s2", blhs2, "blhs2", f"C{tt}.b2")
                emit_mm2(1, range(8, 16))
                for half, bk2 in enumerate((2, 3)):
                    t_op(lambda e, half=half, bk2=bk2: e.matmul(
                            ps_[bk2][:], blhs2[:],
                            b2rows[:, half * 512:half * 512 + 512],
                            start=False, stop=True),
                         ["blhs2", "b2rows"], [f"ps{bk2}"], f"C{tt}.bb2.{half}")
                if tt + 1 < NT_Q:
                    emit_blhs(0, "s1", blhs1, "blhs1", f"C{tt + 1}.b1")

            def c_pload(tt):
                y_dma(xpart[tt % 3][:], part_d[tt * 128:(tt + 1) * 128, :],
                      [f"xinC{tt % 3}"], [f"xinC{tt % 3}"], f"C{tt}.p",
                      sem=f"yc{tt % 3}")

            def c_out(tt):
                xp, xptok = xpart[tt % 3], f"xinC{tt % 3}"
                v_op(lambda e, xp=xp: e.scalar_tensor_tensor(
                        out=xp[:, 0:512], in0=ps_[2][:], scalar=stc(CQ + 13),
                        in1=xp[:, 0:512], op0=ALU.mult, op1=ALU.add),
                     ["ps2", "rq2", xptok], [xptok], f"C{tt}.o0")
                v_op(lambda e, xp=xp: e.scalar_tensor_tensor(
                        out=xp[:, 512:H], in0=ps_[3][:], scalar=stc(CQ + 13),
                        in1=xp[:, 512:H], op0=ALU.mult, op1=ALU.add),
                     ["ps3", "rq2", xptok], [xptok], f"C{tt}.o1")
                g_dma(out_d[tt * 128:(tt + 1) * 128, :], xp[:], [xptok],
                      [xptok], f"C{tt}.st", sem=f"gc{tt % 2}")

            c_front_vs(0)
            c_front_vs2(0)
            emit_blhs(0, "s1", blhs1, "blhs1", "C0.b1")
            c_front_pe(0)
            for tt in range(NT_Q):
                if tt > 0:
                    c_out(tt - 1)
                c_mid(tt,
                      cb1=((lambda t=tt: c_front_vs(t + 1))
                           if tt + 1 < NT_Q else None),
                      cb2=((lambda t=tt: c_front_vs2(t + 1))
                           if tt + 1 < NT_Q else None))
                if tt + 1 < NT_Q:
                    c_front_pe(tt + 1)
                c_pload(tt)
                c_tail(tt)
                if debug and tt == 0:
                    g_dma(dbg["d_xnC"], xnC[:], ["xnC"], [], "dbg.xnC")
                    g_dma(dbg["d_ln2b"], ln2b[:], ["ln2b"], [], "dbg.ln2b")
                    g_dma(dbg["d_z"], zbuf[:], ["zbuf"], [], "dbg.z")
                    g_dma(dbg["d_nxq"], nxq[:], ["nxq"], [], "dbg.nxq")
                    g_dma(dbg["d_cstats"], stats[:], ["sr1"], [], "dbg.cst")
            c_out(NT_Q - 1)
            sch.finalize_phase(blkC, final=True)

    return nc


_NC_CACHE = []


def kernel(**inputs):
    X = np.ascontiguousarray(np.asarray(inputs["L0_attn_hidden_states"],
                                        dtype=np.float32))
    Wq = np.asarray(inputs["L0_attn_Wq"], dtype=np.float32)
    Wk = np.asarray(inputs["L0_attn_Wk"], dtype=np.float32)
    Wv = np.asarray(inputs["L0_attn_Wv"], dtype=np.float32)
    bq = np.asarray(inputs["L0_attn_bq"], dtype=np.float32)
    bk = np.asarray(inputs["L0_attn_bk"], dtype=np.float32)
    bv = np.asarray(inputs["L0_attn_bv"], dtype=np.float32)
    w1 = np.asarray(inputs["L0_ffn_w1"], dtype=np.float32)
    b1 = np.asarray(inputs["L0_ffn_b1"], dtype=np.float32)
    w2 = np.asarray(inputs["L0_ffn_w2"], dtype=np.float32)
    b2 = np.asarray(inputs["L0_ffn_b2"], dtype=np.float32)
    gamma = np.asarray(inputs["L0_ffn_gamma"], dtype=np.float32)
    beta = np.asarray(inputs["L0_ffn_beta"], dtype=np.float32)

    perm = np.concatenate([np.arange(0, H, 2), np.arange(1, H, 2)])
    wqT = np.ascontiguousarray(Wq.T[:, perm])
    wkT = np.ascontiguousarray(Wk.T[:, perm])
    wvT = np.ascontiguousarray(Wv.T)
    w1T = np.ascontiguousarray(w1.T)
    w2T = np.ascontiguousarray(w2.T)

    pos = np.arange(S, dtype=np.float32)
    inv = (10000.0 ** (np.arange(0, 512, 2, dtype=np.float64) / 512.0)).astype(np.float32)
    rad = (pos[:, None] / inv).astype(np.float32)
    sc_hi = np.stack([np.sin(rad.astype(np.float64)),
                      np.cos(rad.astype(np.float64))],
                     axis=-1).reshape(S, 512).astype(np.float32)

    identb = np.eye(128, dtype=BFDT)
    identf = np.eye(128, dtype=np.float32)

    in_maps = []
    for c in range(8):
        b, hh = c // 2, c % 2
        m = {
            "x_full": np.ascontiguousarray(X[b]),
            "x_own": np.ascontiguousarray(X[b, hh * HALF:(hh + 1) * HALF]),
            "wqT": wqT, "wkT": wkT, "wvT": wvT,
            "bq2": bq[perm][None, :], "bk2": bk[perm][None, :], "bv2": bv[None, :],
            "w1T": w1T, "w2T": w2T,
            "b1r": np.ascontiguousarray(b1.reshape(4, FF // 4)),
            "b2r": np.ascontiguousarray(b2.reshape(4, H // 4)),
            "gamma_bc": np.ascontiguousarray(np.broadcast_to(gamma, (128, H))),
            "beta_bc": np.ascontiguousarray(np.broadcast_to(beta, (128, H))),
            "sck_hi": sc_hi,
            "scq_hi": np.ascontiguousarray(sc_hi[hh * HALF:(hh + 1) * HALF]),
            "ident_b": identb, "ident_f": identf,
        }
        in_maps.append(m)

    if not _NC_CACHE:
        _NC_CACHE.append(build_nc())
    nc = _NC_CACHE[0]
    res = run_bass_kernel_spmd(nc, in_maps, core_ids=list(range(8)))
    out = np.empty((B, S, H), dtype=np.float32)
    for c in range(8):
        b, hh = c // 2, c % 2
        out[b, hh * HALF:(hh + 1) * HALF] = res.results[c]["out"]
    return out


# version 24 (final)
# speedup vs baseline: 1.0612x; 1.0167x over previous
import contextlib
import math

import numpy as np
import ml_dtypes

import concourse.bass as bass
import concourse.mybir as mybir
from concourse.bass_utils import run_bass_kernel_spmd

F32 = mybir.dt.float32
BF16 = mybir.dt.bfloat16
I32 = mybir.dt.int32
AF = mybir.ActivationFunctionType
ALU = mybir.AluOpType
AX = mybir.AxisListType
BFDT = ml_dtypes.bfloat16

B, S, H = 4, 4096, 1024
HALF = S // 2
FF = 4096
NT_K = S // 128
NT_Q = HALF // 128
INV_SCALE = 1.0 / math.sqrt(512.0)
EPS = 1e-5


class Sched:

    ENGINES = ("t", "v", "s", "g")
    STREAM_OF_Q = {"y": "y", "gd": "g"}

    def __init__(self, nc, es):
        self.nc = nc
        self.es = es
        self.sems = {}
        self.sem_policy = {}
        self.cnt = {}
        for e in self.ENGINES:
            self._mk(e, "ord")
        self.pending = []
        self.last_writer = {}
        self.readers = {}
        self.hw = {}
        self.bar_vals = None
        self.bar_seen = set()

    def _mk(self, name, policy):
        if name not in self.sems:
            self.sems[name] = self.es.enter_context(self.nc.semaphore("sem_" + name))
            self.sem_policy[name] = policy
            self.cnt[name] = 0

    def dma_sem(self, name, policy="ord"):
        self._mk(name, policy)

    def op(self, eng, emit, reads=(), writes=(), tag="", sem=None, stream=None,
           small=True):
        if eng in ("y", "gd"):
            assert sem is not None, tag
            self._mk(sem, self.sem_policy.get(sem, "ord"))
            stream = self.STREAM_OF_Q[eng]
            step = 16
        else:
            sem = eng
            stream = eng
            step = 1
        i = len(self.pending)
        deps = set()
        for r in reads:
            w = self.last_writer.get(r)
            if w is not None:
                deps.add(w)
            self.readers.setdefault(r, []).append(i)
        for w in writes:
            pw = self.last_writer.get(w)
            if pw is not None:
                deps.add(pw)
            for rd in self.readers.get(w, ()):
                deps.add(rd)
            self.last_writer[w] = i
            self.readers[w] = []
        self.pending.append(dict(eng=eng, sem=sem, stream=stream, step=step,
                                 emit=emit, deps=deps, rawdeps=set(deps),
                                 tag=tag, small=small,
                                 need_inc=(eng in ("y", "gd")), inc_ord=None))

    def finalize_phase(self, block, final=False):
        ops = self.pending
        for i, o in enumerate(ops):
            kept = set()
            same_eng_deps = []
            for dd in o["deps"]:
                po = ops[dd]
                if po["eng"] == o["eng"] and po["eng"] not in ("y", "gd"):
                    same_eng_deps.append(dd)
                    continue
                if po["eng"] == o["eng"] and po["sem"] == o["sem"]:
                    continue
                kept.add(dd)
            o["deps"] = kept
            o["same_eng_deps"] = same_eng_deps
            for dd in kept:
                ops[dd]["need_inc"] = True
        last_flush = {}
        for i, o in enumerate(ops):
            o["pre_drain"] = False
            if o["eng"] in ("v", "s", "g"):
                e = o["eng"]
                for dd in o["same_eng_deps"]:
                    if ops[dd]["small"] and dd > last_flush.get(e, -1):
                        o["pre_drain"] = True
                        last_flush[e] = i - 1 if i else -1
                        break
                if o["need_inc"]:
                    last_flush[e] = i
        last_of = {}
        for i, o in enumerate(ops):
            last_of[o["eng"] if o["eng"] in self.ENGINES else o["sem"]] = i
        for i in last_of.values():
            ops[i]["need_inc"] = True
        totals_at = [None] * len(ops)
        run_tot = dict(self.cnt)
        for i, o in enumerate(ops):
            totals_at[i] = None
            if o["need_inc"]:
                run_tot[o["sem"]] = run_tot.get(o["sem"], 0) + 1
                o["inc_ord"] = run_tot[o["sem"]]
        pre_tot = {}
        run2 = dict(self.cnt)
        pref = []
        for i, o in enumerate(ops):
            pref.append(dict(run2))
            if o["need_inc"]:
                run2[o["sem"]] = run2.get(o["sem"], 0) + 1
        waits = [[] for _ in ops]
        for i, o in enumerate(ops):
            st = o["stream"]
            need = {}
            if self.bar_vals is not None and st not in self.bar_seen:
                self.bar_seen.add(st)
                for pe, v in self.bar_vals.items():
                    if v > 0:
                        need[pe] = v
            for dd in o["deps"]:
                po = ops[dd]
                assert po["inc_ord"] is not None, (po["tag"], o["tag"])
                if self.sem_policy.get(po["sem"], "ord") == "total":
                    v = pref[i].get(po["sem"], 0) * po["step"]
                    v = max(v, po["inc_ord"] * po["step"])
                else:
                    v = po["inc_ord"] * po["step"]
                if v > need.get(po["sem"], 0):
                    need[po["sem"]] = v
            for pe, v in need.items():
                k = (st, pe)
                if v > self.hw.get(k, 0):
                    self.hw[k] = v
                    waits[i].append((pe, v))
        per_stream = {"t": [], "v": [], "s": [], "g": [], "y": []}
        for i, o in enumerate(ops):
            per_stream[o["stream"]].append(i)
        sems = self.sems
        self.cnt = run_tot
        fin_vals = {nm: self.cnt.get(nm, 0) * (16 if self.sem_policy[nm] != "ord"
                    or nm not in self.ENGINES else 1) for nm in self.sems}
        fin_vals = {nm: self.cnt.get(nm, 0) * (1 if nm in self.ENGINES else 16)
                    for nm in self.sems}

        def run_stream(engh, stream):
            for i in per_stream[stream]:
                o = ops[i]
                for (pe, v) in waits[i]:
                    engh.wait_ge(sems[pe], v)
                if o.get("pre_drain"):
                    engh.drain()
                instr = o["emit"](engh)
                if o["need_inc"]:
                    if o["eng"] in ("v", "s", "g"):
                        engh.drain().then_inc(sems[o["sem"]],
                                                          o["step"])
                    else:
                        assert instr is not None, o["tag"]
                        instr.then_inc(sems[o["sem"]], o["step"])
            if final:
                for pe, v in fin_vals.items():
                    k = (stream, pe)
                    if v > self.hw.get(k, 0) and v > 0:
                        engh.wait_ge(sems[pe], v)

        @block.sync
        def _(eng):
            run_stream(eng, "y")

        @block.tensor
        def _(eng):
            run_stream(eng, "t")

        @block.vector
        def _(eng):
            run_stream(eng, "v")

        @block.scalar
        def _(eng):
            run_stream(eng, "s")

        @block.gpsimd
        def _(eng):
            run_stream(eng, "g")

        self.bar_vals = fin_vals
        self.bar_seen = set()
        self.pending = []
        self.last_writer = {}
        self.readers = {}


def build_nc(debug=False, n_ct=NT_Q):
    nc = bass.Bass("TRN2", target_bir_lowering=False, detect_race_conditions=False)

    d = {}
    for name, shape, dt in [
            ("x_full", [S, H], F32), ("x_own", [HALF, H], F32),
            ("wqT", [H, H], F32), ("wkT", [H, H], F32), ("wvT", [H, H], F32),
            ("bq2", [1, H], F32), ("bk2", [1, H], F32), ("bv2", [1, H], F32),
            ("w1T", [H, FF], F32), ("w2T", [FF, H], F32),
            ("b1r", [4, FF // 4], F32), ("b2r", [4, H // 4], F32),
            ("gamma_bc", [128, H], F32), ("beta_bc", [128, H], F32),
            ("sck_hi", [S, 512], F32),
            ("scq_hi", [HALF, 512], F32),
            ("ident_b", [128, 128], BF16), ("ident_f", [128, 128], F32)]:
        d[name] = nc.dram_tensor(name, shape, dt, kind="ExternalInput").ap()
    out_d = nc.dram_tensor("out", [HALF, H], F32, kind="ExternalOutput").ap()
    dbg = {}
    if debug:
        for nm, sh, dt_ in [("d_ssb", [128, S], F32), ("d_qro", [128, H], BF16),
                            ("d_lp2", [128, 8], F32),
                            ("d_xnC", [128, H], F32), ("d_ln2b", [128, H], F32),
                            ("d_z", [128, 1024], F32), ("d_nxq", [128, H], BF16),
                            ("d_cstats", [128, 64], F32),
                            ("d_xin0", [128, H], F32), ("d_xn0", [128, H], BF16),
                            ("d_kro0", [128, H], BF16), ("d_ps0", [128, 512], F32),
                            ("d_big0", [128, 8 * S], BF16), ("d_big1", [128, NT_K * H], BF16),
                            ("d_knorm2", [128, NT_K], F32), ("d_stats", [128, 64], F32),
                            ("d_partial", [HALF, H], F32), ("d_lparts", [128, 8], F32),
                            ("d_wq_mix", [128, 8 * H], BF16), ("d_hbuf", [128, FF], F32),
                            ("d_xnfmA", [128, H], BF16), ("d_w1q", [128, 8 * FF], BF16)]:
            dbg[nm] = nc.dram_tensor(nm, sh, dt_, kind="ExternalOutput").ap()
    part_d = nc.dram_tensor("partial", [HALF, H], F32).ap()
    w1q_d = nc.dram_tensor("w1q_sp", [128, 8 * FF], BF16).ap()
    w2q_d = nc.dram_tensor("w2q_sp", [128, 32 * H], BF16).ap()
    w1v = d["w1T"].rearrange("(a p) n -> p a n", p=128)
    w2v = d["w2T"].rearrange("(a p) n -> p a n", p=128)

    es = contextlib.ExitStack()
    with es:
        sch = Sched(nc, es)
        for nm, pol in (("ys", "total"), ("yr", "total"), ("bm", "total"),
                        ("gm", "total")):
            sch.dma_sem(nm, pol)

        def sb(stk, name, shape, dt):
            return stk.enter_context(nc.sbuf_tensor(name, shape, dt))

        big0 = sb(es, "big0", [128, 8 * S], BF16)
        big1 = sb(es, "big1", [128, NT_K * H], BF16)
        identb = sb(es, "identb", [128, 128], BF16)
        identf = sb(es, "identf", [128, 128], F32)
        onesf = sb(es, "onesf", [128, 128], F32)
        ones2 = sb(es, "ones2", [2, 128], BF16)
        stats = sb(es, "stats", [128, 64], F32)
        st2 = sb(es, "st2", [128, 16], F32)
        knorm2 = sb(es, "knorm2", [128, NT_K], F32)
        lparts = sb(es, "lparts", [128, 8], F32)
        wsums = sb(es, "wsums", [128, 68], F32)
        rowB = sb(es, "rowB", [1, 128], F32)
        ps_ = [es.enter_context(nc.psum_tensor(f"ps{i}", [128, 512], F32))
               for i in range(4)]
        psT = [es.enter_context(nc.psum_tensor(f"psT{i}", [128, 1024], BF16))
               for i in range(4)]

        C_KMAX2, C_MW1, C_MW2, C_SW1, C_SW2 = 16, 17, 18, 19, 20

        def stc(i):
            return stats[:, i:i + 1]

        y_dma = lambda dst, src, reads, writes, tag="", sem="ys": sch.op(
            "y", lambda e, a=dst, b=src: e.dma_start(out=a, in_=b), reads, writes,
            tag, sem=sem)
        g_dma = lambda dst, src, reads, writes, tag="", sem="gm": sch.op(
            "gd", lambda e, a=dst, b=src: e.dma_start(out=a, in_=b), reads, writes,
            tag, sem=sem)
        v_op = lambda fn, reads, writes, tag="": sch.op("v", fn, reads, writes, tag)
        s_op = lambda fn, reads, writes, tag="": sch.op("s", fn, reads, writes, tag)
        t_op = lambda fn, reads, writes, tag="": sch.op("t", fn, reads, writes, tag)
        g_op = lambda fn, reads, writes, tag="": sch.op("g", fn, reads, writes, tag)

        def emit_ln(xin, xtok, xn, xntok, scratch, sctok, c0, xnsum_col=None,
                    pre=""):
            tk = [f"stat{c0 + j}" for j in range(7)]
            v_op(lambda e: e.reduce_sum(stc(c0), xin[:], axis=AX.X),
                 [xtok], [tk[0]], pre + "ln.sum")
            v_op(lambda e: e.tensor_scalar(out=stc(c0 + 1), in0=stc(c0),
                                           scalar1=-1.0 / H, scalar2=None,
                                           op0=ALU.mult),
                 [tk[0]], [tk[1]], pre + "ln.neg")
            s_op(lambda e: e.activation(out=scratch[:], in_=xin[:], func=AF.Square,
                                        bias=stc(c0 + 1), scale=1.0,
                                        accum_out=stc(c0 + 2)),
                 [xtok, tk[1]], [sctok, tk[2]], pre + "ln.var")
            v_op(lambda e: e.tensor_scalar(out=stc(c0 + 3), in0=stc(c0 + 2),
                                           scalar1=1.0 / H, scalar2=EPS,
                                           op0=ALU.mult, op1=ALU.add),
                 [tk[2]], [tk[3]], pre + "ln.vps")
            v_op(lambda e: e.reciprocal(stc(c0 + 4), stc(c0 + 3)),
                 [tk[3]], [tk[4]], pre + "ln.rin")
            s_op(lambda e: e.activation(out=stc(c0 + 5), in_=stc(c0 + 4),
                                        func=AF.Sqrt),
                 [tk[4]], [tk[5]], pre + "ln.rsd")
            v_op(lambda e: e.tensor_tensor(out=stc(c0 + 6), in0=stc(c0 + 1),
                                           in1=stc(c0 + 5), op=ALU.mult),
                 [tk[1], tk[5]], [tk[6]], pre + "ln.nmr")
            acc = None if xnsum_col is None else stc(xnsum_col)
            if acc is None:
                s_op(lambda e: e.activation(out=xn[:], in_=xin[:], func=AF.Identity,
                                            scale=stc(c0 + 5), bias=stc(c0 + 6)),
                     [xtok, tk[5], tk[6]], [xntok], pre + "ln.xn")
            else:
                s_op(lambda e: e.activation(out=xn[:], in_=xin[:], func=AF.Identity,
                                            scale=stc(c0 + 5), bias=stc(c0 + 6),
                                            accum_out=acc),
                     [xtok, tk[5], tk[6]], [xntok, "xnsum7"], pre + "ln.xn")

        def emit_tr(src, stok, nchunks, dst, dtok, colf, tag, src_colf=None,
                    per_chunk=False, ev_alt=False):
            for j in range(nchunks):
                bi = j % 4
                sc = src_colf(j) if src_colf else j * 128
                t_op(lambda e, bi=bi, sc=sc: e.transpose(
                        psT[bi][:, 0:128], src[:, sc:sc + 128], identb[:]),
                     [stok, "identb"], [f"psT{bi}"], f"{tag}.T{j}")
                c0 = colf(j)
                wtok = f"{dtok}#{c0 // 128}" if per_chunk else dtok
                if ev_alt and j % 2 == 1:
                    v_op(lambda e, bi=bi, c0=c0: e.tensor_copy(
                            dst[:, c0:c0 + 128], psT[bi][:, 0:128]),
                         [f"psT{bi}"], [wtok], f"{tag}.E{j}")
                else:
                    s_op(lambda e, bi=bi, c0=c0: e.copy(dst[:, c0:c0 + 128],
                                                        psT[bi][:, 0:128]),
                         [f"psT{bi}"], [wtok], f"{tag}.E{j}")

        es_ab = contextlib.ExitStack()
        with es_ab:
            wq_mix = sb(es_ab, "wq_mix", [128, 8 * H], BF16)
            bq_rows = sb(es_ab, "bq_rows", [2, H], BF16)

            es_a = contextlib.ExitStack()
            with es_a:
                wk_mix = sb(es_a, "wk_mix", [128, 8 * H], BF16)
                wv_bf = sb(es_a, "wv_bf", [128, 8 * H], BF16)
                xinA = [sb(es_a, f"xinA{i}", [128, H], F32) for i in range(2)]
                schi = sb(es_a, "schi", [128, 512], F32)
                xnA = [sb(es_a, f"xnA{i}", [128, H], BF16)
                       for i in range(2)]
                xnfmA = sb(es_a, "xnfmA", [128, H], BF16)
                kro = sb(es_a, "kro", [128, H], BF16)
                wraw = sb(es_a, "wraw", [128, H], F32)
                brow = wraw[0:1, :]
                bmixf2 = xinA[0][0:1, :]
                brow2 = xnA[0][0:1, :]
                bk_rows = sb(es_a, "bk_rows", [2, H], BF16)
                bv_rows = sb(es_a, "bv_rows", [2, H], BF16)

                blkA = es_a.enter_context(nc.Block("phaseA"))

                y_dma(identb[:], d["ident_b"], [], ["identb"], "ld.idb")
                y_dma(identf[:], d["ident_f"], [], ["identf"], "ld.idf")
                g_op(lambda e: e.memset(onesf[:], 1.0), [], ["onesf"], "ms.1f")
                if debug:
                    g_op(lambda e: e.memset(stats[:], 0.0), [], [], "ms.dbg1")
                    g_op(lambda e: e.memset(st2[:], 0.0), [], [], "ms.dbg2")
                    g_op(lambda e: e.memset(wsums[:], 0.0), [], [], "ms.dbg3")
                    g_op(lambda e: e.memset(lparts[:], 0.0), [], [], "ms.dbg4")
                    g_op(lambda e: e.memset(knorm2[:], 0.0), [], [], "ms.dbg5")
                g_op(lambda e: e.memset(ones2[:], 1.0), [], ["ones2"], "ms.12")

                def emit_wprep(src, dst, dtok, mixed, tag):
                    for dj in range(8):
                        y_dma(wraw[:], src[dj * 128:(dj + 1) * 128, :],
                              ["wraw"], ["wraw"], f"{tag}.ld{dj}", sem="yw")
                        if mixed:
                            v_op(lambda e, dj=dj: e.tensor_tensor(
                                    out=dst[:, dj * H:dj * H + 512],
                                    in0=wraw[:, 0:512], in1=wraw[:, 512:H],
                                    op=ALU.subtract),
                                 ["wraw"], [dtok], f"{tag}.s{dj}")
                            v_op(lambda e, dj=dj: e.tensor_tensor(
                                    out=dst[:, dj * H + 512:(dj + 1) * H],
                                    in0=wraw[:, 0:512], in1=wraw[:, 512:H],
                                    op=ALU.add),
                                 ["wraw"], [dtok], f"{tag}.a{dj}")
                        else:
                            v_op(lambda e, dj=dj: e.tensor_copy(
                                    dst[:, dj * H:(dj + 1) * H], wraw[:]),
                                 ["wraw"], [dtok], f"{tag}.c{dj}")

                def emit_brows(src, rows, rtok, mixed, tag):
                    y_dma(brow, src, ["wraw"], ["wraw"], f"{tag}.ld", sem="yw")
                    if mixed:
                        v_op(lambda e: e.tensor_tensor(out=bmixf2[:, 0:512],
                                                       in0=brow[:, 0:512],
                                                       in1=brow[:, 512:H],
                                                       op=ALU.subtract),
                             ["wraw"], ["xinA0"], f"{tag}.s")
                        v_op(lambda e: e.tensor_tensor(out=bmixf2[:, 512:H],
                                                       in0=brow[:, 0:512],
                                                       in1=brow[:, 512:H],
                                                       op=ALU.add),
                             ["wraw"], ["xinA0"], f"{tag}.a")
                        srcb, srct = bmixf2, "xinA0"
                    else:
                        srcb, srct = brow, "wraw"
                    v_op(lambda e, srcb=srcb: e.tensor_copy(rows[0:1, :], srcb),
                         [srct], [rtok], f"{tag}.hi")
                    v_op(lambda e, srcb=srcb: e.scalar_tensor_tensor(
                            out=brow2, in0=srcb, scalar=1.0,
                            in1=rows[0:1, :], op0=ALU.mult, op1=ALU.subtract),
                         [srct, rtok], ["xnA0"], f"{tag}.lo")
                    g_dma(rows[1:2, :], brow2, ["xnA0"], [rtok], f"{tag}.mv",
                          sem="bm")

                emit_wprep(d["wkT"], wk_mix, "wk_mix", True, "wk")
                emit_wprep(d["wvT"], wv_bf, "wv_bf", False, "wv")
                emit_wprep(d["wqT"], wq_mix, "wq_mix", True, "wq")
                emit_brows(d["bk2"][:], bk_rows, "bk_rows", True, "bk")
                emit_brows(d["bv2"][:], bv_rows, "bv_rows", False, "bv")
                emit_brows(d["bq2"][:], bq_rows, "bq_rows", True, "bq")

                def a_xload(kt):
                    xin, xtok = xinA[kt % 2], f"xinA{kt % 2}"
                    y_dma(xin[:], d["x_full"][kt * 128:(kt + 1) * 128, :],
                          [xtok], [xtok], f"A{kt}.x", sem=f"yx{kt % 2}")

                def a_lnops(kt):
                    xin, xtok = xinA[kt % 2], f"xinA{kt % 2}"
                    emit_ln(xin, xtok, xnA[kt % 2], f"xnA{kt % 2}",
                            wraw[:, 0:H], "wraw", 0, pre=f"A{kt}.")

                def a_pe(kt):
                    xn, xntok = xnA[kt % 2], f"xnA{kt % 2}"
                    y_dma(schi[:], d["sck_hi"][kt * 128:(kt + 1) * 128, :],
                          ["schi"], ["schi"], f"A{kt}.h", sem="ysh")
                    if debug and kt == 0:
                        g_dma(dbg["d_xin0"], xinA[0][:], ["xinA0"], [], "dbg.xin0")
                        g_dma(dbg["d_xn0"], xn[:], [xntok], [], "dbg.xn0")
                    emit_tr(xn, xntok, 8, xnfmA, "xnfmA", lambda j: j * 128,
                            f"A{kt}.xn", per_chunk=True, ev_alt=True)
                    for wi, (bks, wsb, wtok, btok, brows) in enumerate((
                            ((0, 1), wk_mix, "wk_mix", "bk_rows", bk_rows),
                            ((2, 3), wv_bf, "wv_bf", "bv_rows", bv_rows))):
                        for half, bk_i in enumerate(bks):
                            for dj in range(8):
                                t_op(lambda e, dj=dj, bk_i=bk_i, wsb=wsb, half=half:
                                     e.matmul(ps_[bk_i][:],
                                              xnfmA[:, dj * 128:(dj + 1) * 128],
                                              wsb[:, dj * H + half * 512:
                                                  dj * H + half * 512 + 512],
                                              start=(dj == 0), stop=False),
                                     [f"xnfmA#{dj}", wtok], [f"ps{bk_i}"],
                                     f"A{kt}.mm{wtok}{half}.{dj}")
                            t_op(lambda e, bk_i=bk_i, brows=brows, half=half:
                                 e.matmul(ps_[bk_i][:], ones2[:],
                                          brows[:, half * 512:half * 512 + 512],
                                          start=False, stop=True),
                                 ["ones2", btok], [f"ps{bk_i}"],
                                 f"A{kt}.b{wtok}{half}")
                        if wi == 0:
                            if kt + 2 < NT_K:
                                a_xload(kt + 2)
                            if kt + 1 < NT_K:
                                a_lnops(kt + 1)
                    v_op(lambda e: e.tensor_tensor(out=kro[:, 0:512], in0=ps_[0][:],
                                                   in1=schi[:], op=ALU.mult),
                         ["ps0", "schi"], ["kro"], f"A{kt}.r0")
                    v_op(lambda e: e.tensor_tensor(out=kro[:, 512:H], in0=ps_[1][:],
                                                   in1=schi[:], op=ALU.mult),
                         ["ps1", "schi"], ["kro"], f"A{kt}.r1")
                    v_op(lambda e, kt=kt: e.tensor_copy(
                            big1[:, kt * H:kt * H + 512], ps_[2][:]),
                         ["ps2"], ["big1"], f"A{kt}.v0")
                    v_op(lambda e, kt=kt: e.tensor_copy(
                            big1[:, kt * H + 512:(kt + 1) * H], ps_[3][:]),
                         ["ps3"], ["big1"], f"A{kt}.v1")
                    if debug and kt == 0:
                        g_dma(dbg["d_kro0"], kro[:], ["kro"], [], "dbg.kro0")
                        g_dma(dbg["d_ps0"], None, [], [], "")
                        sch.pending.pop()
                    emit_tr(kro, "kro", 8, big0, "big0",
                            lambda j, kt=kt: j * S + kt * 128, f"A{kt}.k",
                            ev_alt=True)

                a_xload(0)
                a_xload(1)
                a_lnops(0)
                for kt in range(NT_K):
                    a_pe(kt)

                sch.finalize_phase(blkA)

            es_b = contextlib.ExitStack()
            with es_b:
                xinB = [sb(es_b, f"xinB{i}", [128, H], F32) for i in range(2)]
                schiB = sb(es_b, "schiB", [128, 512], F32)
                xnB = sb(es_b, "xnB", [128, H], BF16)
                xnfmB = sb(es_b, "xnfmB", [128, H], BF16)
                qro = sb(es_b, "qro", [128, H], BF16)
                qfm = [sb(es_b, f"qfm{i}", [128, H], BF16) for i in range(2)]
                probs = [sb(es_b, f"probs{i}", [128, 512], BF16) for i in range(4)]
                pT = [sb(es_b, f"pT{i}", [128, 128], BF16) for i in range(4)]
                outB = [sb(es_b, "outB0", [128, H], F32)] * 2
                wchd = [sb(es_b, f"wch{i}", [128, 1024], F32) for i in range(2)]
                ssb = sb(es_b, "ssb", [128, S], F32)
                wst = [sb(es_b, "wst0", [128, 1024], BF16)] * 2
                wm1 = sb(es_b, "wm1", [128, 1024], BF16)
                wm2 = sb(es_b, "wm2", [128, 1024], BF16)

                blkB = es_b.enter_context(nc.Block("phaseB"))

                def emit_abs_chunk(src, col, width, tag, p):
                    w, wt = wchd[p], f"wch{p}"
                    g_dma(w[:, 0:width], src, [wt], [wt], f"{tag}.ld",
                          sem=f"gw{p}")
                    v_op(lambda e, w=w, col=col, width=width: e.reduce_sum(
                            wsums[:, col:col + 1], w[:, 0:width], axis=AX.X,
                            apply_absolute_value=True),
                         [wt], [f"ws{col}"], f"{tag}.abs")

                def emit_scales(first_col, n_col, tot_col, denom, out_col, tag):
                    v_op(lambda e, first_col=first_col, n_col=n_col, tot_col=tot_col:
                         e.reduce_sum(st2[:, tot_col:tot_col + 1],
                                      wsums[:, first_col:first_col + n_col],
                                      axis=AX.X),
                         [f"ws{c}" for c in range(first_col, first_col + n_col)],
                         [f"tot{tag}"], f"{tag}.tot")
                    t_op(lambda e, tot_col=tot_col: e.matmul(
                            ps_[2][:, 0:1], onesf[:, 0:128],
                            st2[:, tot_col:tot_col + 1], start=True, stop=True),
                         [f"tot{tag}", "onesf"], ["ps2"], f"{tag}.bc")
                    v_op(lambda e, denom=denom, out_col=out_col: e.tensor_scalar(
                            out=stc(out_col), in0=ps_[2][:, 0:1],
                            scalar1=1.0 / denom, scalar2=EPS,
                            op0=ALU.mult, op1=ALU.max),
                         ["ps2"], [f"mwc{tag}"], f"{tag}.mwc")
                    v_op(lambda e, out_col=out_col: e.reciprocal(
                            stc(out_col + 2), stc(out_col)),
                         [f"mwc{tag}"], [f"sw{tag}"], f"{tag}.sw")

                def emit_wq_load(src, parity, tag):
                    w, wt = wchd[parity], f"wch{parity}"
                    g_dma(w[:, 0:1024], src, [wt], [wt], f"{tag}.ld",
                          sem=f"gw{parity}")

                def emit_wquant_chunk(src, dst_dram, parity, mwtag, tag):
                    w, wt = wchd[parity], f"wch{parity}"
                    ws, wst_t = wst[parity], "wst0"
                    v_op(lambda e, w=w, mwtag=mwtag: e.tensor_scalar(
                            out=wm1[:], in0=w[:, 0:1024],
                            scalar1=stc(C_SW1 if mwtag == "w1" else C_SW2),
                            scalar2=0.5, op0=ALU.mult, op1=ALU.is_ge),
                         [wt, f"sw{mwtag}"], ["wm1"], f"{tag}.m1")
                    v_op(lambda e, w=w, mwtag=mwtag: e.tensor_scalar(
                            out=wm2[:], in0=w[:, 0:1024],
                            scalar1=stc(C_SW1 if mwtag == "w1" else C_SW2),
                            scalar2=-0.5, op0=ALU.mult, op1=ALU.is_le),
                         [wt, f"sw{mwtag}"], ["wm2"], f"{tag}.m2")
                    v_op(lambda e, ws=ws: e.tensor_tensor(
                            out=ws[:], in0=wm2[:], in1=wm1[:],
                            op=ALU.subtract),
                         ["wm1", "wm2"], [wst_t], f"{tag}.nq")
                    g_dma(dst_dram, ws[:], [wst_t],
                          [wst_t], f"{tag}.sp", sem=f"gs{parity}")

                abs_jobs = []
                for a in range(8):
                    for q4 in range(4):
                        abs_jobs.append((w1v[:, a, q4 * 1024:(q4 + 1) * 1024],
                                         a * 4 + q4, 1024, f"w1a{a}.{q4}"))
                for a in range(32):
                    abs_jobs.append((w2v[:, a, :], 32 + a, 1024, f"w2a{a}"))

                quant_jobs = []
                for a in range(8):
                    for q4 in range(4):
                        quant_jobs.append((w1v[:, a, q4 * 1024:(q4 + 1) * 1024],
                                           w1q_d[:, (a * 4 + q4) * 1024:
                                                 (a * 4 + q4 + 1) * 1024],
                                           "w1", f"qw1.{a}.{q4}"))
                for a in range(32):
                    quant_jobs.append((w2v[:, a, :],
                                       w2q_d[:, a * 1024:(a + 1) * 1024],
                                       "w2", f"qw2.{a}"))

                abs_i = [0]
                quant_i = [0]
                quant_ld_i = [0]

                def drip_weight_work(n_abs, n_quant):
                    for _ in range(n_abs):
                        if abs_i[0] < len(abs_jobs):
                            srcv, col, width, tag = abs_jobs[abs_i[0]]
                            emit_abs_chunk(srcv, col, width, tag, abs_i[0] % 2)
                            abs_i[0] += 1
                            if abs_i[0] == 32:
                                emit_scales(0, 32, 1, float(H) * FF, C_MW1, "w1")
                            if abs_i[0] == 64:
                                emit_scales(32, 32, 2, float(H) * FF, C_MW2, "w2")
                    for _ in range(n_quant):
                        if abs_i[0] >= 64 and quant_i[0] < len(quant_jobs):
                            i = quant_i[0]
                            while quant_ld_i[0] <= min(i + 1,
                                                       len(quant_jobs) - 1):
                                srcv, dst, mwtag, tag = \
                                    quant_jobs[quant_ld_i[0]]
                                emit_wq_load(srcv, quant_ld_i[0] % 2, tag)
                                quant_ld_i[0] += 1
                            srcv, dst, mwtag, tag = quant_jobs[i]
                            emit_wquant_chunk(srcv, dst, i % 2, mwtag, tag)
                            quant_i[0] += 1

                def stage1_ln(qt):
                    xin, xtok = xinB[qt % 2], f"xinB{qt % 2}"
                    y_dma(xin[:], d["x_own"][qt * 128:(qt + 1) * 128, :],
                          [xtok], [xtok], f"B{qt}.x", sem=f"yb{qt % 2}")
                    y_dma(schiB[:], d["scq_hi"][qt * 128:(qt + 1) * 128, :],
                          ["schiB"], ["schiB"], f"B{qt}.h", sem="ybh")
                    emit_ln(xin, xtok, xnB, "xnB", qro, "qro", 0, pre=f"B{qt}.")

                def stage1_pe(qt):
                    qf, qftok = qfm[qt % 2], f"qfm{qt % 2}"
                    emit_tr(xnB, "xnB", 8, xnfmB, "xnfmB", lambda j: j * 128,
                            f"B{qt}.xn", per_chunk=True)
                    for half, bk_i in enumerate((0, 1)):
                        for dj in range(8):
                            t_op(lambda e, dj=dj, bk_i=bk_i, half=half: e.matmul(
                                    ps_[bk_i][:], xnfmB[:, dj * 128:(dj + 1) * 128],
                                    wq_mix[:, dj * H + half * 512:
                                           dj * H + half * 512 + 512],
                                    start=(dj == 0), stop=False),
                                 [f"xnfmB#{dj}", "wq_mix"], [f"ps{bk_i}"],
                                 f"B{qt}.q{half}.{dj}")
                        t_op(lambda e, bk_i=bk_i, half=half: e.matmul(
                                ps_[bk_i][:], ones2[:],
                                bq_rows[:, half * 512:half * 512 + 512],
                                start=False, stop=True),
                             ["ones2", "bq_rows"], [f"ps{bk_i}"], f"B{qt}.qb{half}")
                    v_op(lambda e: e.tensor_tensor(out=qro[:, 0:512], in0=ps_[0][:],
                                                   in1=schiB[:], op=ALU.mult),
                         ["ps0", "schiB"], ["qro"], f"B{qt}.r0")
                    v_op(lambda e: e.tensor_tensor(out=qro[:, 512:H], in0=ps_[1][:],
                                                   in1=schiB[:], op=ALU.mult),
                         ["ps1", "schiB"], ["qro"], f"B{qt}.r1")

                def stage1_pe2(qt):
                    qf, qftok = qfm[qt % 2], f"qfm{qt % 2}"
                    emit_tr(qro, "qro", 8, qf, qftok, lambda j: j * 128,
                            f"B{qt}.q", per_chunk=True)

                def emit_scores(qt, ln_cb=None):
                    qf, qftok = qfm[qt % 2], f"qfm{qt % 2}"
                    for kc in range(8):
                        bk_i = kc % 4
                        for ej in range(8):
                            t_op(lambda e, ej=ej, kc=kc, bk_i=bk_i, qf=qf: e.matmul(
                                    ps_[bk_i][:], qf[:, ej * 128:(ej + 1) * 128],
                                    big0[:, ej * S + kc * 512:ej * S + kc * 512 + 512],
                                    start=(ej == 0), stop=(ej == 7)),
                                 [f"{qftok}#{ej}", "big0"], [f"ps{bk_i}"],
                                 f"B{qt}.s{kc}.{ej}")
                        s_op(lambda e, kc=kc, bk_i=bk_i: e.copy(
                                ssb[:, kc * 512:(kc + 1) * 512], ps_[bk_i][:]),
                             [f"ps{bk_i}"], [f"ssb{kc}"], f"B{qt}.sev{kc}")
                        v_op(lambda e, kc=kc: e.reduce_max(
                                st2[:, 8 + kc:9 + kc],
                                ssb[:, kc * 512:(kc + 1) * 512], axis=AX.X),
                             [f"ssb{kc}"], [f"smx{kc}"], f"B{qt}.smx{kc}")
                        if kc == 2 and ln_cb is not None:
                            ln_cb()
                        drip_weight_work(1, 0)

                def stage2(qt):
                    xin, xtok = xinB[qt % 2], f"xinB{qt % 2}"
                    v_op(lambda e: e.reduce_max(stc(10), st2[:, 8:16],
                                                axis=AX.X),
                         [f"smx{kc}" for kc in range(8)], ["qM"],
                         f"B{qt}.rmax")
                    s_op(lambda e: e.activation(out=stc(11), in_=stc(10),
                                                func=AF.Identity,
                                                scale=-INV_SCALE),
                         ["qM"], ["ebias"], f"B{qt}.eb")
                    for kc in range(8):
                        bk_i = kc % 4
                        s_op(lambda e, kc=kc, bk_i=bk_i: e.activation(
                                out=probs[bk_i][:],
                                in_=ssb[:, kc * 512:(kc + 1) * 512], func=AF.Exp,
                                scale=INV_SCALE, bias=stc(11),
                                accum_out=lparts[:, kc:kc + 1]),
                             [f"ssb{kc}", "ebias"], [f"probs{bk_i}", "lparts"],
                             f"B{qt}.e{kc}")
                        for sub in range(4):
                            t_op(lambda e, sub=sub, bk_i=bk_i: e.transpose(
                                    psT[sub][:, 0:128],
                                    probs[bk_i][:, sub * 128:(sub + 1) * 128],
                                    identb[:]),
                                 [f"probs{bk_i}", "identb"], [f"psT{sub}"],
                                 f"B{qt}.pT{kc}.{sub}")
                            s_op(lambda e, sub=sub: e.copy(
                                    pT[sub][:], psT[sub][:, 0:128]),
                                 [f"psT{sub}"], [f"pT{sub}"],
                                 f"B{qt}.pE{kc}.{sub}")
                        for sub in range(4):
                            kt64 = kc * 4 + sub
                            for half, bk2 in enumerate((2, 3)):
                                t_op(lambda e, sub=sub, kt64=kt64, half=half,
                                     bk2=bk2, first=(kt64 == 0):
                                     e.matmul(ps_[bk2][:], pT[sub][:],
                                              big1[:, kt64 * H + half * 512:
                                                   kt64 * H + half * 512 + 512],
                                              start=first, stop=(kt64 == 31)),
                                     [f"pT{sub}", "big1"], [f"ps{bk2}"],
                                     f"B{qt}.pv{kc}.{sub}.{half}")
                        drip_weight_work(0, 1)
                    v_op(lambda e: e.reduce_sum(stc(12), lparts[:, 0:8], axis=AX.X),
                         ["lparts"], ["lsum"], f"B{qt}.l")
                    v_op(lambda e: e.reciprocal(stc(13), stc(12)),
                         ["lsum"], ["rl"], f"B{qt}.rl")
                    ob, obt = outB[0], "outB0"
                    v_op(lambda e, ob=ob, xin=xin: e.scalar_tensor_tensor(
                            out=ob[:, 0:512], in0=ps_[2][:], scalar=stc(13),
                            in1=xin[:, 0:512], op0=ALU.mult, op1=ALU.add),
                         ["ps2", "rl", xtok], [obt], f"B{qt}.o0")
                    v_op(lambda e, ob=ob, xin=xin: e.scalar_tensor_tensor(
                            out=ob[:, 512:H], in0=ps_[3][:], scalar=stc(13),
                            in1=xin[:, 512:H], op0=ALU.mult, op1=ALU.add),
                         ["ps3", "rl", xtok], [obt], f"B{qt}.o1")
                    g_dma(part_d[qt * 128:(qt + 1) * 128, :], ob[:], [obt], [obt],
                          f"B{qt}.st", sem=f"go{qt % 2}")

                stage1_ln(0)
                stage1_pe(0)
                stage1_pe2(0)
                for qt in range(NT_Q):
                    emit_scores(qt, ln_cb=(
                        (lambda q=qt: stage1_ln(q + 1))
                        if qt + 1 < NT_Q else None))
                    if qt + 1 < NT_Q:
                        stage1_pe(qt + 1)
                    stage2(qt)
                    if qt + 1 < NT_Q:
                        stage1_pe2(qt + 1)
                drip_weight_work(64, 64)

                if debug:
                    g_dma(dbg["d_ssb"], ssb[:], ["ssb"], [], "dbgBs")
                    g_dma(dbg["d_qro"], qro[:], ["qro"], [], "dbgBq")
                    g_dma(dbg["d_lp2"], lparts[:], ["lparts"], [], "dbgBl")
                    g_dma(dbg["d_stats"], stats[:], ["ebias"], [], "dbgB0")
                    g_dma(dbg["d_lparts"], lparts[:], ["lparts"], [], "dbgB1")
                    g_dma(dbg["d_w1q"], w1q_d[:], [], [], "dbgB2")
                sch.finalize_phase(blkB)

        es_c = contextlib.ExitStack()
        with es_c:
            gamma = sb(es_c, "gamma", [128, H], F32)
            beta = sb(es_c, "beta", [128, H], F32)
            xinC = [sb(es_c, f"xinC{i}", [128, H], F32) for i in range(3)]
            xpart = xinC
            xnC = sb(es_c, "xnC", [128, H], F32)
            ln2b = sb(es_c, "ln2b", [128, H], F32)
            zbuf = sb(es_c, "zbuf", [128, 1024], F32)
            cibuf = sb(es_c, "cibuf", [128, 1024], I32)
            m1b = sb(es_c, "m1b", [128, 1024], BF16)
            nxq = sb(es_c, "nxq", [128, H], BF16)
            xqfm = sb(es_c, "xqfm", [128, H], BF16)
            hbuf = sb(es_c, "hbuf", [128, FF], F32)
            nhqd = [sb(es_c, f"nhq{i}", [128, 1024], BF16) for i in range(2)]
            hqfm = sb(es_c, "hqfm", [128, 2048], BF16)
            b1rows = sb(es_c, "b1rows", [4, FF], BF16)
            b2rows = sb(es_c, "b2rows", [4, H], BF16)
            blhs1 = sb(es_c, "blhs1", [4, 128], BF16)
            blhs2 = sb(es_c, "blhs2", [4, 128], BF16)
            stg4 = zbuf[0:4, :]
            stg4h = m1b[0:4, :]
            stg4l = nhqd[0][0:4, :]

            blkC = es_c.enter_context(nc.Block("phaseC"))

            for j in range(4):
                y_dma(big0[:, j * 8192:(j + 1) * 8192],
                      w1q_d[:, j * 8192:(j + 1) * 8192],
                      ["big0"], ["big0"], f"C.w1q{j}", sem="yr")
            for j in range(4):
                y_dma(big1[:, j * 8192:(j + 1) * 8192],
                      w2q_d[:, j * 8192:(j + 1) * 8192],
                      ["big1"], ["big1"], f"C.w2q{j}", sem="yr")
            y_dma(gamma[:], d["gamma_bc"], [], ["gamma"], "C.g", sem="yr")
            y_dma(beta[:], d["beta_bc"], [], ["beta"], "C.b", sem="yr")

            def emit_bsplit(src, nq, rows, rtok, tag):
                w = nq
                y_dma(stg4[:, 0:w], src, ["zbuf"], ["zbuf"], f"{tag}.ld", sem="yr")
                v_op(lambda e, w=w: e.tensor_copy(stg4h[:, 0:w], stg4[:, 0:w]),
                     ["zbuf"], ["m1b"], f"{tag}.hi")
                v_op(lambda e, w=w: e.scalar_tensor_tensor(
                        out=stg4l[:, 0:w], in0=stg4[:, 0:w], scalar=1.0,
                        in1=stg4h[:, 0:w], op0=ALU.mult, op1=ALU.subtract),
                     ["zbuf", "m1b"], ["nhq0"], f"{tag}.lo")
                full = 4 * w
                for r, srct, srcb in ((0, "m1b", stg4h), (1, "m1b", stg4h),
                                      (2, "nhq0", stg4l), (3, "nhq0", stg4l)):
                    for p in range(4):
                        g_dma(rows[r:r + 1, p * w:(p + 1) * w],
                              srcb[p:p + 1, 0:w],
                              [srct], [rtok], f"{tag}.mv{r}.{p}")

            emit_bsplit(d["b1r"], FF // 4, b1rows, "b1rows", "b1")
            emit_bsplit(d["b2r"], H // 4, b2rows, "b2rows", "b2")

            CQ = 21

            st2b = sb(es_c, "st2b", [128, 8], BF16)

            def emit_aquant(src, stoks, sx_tok, sx_col, width, off, nxq_out,
                            nxq_tok, tag):
                sl = slice(off, off + width)
                v_op(lambda e: e.scalar_tensor_tensor(
                        out=zbuf[:, 0:width], in0=src[:, sl],
                        scalar=stc(sx_col), in1=src[:, sl],
                        op0=ALU.mult, op1=ALU.bypass),
                     list(stoks) + [sx_tok], ["zbuf"], f"{tag}.z")
                v_op(lambda e: e.tensor_copy(cibuf[:, 0:width], zbuf[:, 0:width]),
                     ["zbuf"], ["cibuf"], f"{tag}.ci")
                v_op(lambda e: e.scalar_tensor_tensor(
                        out=zbuf[:, 0:width], in0=zbuf[:, 0:width], scalar=1.0,
                        in1=cibuf[:, 0:width], op0=ALU.mult, op1=ALU.subtract),
                     ["zbuf", "cibuf"], ["zbuf"], f"{tag}.d")
                v_op(lambda e: e.tensor_scalar(
                        out=m1b[:, 0:width], in0=zbuf[:, 0:width],
                        scalar1=0.5, scalar2=None, op0=ALU.is_ge),
                     ["zbuf"], ["m1b"], f"{tag}.m1")
                v_op(lambda e: e.scalar_tensor_tensor(
                        out=m1b[:, 0:width], in0=zbuf[:, 0:width], scalar=-0.5,
                        in1=m1b[:, 0:width], op0=ALU.is_le, op1=ALU.subtract),
                     ["zbuf", "m1b"], ["m1b"], f"{tag}.mc")
                v_op(lambda e: e.scalar_tensor_tensor(
                        out=nxq_out, in0=cibuf[:, 0:width], scalar=-1.0,
                        in1=m1b[:, 0:width], op0=ALU.mult, op1=ALU.add),
                     ["cibuf", "m1b"], [nxq_tok], f"{tag}.nx")

            def emit_split(src_col, srctok, c0, pre):
                s_op(lambda e: e.copy(st2b[:, c0:c0 + 1], stc(src_col)),
                     [srctok], [f"{pre}hb"], f"{pre}.hb")
                s_op(lambda e: e.copy(st2b[:, c0 + 2:c0 + 3],
                                      st2b[:, c0:c0 + 1]),
                     [f"{pre}hb"], [f"{pre}hb2"], f"{pre}.hb2")
                s_op(lambda e: e.copy(st2[:, 5:6], st2b[:, c0:c0 + 1]),
                     [f"{pre}hb"], [f"{pre}hf"], f"{pre}.hf")
                s_op(lambda e: e.activation(out=st2[:, 6:7], in_=st2[:, 5:6],
                                            func=AF.Identity, scale=-1.0,
                                            bias=stc(src_col)),
                     [f"{pre}hf", srctok], [f"{pre}lf"], f"{pre}.lf")
                s_op(lambda e: e.copy(st2b[:, c0 + 1:c0 + 2], st2[:, 6:7]),
                     [f"{pre}lf"], [f"{pre}lb"], f"{pre}.lb")
                s_op(lambda e: e.copy(st2b[:, c0 + 3:c0 + 4], st2[:, 6:7]),
                     [f"{pre}lf"], [f"{pre}lb2"], f"{pre}.lb2")

            def emit_blhs(c0, pre, rows, rtok, tag):
                t_op(lambda e: e.transpose(psT[0][0:4, 0:128],
                                           st2b[:, c0:c0 + 4], identb[:]),
                     [f"{pre}hb", f"{pre}hb2", f"{pre}lb", f"{pre}lb2",
                      "identb"], ["psT0"], f"{tag}.T")
                s_op(lambda e: e.copy(rows[0:4, :], psT[0][0:4, 0:128]),
                     ["psT0"], [rtok], f"{tag}.E")

            def c_front_vs(tt):
                xin, xtok = xinC[tt % 3], f"xinC{tt % 3}"
                y_dma(xin[:], d["x_own"][tt * 128:(tt + 1) * 128, :],
                      [xtok], [xtok], f"C{tt}.x", sem=f"yc{tt % 3}")
                emit_ln(xin, xtok, xnC, "xnC", zbuf[:, 0:H], "zbuf", 0,
                        xnsum_col=7, pre=f"C{tt}.")
                v_op(lambda e: e.tensor_scalar(out=stc(CQ), in0=stc(7),
                                               scalar1=-1.0 / H, scalar2=None,
                                               op0=ALU.mult),
                     ["xnsum7"], ["nmu2"], f"C{tt}.mu2")
                s_op(lambda e: e.activation(out=zbuf[:, 0:H], in_=xnC[:],
                                            func=AF.Square, bias=stc(CQ),
                                            accum_out=stc(CQ + 1)),
                     ["xnC", "nmu2"], ["zbuf", "v2s"], f"C{tt}.v2")
                v_op(lambda e: e.tensor_scalar(out=stc(CQ + 2), in0=stc(CQ + 1),
                                               scalar1=1.0 / H, scalar2=EPS,
                                               op0=ALU.mult, op1=ALU.add),
                     ["v2s"], ["v2p"], f"C{tt}.vp2")
                v_op(lambda e: e.reciprocal(stc(CQ + 3), stc(CQ + 2)),
                     ["v2p"], ["ri2"], f"C{tt}.ri2")
                s_op(lambda e: e.activation(out=stc(CQ + 4), in_=stc(CQ + 3),
                                            func=AF.Sqrt),
                     ["ri2"], ["rs2"], f"C{tt}.rs2")
                v_op(lambda e: e.tensor_tensor(out=stc(CQ + 5), in0=stc(CQ),
                                               in1=stc(CQ + 4), op=ALU.mult),
                     ["nmu2", "rs2"], ["nm2r"], f"C{tt}.nm2r")
                s_op(lambda e: e.activation(out=ln2b[:], in_=xnC[:], func=AF.Identity,
                                            scale=stc(CQ + 4), bias=stc(CQ + 5)),
                     ["xnC", "rs2", "nm2r"], ["ln2b"], f"C{tt}.l2n")
                v_op(lambda e: e.tensor_tensor(out=ln2b[:], in0=ln2b[:],
                                               in1=gamma[:], op=ALU.mult),
                     ["ln2b", "gamma"], ["ln2b"], f"C{tt}.gam")
                v_op(lambda e: e.tensor_tensor(out=ln2b[:], in0=ln2b[:],
                                               in1=beta[:], op=ALU.add),
                     ["ln2b", "beta"], ["ln2b"], f"C{tt}.bet")
                v_op(lambda e: e.reduce_max(stc(CQ + 6), ln2b[:], axis=AX.X,
                                            apply_absolute_value=True),
                     ["ln2b"], ["mx1"], f"C{tt}.mx1")
                v_op(lambda e: e.tensor_scalar(out=stc(CQ + 7), in0=stc(CQ + 6),
                                               scalar1=EPS, scalar2=None,
                                               op0=ALU.max),
                     ["mx1"], ["den1"], f"C{tt}.den1")
                v_op(lambda e: e.reciprocal(stc(CQ + 8), stc(CQ + 7)),
                     ["den1"], ["rd1"], f"C{tt}.rd1")
                v_op(lambda e: e.tensor_scalar(out=stc(CQ + 9), in0=stc(CQ + 8),
                                               scalar1=127.0, scalar2=None,
                                               op0=ALU.mult),
                     ["rd1"], ["sx1"], f"C{tt}.sx1")
                rqc = 40 + (tt % 2)
                v_op(lambda e, rqc=rqc: e.tensor_tensor(
                        out=stc(rqc), in0=stc(CQ + 7),
                        in1=stc(C_MW1), op=ALU.mult),
                     ["den1"], ["rqt1"], f"C{tt}.rt1")
                v_op(lambda e, rqc=rqc: e.tensor_scalar(
                        out=stc(rqc), in0=stc(rqc),
                        scalar1=1.0 / 127.0, scalar2=None,
                        op0=ALU.mult),
                     ["rqt1"], [f"rq1{tt % 2}"], f"C{tt}.rq1")
                v_op(lambda e, rqc=rqc: e.reciprocal(stc(CQ + 11), stc(rqc)),
                     [f"rq1{tt % 2}"], ["sr1"], f"C{tt}.sr1")

            def c_front_vs2(tt):
                emit_split(CQ + 11, "sr1", 0, "s1")
                emit_aquant(ln2b, ["ln2b"], "sx1", CQ + 9, H, 0, nxq[:], "nxq",
                            f"C{tt}.xq")

            def c_front_pe(tt):
                emit_tr(nxq, "nxq", 8, xqfm, "xqfm", lambda j: j * 128,
                        f"C{tt}.xf", per_chunk=True)

            def c_mid(tt, cb1=None, cb2=None):
                for hc in range(8):
                    bk_i = hc % 4
                    if hc == 0 and cb1 is not None:
                        cb1()
                    if hc == 4 and cb2 is not None:
                        cb2()
                    for dj in range(8):
                        t_op(lambda e, dj=dj, hc=hc, bk_i=bk_i: e.matmul(
                                ps_[bk_i][:], xqfm[:, dj * 128:(dj + 1) * 128],
                                big0[:, dj * FF + hc * 512:dj * FF + hc * 512 + 512],
                                start=(dj == 0), stop=False),
                             [f"xqfm#{dj}", "big0"], [f"ps{bk_i}"],
                             f"C{tt}.m1.{hc}.{dj}")
                    t_op(lambda e, hc=hc, bk_i=bk_i: e.matmul(
                            ps_[bk_i][:], blhs1[:],
                            b1rows[:, hc * 512:hc * 512 + 512],
                            start=False, stop=True),
                         ["blhs1", "b1rows"], [f"ps{bk_i}"], f"C{tt}.bb1.{hc}")
                    s_op(lambda e, hc=hc, bk_i=bk_i, tt=tt: e.activation(
                            out=hbuf[:, hc * 512:(hc + 1) * 512], in_=ps_[bk_i][:],
                            func=AF.Gelu, scale=stc(40 + (tt % 2))),
                         [f"ps{bk_i}", f"rq1{tt % 2}"], [f"hbuf#{hc}"],
                         f"C{tt}.g{hc}")

            def c_tail(tt):
                for hc in range(8):
                    v_op(lambda e, hc=hc: e.reduce_max(
                            st2[:, 8 + hc:9 + hc], hbuf[:, hc * 512:(hc + 1) * 512],
                            axis=AX.X, apply_absolute_value=True),
                         [f"hbuf#{hc}"], [f"hm{hc}"], f"C{tt}.hm{hc}")
                v_op(lambda e: e.reduce_max(stc(CQ + 15), st2[:, 8:16], axis=AX.X),
                     [f"hm{h}" for h in range(8)], ["mx2"], f"C{tt}.mx2")
                v_op(lambda e: e.tensor_scalar(out=stc(CQ + 16), in0=stc(CQ + 15),
                                               scalar1=EPS, scalar2=None,
                                               op0=ALU.max),
                     ["mx2"], ["den2"], f"C{tt}.den2")
                v_op(lambda e: e.reciprocal(stc(CQ + 17), stc(CQ + 16)),
                     ["den2"], ["rd2"], f"C{tt}.rd2")
                v_op(lambda e: e.tensor_scalar(out=stc(CQ + 12), in0=stc(CQ + 17),
                                               scalar1=127.0, scalar2=None,
                                               op0=ALU.mult),
                     ["rd2"], ["sx2"], f"C{tt}.sx2")
                v_op(lambda e: e.tensor_tensor(out=stc(CQ + 13), in0=stc(CQ + 16),
                                               in1=stc(C_MW2), op=ALU.mult),
                     ["den2"], ["rqt2"], f"C{tt}.rt2")
                v_op(lambda e: e.tensor_scalar(out=stc(CQ + 13), in0=stc(CQ + 13),
                                               scalar1=1.0 / 127.0, scalar2=None,
                                               op0=ALU.mult),
                     ["rqt2"], ["rq2"], f"C{tt}.rq2")
                v_op(lambda e: e.reciprocal(stc(CQ + 14), stc(CQ + 13)),
                     ["rq2"], ["sr2"], f"C{tt}.sr2")
                emit_split(CQ + 14, "sr2", 4, "s2")

                def emit_aq_q(q):
                    emit_aquant(hbuf, [f"hbuf#{2 * q}", f"hbuf#{2 * q + 1}"],
                                "sx2", CQ + 12, 1024, q * 1024,
                                nhqd[q % 2][:], f"nhq{q % 2}",
                                f"C{tt}.hq{q}")

                def emit_hqT(q):
                    emit_tr(nhqd[q % 2], f"nhq{q % 2}", 8, hqfm, "hqfm",
                            lambda j, q=q: (q % 2) * 1024 + j * 128,
                            f"C{tt}.hf{q}", per_chunk=True)

                def emit_mm2(hf, rng=range(16)):
                    for hj16 in rng:
                        hj = hf * 16 + hj16
                        for half, bk2 in enumerate((2, 3)):
                            t_op(lambda e, hj16=hj16, hj=hj, half=half, bk2=bk2:
                                 e.matmul(ps_[bk2][:],
                                          hqfm[:, hj16 * 128:(hj16 + 1) * 128],
                                          big1[:, hj * H + half * 512:
                                               hj * H + half * 512 + 512],
                                          start=(hj == 0), stop=False),
                                 [f"hqfm#{hj16}", "big1"], [f"ps{bk2}"],
                                 f"C{tt}.m2.{hj}.{half}")

                emit_aq_q(0)
                emit_hqT(0)
                emit_aq_q(1)
                emit_mm2(0, range(8))
                emit_hqT(1)
                emit_aq_q(2)
                emit_mm2(0, range(8, 16))
                emit_hqT(2)
                emit_aq_q(3)
                emit_mm2(1, range(8))
                emit_hqT(3)
                emit_blhs(4, "s2", blhs2, "blhs2", f"C{tt}.b2")
                emit_mm2(1, range(8, 16))
                for half, bk2 in enumerate((2, 3)):
                    t_op(lambda e, half=half, bk2=bk2: e.matmul(
                            ps_[bk2][:], blhs2[:],
                            b2rows[:, half * 512:half * 512 + 512],
                            start=False, stop=True),
                         ["blhs2", "b2rows"], [f"ps{bk2}"], f"C{tt}.bb2.{half}")
                if tt + 1 < NT_Q:
                    emit_blhs(0, "s1", blhs1, "blhs1", f"C{tt + 1}.b1")

            def c_pload(tt):
                y_dma(xpart[tt % 3][:], part_d[tt * 128:(tt + 1) * 128, :],
                      [f"xinC{tt % 3}"], [f"xinC{tt % 3}"], f"C{tt}.p",
                      sem=f"yc{tt % 3}")

            def c_out(tt):
                xp, xptok = xpart[tt % 3], f"xinC{tt % 3}"
                v_op(lambda e, xp=xp: e.scalar_tensor_tensor(
                        out=xp[:, 0:512], in0=ps_[2][:], scalar=stc(CQ + 13),
                        in1=xp[:, 0:512], op0=ALU.mult, op1=ALU.add),
                     ["ps2", "rq2", xptok], [xptok], f"C{tt}.o0")
                v_op(lambda e, xp=xp: e.scalar_tensor_tensor(
                        out=xp[:, 512:H], in0=ps_[3][:], scalar=stc(CQ + 13),
                        in1=xp[:, 512:H], op0=ALU.mult, op1=ALU.add),
                     ["ps3", "rq2", xptok], [xptok], f"C{tt}.o1")
                g_dma(out_d[tt * 128:(tt + 1) * 128, :], xp[:], [xptok],
                      [xptok], f"C{tt}.st", sem=f"gc{tt % 2}")

            c_front_vs(0)
            c_front_vs2(0)
            emit_blhs(0, "s1", blhs1, "blhs1", "C0.b1")
            c_front_pe(0)
            for tt in range(NT_Q):
                if tt > 0:
                    c_out(tt - 1)
                c_mid(tt,
                      cb1=((lambda t=tt: c_front_vs(t + 1))
                           if tt + 1 < NT_Q else None),
                      cb2=((lambda t=tt: c_front_vs2(t + 1))
                           if tt + 1 < NT_Q else None))
                if tt + 1 < NT_Q:
                    c_front_pe(tt + 1)
                c_pload(tt)
                c_tail(tt)
                if debug and tt == 0:
                    g_dma(dbg["d_xnC"], xnC[:], ["xnC"], [], "dbg.xnC")
                    g_dma(dbg["d_ln2b"], ln2b[:], ["ln2b"], [], "dbg.ln2b")
                    g_dma(dbg["d_z"], zbuf[:], ["zbuf"], [], "dbg.z")
                    g_dma(dbg["d_nxq"], nxq[:], ["nxq"], [], "dbg.nxq")
                    g_dma(dbg["d_cstats"], stats[:], ["sr1"], [], "dbg.cst")
            c_out(NT_Q - 1)
            sch.finalize_phase(blkC, final=True)

    return nc


_NC_CACHE = []


def kernel(**inputs):
    X = np.ascontiguousarray(np.asarray(inputs["L0_attn_hidden_states"],
                                        dtype=np.float32))
    Wq = np.asarray(inputs["L0_attn_Wq"], dtype=np.float32)
    Wk = np.asarray(inputs["L0_attn_Wk"], dtype=np.float32)
    Wv = np.asarray(inputs["L0_attn_Wv"], dtype=np.float32)
    bq = np.asarray(inputs["L0_attn_bq"], dtype=np.float32)
    bk = np.asarray(inputs["L0_attn_bk"], dtype=np.float32)
    bv = np.asarray(inputs["L0_attn_bv"], dtype=np.float32)
    w1 = np.asarray(inputs["L0_ffn_w1"], dtype=np.float32)
    b1 = np.asarray(inputs["L0_ffn_b1"], dtype=np.float32)
    w2 = np.asarray(inputs["L0_ffn_w2"], dtype=np.float32)
    b2 = np.asarray(inputs["L0_ffn_b2"], dtype=np.float32)
    gamma = np.asarray(inputs["L0_ffn_gamma"], dtype=np.float32)
    beta = np.asarray(inputs["L0_ffn_beta"], dtype=np.float32)

    perm = np.concatenate([np.arange(0, H, 2), np.arange(1, H, 2)])
    wqT = np.ascontiguousarray(Wq.T[:, perm])
    wkT = np.ascontiguousarray(Wk.T[:, perm])
    wvT = np.ascontiguousarray(Wv.T)
    w1T = np.ascontiguousarray(w1.T)
    w2T = np.ascontiguousarray(w2.T)

    pos = np.arange(S, dtype=np.float32)
    inv = (10000.0 ** (np.arange(0, 512, 2, dtype=np.float64) / 512.0)).astype(np.float32)
    rad = (pos[:, None] / inv).astype(np.float32)
    sc_hi = np.stack([np.sin(rad.astype(np.float64)),
                      np.cos(rad.astype(np.float64))],
                     axis=-1).reshape(S, 512).astype(np.float32)

    identb = np.eye(128, dtype=BFDT)
    identf = np.eye(128, dtype=np.float32)

    in_maps = []
    for c in range(8):
        b, hh = c // 2, c % 2
        m = {
            "x_full": np.ascontiguousarray(X[b]),
            "x_own": np.ascontiguousarray(X[b, hh * HALF:(hh + 1) * HALF]),
            "wqT": wqT, "wkT": wkT, "wvT": wvT,
            "bq2": bq[perm][None, :], "bk2": bk[perm][None, :], "bv2": bv[None, :],
            "w1T": w1T, "w2T": w2T,
            "b1r": np.ascontiguousarray(b1.reshape(4, FF // 4)),
            "b2r": np.ascontiguousarray(b2.reshape(4, H // 4)),
            "gamma_bc": np.ascontiguousarray(np.broadcast_to(gamma, (128, H))),
            "beta_bc": np.ascontiguousarray(np.broadcast_to(beta, (128, H))),
            "sck_hi": sc_hi,
            "scq_hi": np.ascontiguousarray(sc_hi[hh * HALF:(hh + 1) * HALF]),
            "ident_b": identb, "ident_f": identf,
        }
        in_maps.append(m)

    if not _NC_CACHE:
        _NC_CACHE.append(build_nc())
    nc = _NC_CACHE[0]
    res = run_bass_kernel_spmd(nc, in_maps, core_ids=list(range(8)))
    out = np.empty((B, S, H), dtype=np.float32)
    for c in range(8):
        b, hh = c // 2, c % 2
        out[b, hh * HALF:(hh + 1) * HALF] = res.results[c]["out"]
    return out


# version 25
# speedup vs baseline: 1.0613x; 1.0001x over previous
import contextlib
import math

import numpy as np
import ml_dtypes

import concourse.bass as bass
import concourse.mybir as mybir
from concourse.bass_utils import run_bass_kernel_spmd

F32 = mybir.dt.float32
BF16 = mybir.dt.bfloat16
I32 = mybir.dt.int32
AF = mybir.ActivationFunctionType
ALU = mybir.AluOpType
AX = mybir.AxisListType
BFDT = ml_dtypes.bfloat16

B, S, H = 4, 4096, 1024
HALF = S // 2
FF = 4096
NT_K = S // 128
NT_Q = HALF // 128
INV_SCALE = 1.0 / math.sqrt(512.0)
EPS = 1e-5


class Sched:

    ENGINES = ("t", "v", "s", "g")
    STREAM_OF_Q = {"y": "y", "gd": "g"}

    def __init__(self, nc, es):
        self.nc = nc
        self.es = es
        self.sems = {}
        self.sem_policy = {}
        self.cnt = {}
        for e in self.ENGINES:
            self._mk(e, "ord")
        self.pending = []
        self.last_writer = {}
        self.readers = {}
        self.hw = {}
        self.bar_vals = None
        self.bar_seen = set()

    def _mk(self, name, policy):
        if name not in self.sems:
            self.sems[name] = self.es.enter_context(self.nc.semaphore("sem_" + name))
            self.sem_policy[name] = policy
            self.cnt[name] = 0

    def dma_sem(self, name, policy="ord"):
        self._mk(name, policy)

    def op(self, eng, emit, reads=(), writes=(), tag="", sem=None, stream=None,
           small=True):
        if eng in ("y", "gd"):
            assert sem is not None, tag
            self._mk(sem, self.sem_policy.get(sem, "ord"))
            stream = self.STREAM_OF_Q[eng]
            step = 16
        else:
            sem = eng
            stream = eng
            step = 1
        i = len(self.pending)
        deps = set()
        for r in reads:
            w = self.last_writer.get(r)
            if w is not None:
                deps.add(w)
            self.readers.setdefault(r, []).append(i)
        for w in writes:
            pw = self.last_writer.get(w)
            if pw is not None:
                deps.add(pw)
            for rd in self.readers.get(w, ()):
                deps.add(rd)
            self.last_writer[w] = i
            self.readers[w] = []
        self.pending.append(dict(eng=eng, sem=sem, stream=stream, step=step,
                                 emit=emit, deps=deps, rawdeps=set(deps),
                                 tag=tag, small=small,
                                 need_inc=(eng in ("y", "gd")), inc_ord=None))

    def finalize_phase(self, block, final=False):
        ops = self.pending
        for i, o in enumerate(ops):
            kept = set()
            same_eng_deps = []
            for dd in o["deps"]:
                po = ops[dd]
                if po["eng"] == o["eng"] and po["eng"] not in ("y", "gd"):
                    same_eng_deps.append(dd)
                    continue
                if po["eng"] == o["eng"] and po["sem"] == o["sem"]:
                    continue
                kept.add(dd)
            o["deps"] = kept
            o["same_eng_deps"] = same_eng_deps
            for dd in kept:
                ops[dd]["need_inc"] = True
        last_flush = {}
        for i, o in enumerate(ops):
            o["pre_drain"] = False
            if o["eng"] in ("v", "s", "g"):
                e = o["eng"]
                for dd in o["same_eng_deps"]:
                    if ops[dd]["small"] and dd > last_flush.get(e, -1):
                        o["pre_drain"] = True
                        last_flush[e] = i - 1 if i else -1
                        break
                if o["need_inc"]:
                    last_flush[e] = i
        last_of = {}
        for i, o in enumerate(ops):
            last_of[o["eng"] if o["eng"] in self.ENGINES else o["sem"]] = i
        for i in last_of.values():
            ops[i]["need_inc"] = True
        totals_at = [None] * len(ops)
        run_tot = dict(self.cnt)
        for i, o in enumerate(ops):
            totals_at[i] = None
            if o["need_inc"]:
                run_tot[o["sem"]] = run_tot.get(o["sem"], 0) + 1
                o["inc_ord"] = run_tot[o["sem"]]
        pre_tot = {}
        run2 = dict(self.cnt)
        pref = []
        for i, o in enumerate(ops):
            pref.append(dict(run2))
            if o["need_inc"]:
                run2[o["sem"]] = run2.get(o["sem"], 0) + 1
        waits = [[] for _ in ops]
        for i, o in enumerate(ops):
            st = o["stream"]
            need = {}
            if self.bar_vals is not None and st not in self.bar_seen:
                self.bar_seen.add(st)
                for pe, v in self.bar_vals.items():
                    if v > 0:
                        need[pe] = v
            for dd in o["deps"]:
                po = ops[dd]
                assert po["inc_ord"] is not None, (po["tag"], o["tag"])
                if self.sem_policy.get(po["sem"], "ord") == "total":
                    v = pref[i].get(po["sem"], 0) * po["step"]
                    v = max(v, po["inc_ord"] * po["step"])
                else:
                    v = po["inc_ord"] * po["step"]
                if v > need.get(po["sem"], 0):
                    need[po["sem"]] = v
            for pe, v in need.items():
                k = (st, pe)
                if v > self.hw.get(k, 0):
                    self.hw[k] = v
                    waits[i].append((pe, v))
        per_stream = {"t": [], "v": [], "s": [], "g": [], "y": []}
        for i, o in enumerate(ops):
            per_stream[o["stream"]].append(i)
        sems = self.sems
        self.cnt = run_tot
        fin_vals = {nm: self.cnt.get(nm, 0) * (16 if self.sem_policy[nm] != "ord"
                    or nm not in self.ENGINES else 1) for nm in self.sems}
        fin_vals = {nm: self.cnt.get(nm, 0) * (1 if nm in self.ENGINES else 16)
                    for nm in self.sems}

        def run_stream(engh, stream):
            for i in per_stream[stream]:
                o = ops[i]
                for (pe, v) in waits[i]:
                    engh.wait_ge(sems[pe], v)
                if o.get("pre_drain"):
                    engh.drain()
                instr = o["emit"](engh)
                if o["need_inc"]:
                    if o["eng"] in ("v", "s", "g"):
                        engh.drain().then_inc(sems[o["sem"]],
                                                          o["step"])
                    else:
                        assert instr is not None, o["tag"]
                        instr.then_inc(sems[o["sem"]], o["step"])
            if final:
                for pe, v in fin_vals.items():
                    k = (stream, pe)
                    if v > self.hw.get(k, 0) and v > 0:
                        engh.wait_ge(sems[pe], v)

        @block.sync
        def _(eng):
            run_stream(eng, "y")

        @block.tensor
        def _(eng):
            run_stream(eng, "t")

        @block.vector
        def _(eng):
            run_stream(eng, "v")

        @block.scalar
        def _(eng):
            run_stream(eng, "s")

        @block.gpsimd
        def _(eng):
            run_stream(eng, "g")

        self.bar_vals = fin_vals
        self.bar_seen = set()
        self.pending = []
        self.last_writer = {}
        self.readers = {}


def build_nc(debug=False, n_ct=NT_Q):
    nc = bass.Bass("TRN2", target_bir_lowering=False, detect_race_conditions=False)

    d = {}
    for name, shape, dt in [
            ("x_full", [S, H], F32), ("x_own", [HALF, H], F32),
            ("wqT", [H, H], F32), ("wkT", [H, H], F32), ("wvT", [H, H], F32),
            ("bq2", [1, H], F32), ("bk2", [1, H], F32), ("bv2", [1, H], F32),
            ("w1T", [H, FF], F32), ("w2T", [FF, H], F32),
            ("b1r", [4, FF // 4], F32), ("b2r", [4, H // 4], F32),
            ("gamma_bc", [128, H], F32), ("beta_bc", [128, H], F32),
            ("sck_hi", [S, 512], F32),
            ("scq_hi", [HALF, 512], F32),
            ("ident_b", [128, 128], BF16), ("ident_f", [128, 128], F32)]:
        d[name] = nc.dram_tensor(name, shape, dt, kind="ExternalInput").ap()
    out_d = nc.dram_tensor("out", [HALF, H], F32, kind="ExternalOutput").ap()
    dbg = {}
    if debug:
        for nm, sh, dt_ in [("d_ssb", [128, S], F32), ("d_qro", [128, H], BF16),
                            ("d_lp2", [128, 8], F32),
                            ("d_xnC", [128, H], F32), ("d_ln2b", [128, H], F32),
                            ("d_z", [128, 1024], F32), ("d_nxq", [128, H], BF16),
                            ("d_cstats", [128, 64], F32),
                            ("d_xin0", [128, H], F32), ("d_xn0", [128, H], BF16),
                            ("d_kro0", [128, H], BF16), ("d_ps0", [128, 512], F32),
                            ("d_big0", [128, 8 * S], BF16), ("d_big1", [128, NT_K * H], BF16),
                            ("d_knorm2", [128, NT_K], F32), ("d_stats", [128, 64], F32),
                            ("d_partial", [HALF, H], F32), ("d_lparts", [128, 8], F32),
                            ("d_wq_mix", [128, 8 * H], BF16), ("d_hbuf", [128, FF], F32),
                            ("d_xnfmA", [128, H], BF16), ("d_w1q", [128, 8 * FF], BF16)]:
            dbg[nm] = nc.dram_tensor(nm, sh, dt_, kind="ExternalOutput").ap()
    part_d = nc.dram_tensor("partial", [HALF, H], F32).ap()
    w1q_d = nc.dram_tensor("w1q_sp", [128, 8 * FF], BF16).ap()
    w2q_d = nc.dram_tensor("w2q_sp", [128, 32 * H], BF16).ap()
    w1v = d["w1T"].rearrange("(a p) n -> p a n", p=128)
    w2v = d["w2T"].rearrange("(a p) n -> p a n", p=128)

    es = contextlib.ExitStack()
    with es:
        sch = Sched(nc, es)
        for nm, pol in (("ys", "total"), ("yr", "total"), ("bm", "total"),
                        ("gm", "total")):
            sch.dma_sem(nm, pol)

        def sb(stk, name, shape, dt):
            return stk.enter_context(nc.sbuf_tensor(name, shape, dt))

        big0 = sb(es, "big0", [128, 8 * S], BF16)
        big1 = sb(es, "big1", [128, NT_K * H], BF16)
        identb = sb(es, "identb", [128, 128], BF16)
        identf = sb(es, "identf", [128, 128], F32)
        onesf = sb(es, "onesf", [128, 128], F32)
        ones2 = sb(es, "ones2", [2, 128], BF16)
        stats = sb(es, "stats", [128, 64], F32)
        st2 = sb(es, "st2", [128, 16], F32)
        knorm2 = sb(es, "knorm2", [128, NT_K], F32)
        lparts = sb(es, "lparts", [128, 8], F32)
        wsums = sb(es, "wsums", [128, 68], F32)
        rowB = sb(es, "rowB", [1, 128], F32)
        ps_ = [es.enter_context(nc.psum_tensor(f"ps{i}", [128, 512], F32))
               for i in range(4)]
        psT = [es.enter_context(nc.psum_tensor(f"psT{i}", [128, 1024], BF16))
               for i in range(4)]

        C_KMAX2, C_MW1, C_MW2, C_SW1, C_SW2 = 16, 17, 18, 19, 20

        def stc(i):
            return stats[:, i:i + 1]

        y_dma = lambda dst, src, reads, writes, tag="", sem="ys": sch.op(
            "y", lambda e, a=dst, b=src: e.dma_start(out=a, in_=b), reads, writes,
            tag, sem=sem)
        g_dma = lambda dst, src, reads, writes, tag="", sem="gm": sch.op(
            "gd", lambda e, a=dst, b=src: e.dma_start(out=a, in_=b), reads, writes,
            tag, sem=sem)
        v_op = lambda fn, reads, writes, tag="": sch.op("v", fn, reads, writes, tag)
        s_op = lambda fn, reads, writes, tag="": sch.op("s", fn, reads, writes, tag)
        t_op = lambda fn, reads, writes, tag="": sch.op("t", fn, reads, writes, tag)
        g_op = lambda fn, reads, writes, tag="": sch.op("g", fn, reads, writes, tag)

        def emit_ln(xin, xtok, xn, xntok, scratch, sctok, c0, xnsum_col=None,
                    pre=""):
            tk = [f"stat{c0 + j}" for j in range(7)]
            v_op(lambda e: e.reduce_sum(stc(c0), xin[:], axis=AX.X),
                 [xtok], [tk[0]], pre + "ln.sum")
            v_op(lambda e: e.tensor_scalar(out=stc(c0 + 1), in0=stc(c0),
                                           scalar1=-1.0 / H, scalar2=None,
                                           op0=ALU.mult),
                 [tk[0]], [tk[1]], pre + "ln.neg")
            s_op(lambda e: e.activation(out=scratch[:], in_=xin[:], func=AF.Square,
                                        bias=stc(c0 + 1), scale=1.0,
                                        accum_out=stc(c0 + 2)),
                 [xtok, tk[1]], [sctok, tk[2]], pre + "ln.var")
            v_op(lambda e: e.tensor_scalar(out=stc(c0 + 3), in0=stc(c0 + 2),
                                           scalar1=1.0 / H, scalar2=EPS,
                                           op0=ALU.mult, op1=ALU.add),
                 [tk[2]], [tk[3]], pre + "ln.vps")
            v_op(lambda e: e.reciprocal(stc(c0 + 4), stc(c0 + 3)),
                 [tk[3]], [tk[4]], pre + "ln.rin")
            s_op(lambda e: e.activation(out=stc(c0 + 5), in_=stc(c0 + 4),
                                        func=AF.Sqrt),
                 [tk[4]], [tk[5]], pre + "ln.rsd")
            v_op(lambda e: e.tensor_tensor(out=stc(c0 + 6), in0=stc(c0 + 1),
                                           in1=stc(c0 + 5), op=ALU.mult),
                 [tk[1], tk[5]], [tk[6]], pre + "ln.nmr")
            acc = None if xnsum_col is None else stc(xnsum_col)
            if acc is None:
                s_op(lambda e: e.activation(out=xn[:], in_=xin[:], func=AF.Identity,
                                            scale=stc(c0 + 5), bias=stc(c0 + 6)),
                     [xtok, tk[5], tk[6]], [xntok], pre + "ln.xn")
            else:
                s_op(lambda e: e.activation(out=xn[:], in_=xin[:], func=AF.Identity,
                                            scale=stc(c0 + 5), bias=stc(c0 + 6),
                                            accum_out=acc),
                     [xtok, tk[5], tk[6]], [xntok, "xnsum7"], pre + "ln.xn")

        def emit_tr(src, stok, nchunks, dst, dtok, colf, tag, src_colf=None,
                    per_chunk=False, ev_alt=False):
            for j in range(nchunks):
                bi = j % 4
                sc = src_colf(j) if src_colf else j * 128
                t_op(lambda e, bi=bi, sc=sc: e.transpose(
                        psT[bi][:, 0:128], src[:, sc:sc + 128], identb[:]),
                     [stok, "identb"], [f"psT{bi}"], f"{tag}.T{j}")
                c0 = colf(j)
                wtok = f"{dtok}#{c0 // 128}" if per_chunk else dtok
                if ev_alt and j % 2 == 1:
                    v_op(lambda e, bi=bi, c0=c0: e.tensor_copy(
                            dst[:, c0:c0 + 128], psT[bi][:, 0:128]),
                         [f"psT{bi}"], [wtok], f"{tag}.E{j}")
                else:
                    s_op(lambda e, bi=bi, c0=c0: e.copy(dst[:, c0:c0 + 128],
                                                        psT[bi][:, 0:128]),
                         [f"psT{bi}"], [wtok], f"{tag}.E{j}")

        es_ab = contextlib.ExitStack()
        with es_ab:
            wq_mix = sb(es_ab, "wq_mix", [128, 8 * H], BF16)
            bq_rows = sb(es_ab, "bq_rows", [2, H], BF16)

            es_a = contextlib.ExitStack()
            with es_a:
                wk_mix = sb(es_a, "wk_mix", [128, 8 * H], BF16)
                wv_bf = sb(es_a, "wv_bf", [128, 8 * H], BF16)
                xinA = [sb(es_a, f"xinA{i}", [128, H], F32) for i in range(2)]
                schi = sb(es_a, "schi", [128, 512], F32)
                xnA = [sb(es_a, f"xnA{i}", [128, H], BF16)
                       for i in range(2)]
                xnfmA = sb(es_a, "xnfmA", [128, H], BF16)
                kro = sb(es_a, "kro", [128, H], BF16)
                wraw = sb(es_a, "wraw", [128, H], F32)
                brow = wraw[0:1, :]
                bmixf2 = xinA[0][0:1, :]
                brow2 = xnA[0][0:1, :]
                bk_rows = sb(es_a, "bk_rows", [2, H], BF16)
                bv_rows = sb(es_a, "bv_rows", [2, H], BF16)

                blkA = es_a.enter_context(nc.Block("phaseA"))

                y_dma(identb[:], d["ident_b"], [], ["identb"], "ld.idb")
                y_dma(identf[:], d["ident_f"], [], ["identf"], "ld.idf")
                g_op(lambda e: e.memset(onesf[:], 1.0), [], ["onesf"], "ms.1f")
                if debug:
                    g_op(lambda e: e.memset(stats[:], 0.0), [], [], "ms.dbg1")
                    g_op(lambda e: e.memset(st2[:], 0.0), [], [], "ms.dbg2")
                    g_op(lambda e: e.memset(wsums[:], 0.0), [], [], "ms.dbg3")
                    g_op(lambda e: e.memset(lparts[:], 0.0), [], [], "ms.dbg4")
                    g_op(lambda e: e.memset(knorm2[:], 0.0), [], [], "ms.dbg5")
                g_op(lambda e: e.memset(ones2[:], 1.0), [], ["ones2"], "ms.12")

                def emit_wprep(src, dst, dtok, mixed, tag):
                    for dj in range(8):
                        y_dma(wraw[:], src[dj * 128:(dj + 1) * 128, :],
                              ["wraw"], ["wraw"], f"{tag}.ld{dj}", sem="yw")
                        if mixed:
                            v_op(lambda e, dj=dj: e.tensor_tensor(
                                    out=dst[:, dj * H:dj * H + 512],
                                    in0=wraw[:, 0:512], in1=wraw[:, 512:H],
                                    op=ALU.subtract),
                                 ["wraw"], [dtok], f"{tag}.s{dj}")
                            v_op(lambda e, dj=dj: e.tensor_tensor(
                                    out=dst[:, dj * H + 512:(dj + 1) * H],
                                    in0=wraw[:, 0:512], in1=wraw[:, 512:H],
                                    op=ALU.add),
                                 ["wraw"], [dtok], f"{tag}.a{dj}")
                        else:
                            v_op(lambda e, dj=dj: e.tensor_copy(
                                    dst[:, dj * H:(dj + 1) * H], wraw[:]),
                                 ["wraw"], [dtok], f"{tag}.c{dj}")

                def emit_brows(src, rows, rtok, mixed, tag):
                    y_dma(brow, src, ["wraw"], ["wraw"], f"{tag}.ld", sem="yw")
                    if mixed:
                        v_op(lambda e: e.tensor_tensor(out=bmixf2[:, 0:512],
                                                       in0=brow[:, 0:512],
                                                       in1=brow[:, 512:H],
                                                       op=ALU.subtract),
                             ["wraw"], ["xinA0"], f"{tag}.s")
                        v_op(lambda e: e.tensor_tensor(out=bmixf2[:, 512:H],
                                                       in0=brow[:, 0:512],
                                                       in1=brow[:, 512:H],
                                                       op=ALU.add),
                             ["wraw"], ["xinA0"], f"{tag}.a")
                        srcb, srct = bmixf2, "xinA0"
                    else:
                        srcb, srct = brow, "wraw"
                    v_op(lambda e, srcb=srcb: e.tensor_copy(rows[0:1, :], srcb),
                         [srct], [rtok], f"{tag}.hi")
                    v_op(lambda e, srcb=srcb: e.scalar_tensor_tensor(
                            out=brow2, in0=srcb, scalar=1.0,
                            in1=rows[0:1, :], op0=ALU.mult, op1=ALU.subtract),
                         [srct, rtok], ["xnA0"], f"{tag}.lo")
                    g_dma(rows[1:2, :], brow2, ["xnA0"], [rtok], f"{tag}.mv",
                          sem="bm")

                emit_wprep(d["wkT"], wk_mix, "wk_mix", True, "wk")
                emit_wprep(d["wvT"], wv_bf, "wv_bf", False, "wv")
                emit_wprep(d["wqT"], wq_mix, "wq_mix", True, "wq")
                emit_brows(d["bk2"][:], bk_rows, "bk_rows", True, "bk")
                emit_brows(d["bv2"][:], bv_rows, "bv_rows", False, "bv")
                emit_brows(d["bq2"][:], bq_rows, "bq_rows", True, "bq")

                def a_xload(kt):
                    xin, xtok = xinA[kt % 2], f"xinA{kt % 2}"
                    y_dma(xin[:], d["x_full"][kt * 128:(kt + 1) * 128, :],
                          [xtok], [xtok], f"A{kt}.x", sem=f"yx{kt % 2}")

                def a_lnops(kt):
                    xin, xtok = xinA[kt % 2], f"xinA{kt % 2}"
                    emit_ln(xin, xtok, xnA[kt % 2], f"xnA{kt % 2}",
                            wraw[:, 0:H], "wraw", 0, pre=f"A{kt}.")

                def a_pe(kt):
                    xn, xntok = xnA[kt % 2], f"xnA{kt % 2}"
                    y_dma(schi[:], d["sck_hi"][kt * 128:(kt + 1) * 128, :],
                          ["schi"], ["schi"], f"A{kt}.h", sem="ysh")
                    if debug and kt == 0:
                        g_dma(dbg["d_xin0"], xinA[0][:], ["xinA0"], [], "dbg.xin0")
                        g_dma(dbg["d_xn0"], xn[:], [xntok], [], "dbg.xn0")
                    emit_tr(xn, xntok, 8, xnfmA, "xnfmA", lambda j: j * 128,
                            f"A{kt}.xn", per_chunk=True, ev_alt=True)
                    for wi, (bks, wsb, wtok, btok, brows) in enumerate((
                            ((0, 1), wk_mix, "wk_mix", "bk_rows", bk_rows),
                            ((2, 3), wv_bf, "wv_bf", "bv_rows", bv_rows))):
                        for half, bk_i in enumerate(bks):
                            for dj in range(8):
                                t_op(lambda e, dj=dj, bk_i=bk_i, wsb=wsb, half=half:
                                     e.matmul(ps_[bk_i][:],
                                              xnfmA[:, dj * 128:(dj + 1) * 128],
                                              wsb[:, dj * H + half * 512:
                                                  dj * H + half * 512 + 512],
                                              start=(dj == 0), stop=False),
                                     [f"xnfmA#{dj}", wtok], [f"ps{bk_i}"],
                                     f"A{kt}.mm{wtok}{half}.{dj}")
                            t_op(lambda e, bk_i=bk_i, brows=brows, half=half:
                                 e.matmul(ps_[bk_i][:], ones2[:],
                                          brows[:, half * 512:half * 512 + 512],
                                          start=False, stop=True),
                                 ["ones2", btok], [f"ps{bk_i}"],
                                 f"A{kt}.b{wtok}{half}")
                        if wi == 0:
                            if kt + 2 < NT_K:
                                a_xload(kt + 2)
                            if kt + 1 < NT_K:
                                a_lnops(kt + 1)
                    v_op(lambda e: e.tensor_tensor(out=kro[:, 0:512], in0=ps_[0][:],
                                                   in1=schi[:], op=ALU.mult),
                         ["ps0", "schi"], ["kro"], f"A{kt}.r0")
                    v_op(lambda e: e.tensor_tensor(out=kro[:, 512:H], in0=ps_[1][:],
                                                   in1=schi[:], op=ALU.mult),
                         ["ps1", "schi"], ["kro"], f"A{kt}.r1")
                    v_op(lambda e, kt=kt: e.tensor_copy(
                            big1[:, kt * H:kt * H + 512], ps_[2][:]),
                         ["ps2"], ["big1"], f"A{kt}.v0")
                    v_op(lambda e, kt=kt: e.tensor_copy(
                            big1[:, kt * H + 512:(kt + 1) * H], ps_[3][:]),
                         ["ps3"], ["big1"], f"A{kt}.v1")
                    if debug and kt == 0:
                        g_dma(dbg["d_kro0"], kro[:], ["kro"], [], "dbg.kro0")
                        g_dma(dbg["d_ps0"], None, [], [], "")
                        sch.pending.pop()
                    emit_tr(kro, "kro", 8, big0, "big0",
                            lambda j, kt=kt: j * S + kt * 128, f"A{kt}.k",
                            ev_alt=True)

                a_xload(0)
                a_xload(1)
                a_lnops(0)
                for kt in range(NT_K):
                    a_pe(kt)

                sch.finalize_phase(blkA)

            es_b = contextlib.ExitStack()
            with es_b:
                xinB = [sb(es_b, f"xinB{i}", [128, H], F32) for i in range(2)]
                schiB = sb(es_b, "schiB", [128, 512], F32)
                xnB = sb(es_b, "xnB", [128, H], BF16)
                xnfmB = sb(es_b, "xnfmB", [128, H], BF16)
                qro = sb(es_b, "qro", [128, H], BF16)
                qfm = [sb(es_b, f"qfm{i}", [128, H], BF16) for i in range(2)]
                probs = [sb(es_b, f"probs{i}", [128, 512], BF16) for i in range(4)]
                pT = [sb(es_b, f"pT{i}", [128, 128], BF16) for i in range(4)]
                outB = [sb(es_b, "outB0", [128, H], F32)] * 2
                wchd = [sb(es_b, f"wch{i}", [128, 1024], F32) for i in range(2)]
                ssb = sb(es_b, "ssb", [128, S], F32)
                wst = [sb(es_b, "wst0", [128, 1024], BF16)] * 2
                wm1 = sb(es_b, "wm1", [128, 1024], BF16)
                wm2 = sb(es_b, "wm2", [128, 1024], BF16)

                blkB = es_b.enter_context(nc.Block("phaseB"))

                def emit_abs_chunk(src, col, width, tag, p):
                    w, wt = wchd[p], f"wch{p}"
                    g_dma(w[:, 0:width], src, [wt], [wt], f"{tag}.ld",
                          sem=f"gw{p}")
                    v_op(lambda e, w=w, col=col, width=width: e.reduce_sum(
                            wsums[:, col:col + 1], w[:, 0:width], axis=AX.X,
                            apply_absolute_value=True),
                         [wt], [f"ws{col}"], f"{tag}.abs")

                def emit_scales(first_col, n_col, tot_col, denom, out_col, tag):
                    v_op(lambda e, first_col=first_col, n_col=n_col, tot_col=tot_col:
                         e.reduce_sum(st2[:, tot_col:tot_col + 1],
                                      wsums[:, first_col:first_col + n_col],
                                      axis=AX.X),
                         [f"ws{c}" for c in range(first_col, first_col + n_col)],
                         [f"tot{tag}"], f"{tag}.tot")
                    t_op(lambda e, tot_col=tot_col: e.matmul(
                            ps_[2][:, 0:1], onesf[:, 0:128],
                            st2[:, tot_col:tot_col + 1], start=True, stop=True),
                         [f"tot{tag}", "onesf"], ["ps2"], f"{tag}.bc")
                    v_op(lambda e, denom=denom, out_col=out_col: e.tensor_scalar(
                            out=stc(out_col), in0=ps_[2][:, 0:1],
                            scalar1=1.0 / denom, scalar2=EPS,
                            op0=ALU.mult, op1=ALU.max),
                         ["ps2"], [f"mwc{tag}"], f"{tag}.mwc")
                    v_op(lambda e, out_col=out_col: e.reciprocal(
                            stc(out_col + 2), stc(out_col)),
                         [f"mwc{tag}"], [f"sw{tag}"], f"{tag}.sw")

                def emit_wq_load(src, parity, tag):
                    w, wt = wchd[parity], f"wch{parity}"
                    g_dma(w[:, 0:1024], src, [wt], [wt], f"{tag}.ld",
                          sem=f"gw{parity}")

                def emit_wquant_chunk(src, dst_dram, parity, mwtag, tag):
                    w, wt = wchd[parity], f"wch{parity}"
                    ws, wst_t = wst[parity], "wst0"
                    v_op(lambda e, w=w, mwtag=mwtag: e.tensor_scalar(
                            out=wm1[:], in0=w[:, 0:1024],
                            scalar1=stc(C_SW1 if mwtag == "w1" else C_SW2),
                            scalar2=0.5, op0=ALU.mult, op1=ALU.is_ge),
                         [wt, f"sw{mwtag}"], ["wm1"], f"{tag}.m1")
                    v_op(lambda e, w=w, mwtag=mwtag: e.tensor_scalar(
                            out=wm2[:], in0=w[:, 0:1024],
                            scalar1=stc(C_SW1 if mwtag == "w1" else C_SW2),
                            scalar2=-0.5, op0=ALU.mult, op1=ALU.is_le),
                         [wt, f"sw{mwtag}"], ["wm2"], f"{tag}.m2")
                    v_op(lambda e, ws=ws: e.tensor_tensor(
                            out=ws[:], in0=wm2[:], in1=wm1[:],
                            op=ALU.subtract),
                         ["wm1", "wm2"], [wst_t], f"{tag}.nq")
                    g_dma(dst_dram, ws[:], [wst_t],
                          [wst_t], f"{tag}.sp", sem=f"gs{parity}")

                abs_jobs = []
                for a in range(8):
                    for q4 in range(4):
                        abs_jobs.append((w1v[:, a, q4 * 1024:(q4 + 1) * 1024],
                                         a * 4 + q4, 1024, f"w1a{a}.{q4}"))
                for a in range(32):
                    abs_jobs.append((w2v[:, a, :], 32 + a, 1024, f"w2a{a}"))

                quant_jobs = []
                for a in range(8):
                    for q4 in range(4):
                        quant_jobs.append((w1v[:, a, q4 * 1024:(q4 + 1) * 1024],
                                           w1q_d[:, (a * 4 + q4) * 1024:
                                                 (a * 4 + q4 + 1) * 1024],
                                           "w1", f"qw1.{a}.{q4}"))
                for a in range(32):
                    quant_jobs.append((w2v[:, a, :],
                                       w2q_d[:, a * 1024:(a + 1) * 1024],
                                       "w2", f"qw2.{a}"))

                abs_i = [0]
                quant_i = [0]
                quant_ld_i = [0]

                def drip_weight_work(n_abs, n_quant):
                    for _ in range(n_abs):
                        if abs_i[0] < len(abs_jobs):
                            srcv, col, width, tag = abs_jobs[abs_i[0]]
                            emit_abs_chunk(srcv, col, width, tag, abs_i[0] % 2)
                            abs_i[0] += 1
                            if abs_i[0] == 32:
                                emit_scales(0, 32, 1, float(H) * FF, C_MW1, "w1")
                            if abs_i[0] == 64:
                                emit_scales(32, 32, 2, float(H) * FF, C_MW2, "w2")
                    for _ in range(n_quant):
                        if abs_i[0] >= 64 and quant_i[0] < len(quant_jobs):
                            i = quant_i[0]
                            while quant_ld_i[0] <= min(i + 1,
                                                       len(quant_jobs) - 1):
                                srcv, dst, mwtag, tag = \
                                    quant_jobs[quant_ld_i[0]]
                                emit_wq_load(srcv, quant_ld_i[0] % 2, tag)
                                quant_ld_i[0] += 1
                            srcv, dst, mwtag, tag = quant_jobs[i]
                            emit_wquant_chunk(srcv, dst, i % 2, mwtag, tag)
                            quant_i[0] += 1

                def stage1_ln(qt):
                    xin, xtok = xinB[qt % 2], f"xinB{qt % 2}"
                    y_dma(xin[:], d["x_own"][qt * 128:(qt + 1) * 128, :],
                          [xtok], [xtok], f"B{qt}.x", sem=f"yb{qt % 2}")
                    y_dma(schiB[:], d["scq_hi"][qt * 128:(qt + 1) * 128, :],
                          ["schiB"], ["schiB"], f"B{qt}.h", sem="ybh")
                    emit_ln(xin, xtok, xnB, "xnB", qro, "qro", 0, pre=f"B{qt}.")

                def stage1_pe(qt):
                    qf, qftok = qfm[qt % 2], f"qfm{qt % 2}"
                    emit_tr(xnB, "xnB", 8, xnfmB, "xnfmB", lambda j: j * 128,
                            f"B{qt}.xn", per_chunk=True)
                    for half, bk_i in enumerate((0, 1)):
                        for dj in range(8):
                            t_op(lambda e, dj=dj, bk_i=bk_i, half=half: e.matmul(
                                    ps_[bk_i][:], xnfmB[:, dj * 128:(dj + 1) * 128],
                                    wq_mix[:, dj * H + half * 512:
                                           dj * H + half * 512 + 512],
                                    start=(dj == 0), stop=False),
                                 [f"xnfmB#{dj}", "wq_mix"], [f"ps{bk_i}"],
                                 f"B{qt}.q{half}.{dj}")
                        t_op(lambda e, bk_i=bk_i, half=half: e.matmul(
                                ps_[bk_i][:], ones2[:],
                                bq_rows[:, half * 512:half * 512 + 512],
                                start=False, stop=True),
                             ["ones2", "bq_rows"], [f"ps{bk_i}"], f"B{qt}.qb{half}")
                    v_op(lambda e: e.tensor_tensor(out=qro[:, 0:512], in0=ps_[0][:],
                                                   in1=schiB[:], op=ALU.mult),
                         ["ps0", "schiB"], ["qro"], f"B{qt}.r0")
                    v_op(lambda e: e.tensor_tensor(out=qro[:, 512:H], in0=ps_[1][:],
                                                   in1=schiB[:], op=ALU.mult),
                         ["ps1", "schiB"], ["qro"], f"B{qt}.r1")

                def stage1_pe2(qt):
                    qf, qftok = qfm[qt % 2], f"qfm{qt % 2}"
                    emit_tr(qro, "qro", 8, qf, qftok, lambda j: j * 128,
                            f"B{qt}.q", per_chunk=True)

                def emit_scores(qt, ln_cb=None):
                    qf, qftok = qfm[qt % 2], f"qfm{qt % 2}"
                    for kc in range(8):
                        bk_i = kc % 4
                        for ej in range(8):
                            t_op(lambda e, ej=ej, kc=kc, bk_i=bk_i, qf=qf: e.matmul(
                                    ps_[bk_i][:], qf[:, ej * 128:(ej + 1) * 128],
                                    big0[:, ej * S + kc * 512:ej * S + kc * 512 + 512],
                                    start=(ej == 0), stop=(ej == 7)),
                                 [f"{qftok}#{ej}", "big0"], [f"ps{bk_i}"],
                                 f"B{qt}.s{kc}.{ej}")
                        s_op(lambda e, kc=kc, bk_i=bk_i: e.copy(
                                ssb[:, kc * 512:(kc + 1) * 512], ps_[bk_i][:]),
                             [f"ps{bk_i}"], [f"ssb{kc}"], f"B{qt}.sev{kc}")
                        v_op(lambda e, kc=kc: e.reduce_max(
                                st2[:, 8 + kc:9 + kc],
                                ssb[:, kc * 512:(kc + 1) * 512], axis=AX.X),
                             [f"ssb{kc}"], [f"smx{kc}"], f"B{qt}.smx{kc}")
                        if kc == 2 and ln_cb is not None:
                            ln_cb()
                        drip_weight_work(1, 0)

                def stage2(qt):
                    xin, xtok = xinB[qt % 2], f"xinB{qt % 2}"
                    v_op(lambda e: e.reduce_max(stc(10), st2[:, 8:16],
                                                axis=AX.X),
                         [f"smx{kc}" for kc in range(8)], ["qM"],
                         f"B{qt}.rmax")
                    s_op(lambda e: e.activation(out=stc(11), in_=stc(10),
                                                func=AF.Identity,
                                                scale=-INV_SCALE),
                         ["qM"], ["ebias"], f"B{qt}.eb")
                    for kc in range(8):
                        bk_i = kc % 4
                        s_op(lambda e, kc=kc, bk_i=bk_i: e.activation(
                                out=probs[bk_i][:],
                                in_=ssb[:, kc * 512:(kc + 1) * 512], func=AF.Exp,
                                scale=INV_SCALE, bias=stc(11),
                                accum_out=lparts[:, kc:kc + 1]),
                             [f"ssb{kc}", "ebias"], [f"probs{bk_i}", "lparts"],
                             f"B{qt}.e{kc}")
                        for sub in range(4):
                            t_op(lambda e, sub=sub, bk_i=bk_i: e.transpose(
                                    psT[sub][:, 0:128],
                                    probs[bk_i][:, sub * 128:(sub + 1) * 128],
                                    identb[:]),
                                 [f"probs{bk_i}", "identb"], [f"psT{sub}"],
                                 f"B{qt}.pT{kc}.{sub}")
                            s_op(lambda e, sub=sub: e.copy(
                                    pT[sub][:], psT[sub][:, 0:128]),
                                 [f"psT{sub}"], [f"pT{sub}"],
                                 f"B{qt}.pE{kc}.{sub}")
                        for sub in range(4):
                            kt64 = kc * 4 + sub
                            for half, bk2 in enumerate((2, 3)):
                                t_op(lambda e, sub=sub, kt64=kt64, half=half,
                                     bk2=bk2, first=(kt64 == 0):
                                     e.matmul(ps_[bk2][:], pT[sub][:],
                                              big1[:, kt64 * H + half * 512:
                                                   kt64 * H + half * 512 + 512],
                                              start=first, stop=(kt64 == 31)),
                                     [f"pT{sub}", "big1"], [f"ps{bk2}"],
                                     f"B{qt}.pv{kc}.{sub}.{half}")
                        drip_weight_work(0, 1)
                    v_op(lambda e: e.reduce_sum(stc(12), lparts[:, 0:8], axis=AX.X),
                         ["lparts"], ["lsum"], f"B{qt}.l")
                    v_op(lambda e: e.reciprocal(stc(13), stc(12)),
                         ["lsum"], ["rl"], f"B{qt}.rl")
                    ob, obt = outB[0], "outB0"
                    v_op(lambda e, ob=ob, xin=xin: e.scalar_tensor_tensor(
                            out=ob[:, 0:512], in0=ps_[2][:], scalar=stc(13),
                            in1=xin[:, 0:512], op0=ALU.mult, op1=ALU.add),
                         ["ps2", "rl", xtok], [obt], f"B{qt}.o0")
                    v_op(lambda e, ob=ob, xin=xin: e.scalar_tensor_tensor(
                            out=ob[:, 512:H], in0=ps_[3][:], scalar=stc(13),
                            in1=xin[:, 512:H], op0=ALU.mult, op1=ALU.add),
                         ["ps3", "rl", xtok], [obt], f"B{qt}.o1")
                    g_dma(part_d[qt * 128:(qt + 1) * 128, :], ob[:], [obt], [obt],
                          f"B{qt}.st", sem=f"go{qt % 2}")

                stage1_ln(0)
                stage1_pe(0)
                stage1_pe2(0)
                for qt in range(NT_Q):
                    emit_scores(qt, ln_cb=(
                        (lambda q=qt: stage1_ln(q + 1))
                        if qt + 1 < NT_Q else None))
                    if qt + 1 < NT_Q:
                        stage1_pe(qt + 1)
                    stage2(qt)
                    if qt + 1 < NT_Q:
                        stage1_pe2(qt + 1)
                drip_weight_work(64, 64)

                if debug:
                    g_dma(dbg["d_ssb"], ssb[:], ["ssb"], [], "dbgBs")
                    g_dma(dbg["d_qro"], qro[:], ["qro"], [], "dbgBq")
                    g_dma(dbg["d_lp2"], lparts[:], ["lparts"], [], "dbgBl")
                    g_dma(dbg["d_stats"], stats[:], ["ebias"], [], "dbgB0")
                    g_dma(dbg["d_lparts"], lparts[:], ["lparts"], [], "dbgB1")
                    g_dma(dbg["d_w1q"], w1q_d[:], [], [], "dbgB2")
                sch.finalize_phase(blkB)

        es_c = contextlib.ExitStack()
        with es_c:
            gamma = sb(es_c, "gamma", [128, H], F32)
            beta = sb(es_c, "beta", [128, H], F32)
            xinC = [sb(es_c, f"xinC{i}", [128, H], F32) for i in range(3)]
            xpart = xinC
            xnC = sb(es_c, "xnC", [128, H], F32)
            ln2b = sb(es_c, "ln2b", [128, H], F32)
            zbuf = sb(es_c, "zbuf", [128, 1024], F32)
            cibuf = sb(es_c, "cibuf", [128, 1024], I32)
            m1b = sb(es_c, "m1b", [128, 1024], BF16)
            nxq = sb(es_c, "nxq", [128, H], BF16)
            xqfm = sb(es_c, "xqfm", [128, H], BF16)
            hbuf = sb(es_c, "hbuf", [128, FF], F32)
            nhqd = [sb(es_c, f"nhq{i}", [128, 1024], BF16) for i in range(2)]
            hqfm = sb(es_c, "hqfm", [128, 2048], BF16)
            b1rows = sb(es_c, "b1rows", [4, FF], BF16)
            b2rows = sb(es_c, "b2rows", [4, H], BF16)
            blhs1 = sb(es_c, "blhs1", [4, 128], BF16)
            blhs2 = sb(es_c, "blhs2", [4, 128], BF16)
            stg4 = zbuf[0:4, :]
            stg4h = m1b[0:4, :]
            stg4l = nhqd[0][0:4, :]

            blkC = es_c.enter_context(nc.Block("phaseC"))

            for j in range(4):
                y_dma(big0[:, j * 8192:(j + 1) * 8192],
                      w1q_d[:, j * 8192:(j + 1) * 8192],
                      ["big0"], ["big0"], f"C.w1q{j}", sem="yr")
            for j in range(4):
                y_dma(big1[:, j * 8192:(j + 1) * 8192],
                      w2q_d[:, j * 8192:(j + 1) * 8192],
                      ["big1"], ["big1"], f"C.w2q{j}", sem="yr")
            y_dma(gamma[:], d["gamma_bc"], [], ["gamma"], "C.g", sem="yr")
            y_dma(beta[:], d["beta_bc"], [], ["beta"], "C.b", sem="yr")

            def emit_bsplit(src, nq, rows, rtok, tag):
                w = nq
                y_dma(stg4[:, 0:w], src, ["zbuf"], ["zbuf"], f"{tag}.ld", sem="yr")
                v_op(lambda e, w=w: e.tensor_copy(stg4h[:, 0:w], stg4[:, 0:w]),
                     ["zbuf"], ["m1b"], f"{tag}.hi")
                v_op(lambda e, w=w: e.scalar_tensor_tensor(
                        out=stg4l[:, 0:w], in0=stg4[:, 0:w], scalar=1.0,
                        in1=stg4h[:, 0:w], op0=ALU.mult, op1=ALU.subtract),
                     ["zbuf", "m1b"], ["nhq0"], f"{tag}.lo")
                full = 4 * w
                for r, srct, srcb in ((0, "m1b", stg4h), (1, "m1b", stg4h),
                                      (2, "nhq0", stg4l), (3, "nhq0", stg4l)):
                    for p in range(4):
                        g_dma(rows[r:r + 1, p * w:(p + 1) * w],
                              srcb[p:p + 1, 0:w],
                              [srct], [rtok], f"{tag}.mv{r}.{p}")

            emit_bsplit(d["b1r"], FF // 4, b1rows, "b1rows", "b1")
            emit_bsplit(d["b2r"], H // 4, b2rows, "b2rows", "b2")

            CQ = 21

            st2b = sb(es_c, "st2b", [128, 8], BF16)

            def emit_aquant(src, stoks, sx_tok, sx_col, width, off, nxq_out,
                            nxq_tok, tag):
                sl = slice(off, off + width)
                v_op(lambda e: e.scalar_tensor_tensor(
                        out=zbuf[:, 0:width], in0=src[:, sl],
                        scalar=stc(sx_col), in1=src[:, sl],
                        op0=ALU.mult, op1=ALU.bypass),
                     list(stoks) + [sx_tok], ["zbuf"], f"{tag}.z")
                v_op(lambda e: e.tensor_copy(cibuf[:, 0:width], zbuf[:, 0:width]),
                     ["zbuf"], ["cibuf"], f"{tag}.ci")
                v_op(lambda e: e.scalar_tensor_tensor(
                        out=zbuf[:, 0:width], in0=zbuf[:, 0:width], scalar=1.0,
                        in1=cibuf[:, 0:width], op0=ALU.mult, op1=ALU.subtract),
                     ["zbuf", "cibuf"], ["zbuf"], f"{tag}.d")
                v_op(lambda e: e.tensor_scalar(
                        out=m1b[:, 0:width], in0=zbuf[:, 0:width],
                        scalar1=0.5, scalar2=None, op0=ALU.is_ge),
                     ["zbuf"], ["m1b"], f"{tag}.m1")
                v_op(lambda e: e.scalar_tensor_tensor(
                        out=m1b[:, 0:width], in0=zbuf[:, 0:width], scalar=-0.5,
                        in1=m1b[:, 0:width], op0=ALU.is_le, op1=ALU.subtract),
                     ["zbuf", "m1b"], ["m1b"], f"{tag}.mc")
                v_op(lambda e: e.scalar_tensor_tensor(
                        out=nxq_out, in0=cibuf[:, 0:width], scalar=-1.0,
                        in1=m1b[:, 0:width], op0=ALU.mult, op1=ALU.add),
                     ["cibuf", "m1b"], [nxq_tok], f"{tag}.nx")

            def emit_split(src_col, srctok, c0, pre):
                s_op(lambda e: e.copy(st2b[:, c0:c0 + 1], stc(src_col)),
                     [srctok], [f"{pre}hb"], f"{pre}.hb")
                s_op(lambda e: e.copy(st2b[:, c0 + 2:c0 + 3],
                                      st2b[:, c0:c0 + 1]),
                     [f"{pre}hb"], [f"{pre}hb2"], f"{pre}.hb2")
                s_op(lambda e: e.copy(st2[:, 5:6], st2b[:, c0:c0 + 1]),
                     [f"{pre}hb"], [f"{pre}hf"], f"{pre}.hf")
                s_op(lambda e: e.activation(out=st2[:, 6:7], in_=st2[:, 5:6],
                                            func=AF.Identity, scale=-1.0,
                                            bias=stc(src_col)),
                     [f"{pre}hf", srctok], [f"{pre}lf"], f"{pre}.lf")
                s_op(lambda e: e.copy(st2b[:, c0 + 1:c0 + 2], st2[:, 6:7]),
                     [f"{pre}lf"], [f"{pre}lb"], f"{pre}.lb")
                s_op(lambda e: e.copy(st2b[:, c0 + 3:c0 + 4], st2[:, 6:7]),
                     [f"{pre}lf"], [f"{pre}lb2"], f"{pre}.lb2")

            def emit_blhs(c0, pre, rows, rtok, tag):
                t_op(lambda e: e.transpose(psT[0][0:4, 0:128],
                                           st2b[:, c0:c0 + 4], identb[:]),
                     [f"{pre}hb", f"{pre}hb2", f"{pre}lb", f"{pre}lb2",
                      "identb"], ["psT0"], f"{tag}.T")
                s_op(lambda e: e.copy(rows[0:4, :], psT[0][0:4, 0:128]),
                     ["psT0"], [rtok], f"{tag}.E")

            def c_front_vs(tt):
                xin, xtok = xinC[tt % 3], f"xinC{tt % 3}"
                y_dma(xin[:], d["x_own"][tt * 128:(tt + 1) * 128, :],
                      [xtok], [xtok], f"C{tt}.x", sem=f"yc{tt % 3}")
                emit_ln(xin, xtok, xnC, "xnC", zbuf[:, 0:H], "zbuf", 0,
                        xnsum_col=7, pre=f"C{tt}.")
                v_op(lambda e: e.tensor_scalar(out=stc(CQ), in0=stc(7),
                                               scalar1=-1.0 / H, scalar2=None,
                                               op0=ALU.mult),
                     ["xnsum7"], ["nmu2"], f"C{tt}.mu2")
                s_op(lambda e: e.activation(out=zbuf[:, 0:H], in_=xnC[:],
                                            func=AF.Square, bias=stc(CQ),
                                            accum_out=stc(CQ + 1)),
                     ["xnC", "nmu2"], ["zbuf", "v2s"], f"C{tt}.v2")
                v_op(lambda e: e.tensor_scalar(out=stc(CQ + 2), in0=stc(CQ + 1),
                                               scalar1=1.0 / H, scalar2=EPS,
                                               op0=ALU.mult, op1=ALU.add),
                     ["v2s"], ["v2p"], f"C{tt}.vp2")
                v_op(lambda e: e.reciprocal(stc(CQ + 3), stc(CQ + 2)),
                     ["v2p"], ["ri2"], f"C{tt}.ri2")
                s_op(lambda e: e.activation(out=stc(CQ + 4), in_=stc(CQ + 3),
                                            func=AF.Sqrt),
                     ["ri2"], ["rs2"], f"C{tt}.rs2")
                v_op(lambda e: e.tensor_tensor(out=stc(CQ + 5), in0=stc(CQ),
                                               in1=stc(CQ + 4), op=ALU.mult),
                     ["nmu2", "rs2"], ["nm2r"], f"C{tt}.nm2r")
                s_op(lambda e: e.activation(out=ln2b[:], in_=xnC[:], func=AF.Identity,
                                            scale=stc(CQ + 4), bias=stc(CQ + 5)),
                     ["xnC", "rs2", "nm2r"], ["ln2b"], f"C{tt}.l2n")
                v_op(lambda e: e.tensor_tensor(out=ln2b[:], in0=ln2b[:],
                                               in1=gamma[:], op=ALU.mult),
                     ["ln2b", "gamma"], ["ln2b"], f"C{tt}.gam")
                v_op(lambda e: e.tensor_tensor(out=ln2b[:], in0=ln2b[:],
                                               in1=beta[:], op=ALU.add),
                     ["ln2b", "beta"], ["ln2b"], f"C{tt}.bet")
                v_op(lambda e: e.reduce_max(stc(CQ + 6), ln2b[:], axis=AX.X,
                                            apply_absolute_value=True),
                     ["ln2b"], ["mx1"], f"C{tt}.mx1")
                v_op(lambda e: e.tensor_scalar(out=stc(CQ + 7), in0=stc(CQ + 6),
                                               scalar1=EPS, scalar2=None,
                                               op0=ALU.max),
                     ["mx1"], ["den1"], f"C{tt}.den1")
                v_op(lambda e: e.reciprocal(stc(CQ + 8), stc(CQ + 7)),
                     ["den1"], ["rd1"], f"C{tt}.rd1")
                v_op(lambda e: e.tensor_scalar(out=stc(CQ + 9), in0=stc(CQ + 8),
                                               scalar1=127.0, scalar2=None,
                                               op0=ALU.mult),
                     ["rd1"], ["sx1"], f"C{tt}.sx1")
                rqc = 40 + (tt % 2)
                v_op(lambda e, rqc=rqc: e.tensor_tensor(
                        out=stc(rqc), in0=stc(CQ + 7),
                        in1=stc(C_MW1), op=ALU.mult),
                     ["den1"], ["rqt1"], f"C{tt}.rt1")
                v_op(lambda e, rqc=rqc: e.tensor_scalar(
                        out=stc(rqc), in0=stc(rqc),
                        scalar1=1.0 / 127.0, scalar2=None,
                        op0=ALU.mult),
                     ["rqt1"], [f"rq1{tt % 2}"], f"C{tt}.rq1")
                v_op(lambda e, rqc=rqc: e.reciprocal(stc(CQ + 11), stc(rqc)),
                     [f"rq1{tt % 2}"], ["sr1"], f"C{tt}.sr1")

            def c_front_vs2(tt):
                emit_split(CQ + 11, "sr1", 0, "s1")
                emit_aquant(ln2b, ["ln2b"], "sx1", CQ + 9, H, 0, nxq[:], "nxq",
                            f"C{tt}.xq")

            def c_front_pe(tt):
                emit_tr(nxq, "nxq", 8, xqfm, "xqfm", lambda j: j * 128,
                        f"C{tt}.xf", per_chunk=True)

            def c_mid(tt, cb1=None, cb2=None):
                for hc in range(8):
                    bk_i = hc % 4
                    if hc == 0 and cb1 is not None:
                        cb1()
                    if hc == 4 and cb2 is not None:
                        cb2()
                    for dj in range(8):
                        t_op(lambda e, dj=dj, hc=hc, bk_i=bk_i: e.matmul(
                                ps_[bk_i][:], xqfm[:, dj * 128:(dj + 1) * 128],
                                big0[:, dj * FF + hc * 512:dj * FF + hc * 512 + 512],
                                start=(dj == 0), stop=False),
                             [f"xqfm#{dj}", "big0"], [f"ps{bk_i}"],
                             f"C{tt}.m1.{hc}.{dj}")
                    t_op(lambda e, hc=hc, bk_i=bk_i: e.matmul(
                            ps_[bk_i][:], blhs1[:],
                            b1rows[:, hc * 512:hc * 512 + 512],
                            start=False, stop=True),
                         ["blhs1", "b1rows"], [f"ps{bk_i}"], f"C{tt}.bb1.{hc}")
                    s_op(lambda e, hc=hc, bk_i=bk_i, tt=tt: e.activation(
                            out=hbuf[:, hc * 512:(hc + 1) * 512], in_=ps_[bk_i][:],
                            func=AF.Gelu, scale=stc(40 + (tt % 2))),
                         [f"ps{bk_i}", f"rq1{tt % 2}"], [f"hbuf#{hc}"],
                         f"C{tt}.g{hc}")
                    v_op(lambda e, hc=hc: e.reduce_max(
                            st2[:, 8 + hc:9 + hc],
                            hbuf[:, hc * 512:(hc + 1) * 512],
                            axis=AX.X, apply_absolute_value=True),
                         [f"hbuf#{hc}"], [f"hm{hc}"], f"C{tt}.hm{hc}")

            def c_tail(tt):
                v_op(lambda e: e.reduce_max(stc(CQ + 15), st2[:, 8:16], axis=AX.X),
                     [f"hm{h}" for h in range(8)], ["mx2"], f"C{tt}.mx2")
                v_op(lambda e: e.tensor_scalar(out=stc(CQ + 16), in0=stc(CQ + 15),
                                               scalar1=EPS, scalar2=None,
                                               op0=ALU.max),
                     ["mx2"], ["den2"], f"C{tt}.den2")
                v_op(lambda e: e.reciprocal(stc(CQ + 17), stc(CQ + 16)),
                     ["den2"], ["rd2"], f"C{tt}.rd2")
                v_op(lambda e: e.tensor_scalar(out=stc(CQ + 12), in0=stc(CQ + 17),
                                               scalar1=127.0, scalar2=None,
                                               op0=ALU.mult),
                     ["rd2"], ["sx2"], f"C{tt}.sx2")
                v_op(lambda e: e.tensor_tensor(out=stc(CQ + 13), in0=stc(CQ + 16),
                                               in1=stc(C_MW2), op=ALU.mult),
                     ["den2"], ["rqt2"], f"C{tt}.rt2")
                v_op(lambda e: e.tensor_scalar(out=stc(CQ + 13), in0=stc(CQ + 13),
                                               scalar1=1.0 / 127.0, scalar2=None,
                                               op0=ALU.mult),
                     ["rqt2"], ["rq2"], f"C{tt}.rq2")
                v_op(lambda e: e.reciprocal(stc(CQ + 14), stc(CQ + 13)),
                     ["rq2"], ["sr2"], f"C{tt}.sr2")
                emit_split(CQ + 14, "sr2", 4, "s2")

                def emit_aq_q(q):
                    emit_aquant(hbuf, [f"hbuf#{2 * q}", f"hbuf#{2 * q + 1}"],
                                "sx2", CQ + 12, 1024, q * 1024,
                                nhqd[q % 2][:], f"nhq{q % 2}",
                                f"C{tt}.hq{q}")

                def emit_hqT(q):
                    emit_tr(nhqd[q % 2], f"nhq{q % 2}", 8, hqfm, "hqfm",
                            lambda j, q=q: (q % 2) * 1024 + j * 128,
                            f"C{tt}.hf{q}", per_chunk=True)

                def emit_mm2(hf, rng=range(16)):
                    for hj16 in rng:
                        hj = hf * 16 + hj16
                        for half, bk2 in enumerate((2, 3)):
                            t_op(lambda e, hj16=hj16, hj=hj, half=half, bk2=bk2:
                                 e.matmul(ps_[bk2][:],
                                          hqfm[:, hj16 * 128:(hj16 + 1) * 128],
                                          big1[:, hj * H + half * 512:
                                               hj * H + half * 512 + 512],
                                          start=(hj == 0), stop=False),
                                 [f"hqfm#{hj16}", "big1"], [f"ps{bk2}"],
                                 f"C{tt}.m2.{hj}.{half}")

                emit_aq_q(0)
                emit_hqT(0)
                emit_aq_q(1)
                emit_mm2(0, range(8))
                emit_hqT(1)
                emit_aq_q(2)
                emit_mm2(0, range(8, 16))
                emit_hqT(2)
                emit_aq_q(3)
                emit_mm2(1, range(8))
                emit_hqT(3)
                emit_blhs(4, "s2", blhs2, "blhs2", f"C{tt}.b2")
                emit_mm2(1, range(8, 16))
                for half, bk2 in enumerate((2, 3)):
                    t_op(lambda e, half=half, bk2=bk2: e.matmul(
                            ps_[bk2][:], blhs2[:],
                            b2rows[:, half * 512:half * 512 + 512],
                            start=False, stop=True),
                         ["blhs2", "b2rows"], [f"ps{bk2}"], f"C{tt}.bb2.{half}")
                if tt + 1 < NT_Q:
                    emit_blhs(0, "s1", blhs1, "blhs1", f"C{tt + 1}.b1")

            def c_pload(tt):
                y_dma(xpart[tt % 3][:], part_d[tt * 128:(tt + 1) * 128, :],
                      [f"xinC{tt % 3}"], [f"xinC{tt % 3}"], f"C{tt}.p",
                      sem=f"yc{tt % 3}")

            def c_out(tt):
                xp, xptok = xpart[tt % 3], f"xinC{tt % 3}"
                v_op(lambda e, xp=xp: e.scalar_tensor_tensor(
                        out=xp[:, 0:512], in0=ps_[2][:], scalar=stc(CQ + 13),
                        in1=xp[:, 0:512], op0=ALU.mult, op1=ALU.add),
                     ["ps2", "rq2", xptok], [xptok], f"C{tt}.o0")
                v_op(lambda e, xp=xp: e.scalar_tensor_tensor(
                        out=xp[:, 512:H], in0=ps_[3][:], scalar=stc(CQ + 13),
                        in1=xp[:, 512:H], op0=ALU.mult, op1=ALU.add),
                     ["ps3", "rq2", xptok], [xptok], f"C{tt}.o1")
                g_dma(out_d[tt * 128:(tt + 1) * 128, :], xp[:], [xptok],
                      [xptok], f"C{tt}.st", sem=f"gc{tt % 2}")

            c_front_vs(0)
            c_front_vs2(0)
            emit_blhs(0, "s1", blhs1, "blhs1", "C0.b1")
            c_front_pe(0)
            for tt in range(NT_Q):
                if tt > 0:
                    c_out(tt - 1)
                c_mid(tt,
                      cb1=((lambda t=tt: c_front_vs(t + 1))
                           if tt + 1 < NT_Q else None),
                      cb2=((lambda t=tt: c_front_vs2(t + 1))
                           if tt + 1 < NT_Q else None))
                if tt + 1 < NT_Q:
                    c_front_pe(tt + 1)
                c_pload(tt)
                c_tail(tt)
                if debug and tt == 0:
                    g_dma(dbg["d_xnC"], xnC[:], ["xnC"], [], "dbg.xnC")
                    g_dma(dbg["d_ln2b"], ln2b[:], ["ln2b"], [], "dbg.ln2b")
                    g_dma(dbg["d_z"], zbuf[:], ["zbuf"], [], "dbg.z")
                    g_dma(dbg["d_nxq"], nxq[:], ["nxq"], [], "dbg.nxq")
                    g_dma(dbg["d_cstats"], stats[:], ["sr1"], [], "dbg.cst")
            c_out(NT_Q - 1)
            sch.finalize_phase(blkC, final=True)

    return nc


_NC_CACHE = []


def kernel(**inputs):
    X = np.ascontiguousarray(np.asarray(inputs["L0_attn_hidden_states"],
                                        dtype=np.float32))
    Wq = np.asarray(inputs["L0_attn_Wq"], dtype=np.float32)
    Wk = np.asarray(inputs["L0_attn_Wk"], dtype=np.float32)
    Wv = np.asarray(inputs["L0_attn_Wv"], dtype=np.float32)
    bq = np.asarray(inputs["L0_attn_bq"], dtype=np.float32)
    bk = np.asarray(inputs["L0_attn_bk"], dtype=np.float32)
    bv = np.asarray(inputs["L0_attn_bv"], dtype=np.float32)
    w1 = np.asarray(inputs["L0_ffn_w1"], dtype=np.float32)
    b1 = np.asarray(inputs["L0_ffn_b1"], dtype=np.float32)
    w2 = np.asarray(inputs["L0_ffn_w2"], dtype=np.float32)
    b2 = np.asarray(inputs["L0_ffn_b2"], dtype=np.float32)
    gamma = np.asarray(inputs["L0_ffn_gamma"], dtype=np.float32)
    beta = np.asarray(inputs["L0_ffn_beta"], dtype=np.float32)

    perm = np.concatenate([np.arange(0, H, 2), np.arange(1, H, 2)])
    wqT = np.ascontiguousarray(Wq.T[:, perm])
    wkT = np.ascontiguousarray(Wk.T[:, perm])
    wvT = np.ascontiguousarray(Wv.T)
    w1T = np.ascontiguousarray(w1.T)
    w2T = np.ascontiguousarray(w2.T)

    pos = np.arange(S, dtype=np.float32)
    inv = (10000.0 ** (np.arange(0, 512, 2, dtype=np.float64) / 512.0)).astype(np.float32)
    rad = (pos[:, None] / inv).astype(np.float32)
    sc_hi = np.stack([np.sin(rad.astype(np.float64)),
                      np.cos(rad.astype(np.float64))],
                     axis=-1).reshape(S, 512).astype(np.float32)

    identb = np.eye(128, dtype=BFDT)
    identf = np.eye(128, dtype=np.float32)

    in_maps = []
    for c in range(8):
        b, hh = c // 2, c % 2
        m = {
            "x_full": np.ascontiguousarray(X[b]),
            "x_own": np.ascontiguousarray(X[b, hh * HALF:(hh + 1) * HALF]),
            "wqT": wqT, "wkT": wkT, "wvT": wvT,
            "bq2": bq[perm][None, :], "bk2": bk[perm][None, :], "bv2": bv[None, :],
            "w1T": w1T, "w2T": w2T,
            "b1r": np.ascontiguousarray(b1.reshape(4, FF // 4)),
            "b2r": np.ascontiguousarray(b2.reshape(4, H // 4)),
            "gamma_bc": np.ascontiguousarray(np.broadcast_to(gamma, (128, H))),
            "beta_bc": np.ascontiguousarray(np.broadcast_to(beta, (128, H))),
            "sck_hi": sc_hi,
            "scq_hi": np.ascontiguousarray(sc_hi[hh * HALF:(hh + 1) * HALF]),
            "ident_b": identb, "ident_f": identf,
        }
        in_maps.append(m)

    if not _NC_CACHE:
        _NC_CACHE.append(build_nc())
    nc = _NC_CACHE[0]
    res = run_bass_kernel_spmd(nc, in_maps, core_ids=list(range(8)))
    out = np.empty((B, S, H), dtype=np.float32)
    for c in range(8):
        b, hh = c // 2, c % 2
        out[b, hh * HALF:(hh + 1) * HALF] = res.results[c]["out"]
    return out


# version 26
# speedup vs baseline: 1.0686x; 1.0069x over previous
import contextlib
import math

import numpy as np
import ml_dtypes

import concourse.bass as bass
import concourse.mybir as mybir
from concourse.bass_utils import run_bass_kernel_spmd

F32 = mybir.dt.float32
BF16 = mybir.dt.bfloat16
I32 = mybir.dt.int32
AF = mybir.ActivationFunctionType
ALU = mybir.AluOpType
AX = mybir.AxisListType
BFDT = ml_dtypes.bfloat16

B, S, H = 4, 4096, 1024
HALF = S // 2
FF = 4096
NT_K = S // 128
NT_Q = HALF // 128
INV_SCALE = 1.0 / math.sqrt(512.0)
EPS = 1e-5


class Sched:

    ENGINES = ("t", "v", "s", "g")
    STREAM_OF_Q = {"y": "y", "gd": "g"}

    def __init__(self, nc, es):
        self.nc = nc
        self.es = es
        self.sems = {}
        self.sem_policy = {}
        self.cnt = {}
        for e in self.ENGINES:
            self._mk(e, "ord")
        self.pending = []
        self.last_writer = {}
        self.readers = {}
        self.hw = {}
        self.bar_vals = None
        self.bar_seen = set()

    def _mk(self, name, policy):
        if name not in self.sems:
            self.sems[name] = self.es.enter_context(self.nc.semaphore("sem_" + name))
            self.sem_policy[name] = policy
            self.cnt[name] = 0

    def dma_sem(self, name, policy="ord"):
        self._mk(name, policy)

    def op(self, eng, emit, reads=(), writes=(), tag="", sem=None, stream=None,
           small=True):
        if eng in ("y", "gd"):
            assert sem is not None, tag
            self._mk(sem, self.sem_policy.get(sem, "ord"))
            stream = self.STREAM_OF_Q[eng]
            step = 16
        else:
            sem = eng
            stream = eng
            step = 1
        i = len(self.pending)
        deps = set()
        for r in reads:
            w = self.last_writer.get(r)
            if w is not None:
                deps.add(w)
            self.readers.setdefault(r, []).append(i)
        for w in writes:
            pw = self.last_writer.get(w)
            if pw is not None:
                deps.add(pw)
            for rd in self.readers.get(w, ()):
                deps.add(rd)
            self.last_writer[w] = i
            self.readers[w] = []
        self.pending.append(dict(eng=eng, sem=sem, stream=stream, step=step,
                                 emit=emit, deps=deps, rawdeps=set(deps),
                                 tag=tag, small=small,
                                 need_inc=(eng in ("y", "gd")), inc_ord=None))

    def finalize_phase(self, block, final=False):
        ops = self.pending
        for i, o in enumerate(ops):
            kept = set()
            same_eng_deps = []
            for dd in o["deps"]:
                po = ops[dd]
                if po["eng"] == o["eng"] and po["eng"] not in ("y", "gd"):
                    same_eng_deps.append(dd)
                    continue
                if po["eng"] == o["eng"] and po["sem"] == o["sem"]:
                    continue
                kept.add(dd)
            o["deps"] = kept
            o["same_eng_deps"] = same_eng_deps
            for dd in kept:
                ops[dd]["need_inc"] = True
        last_flush = {}
        for i, o in enumerate(ops):
            o["pre_drain"] = False
            if o["eng"] in ("v", "s", "g"):
                e = o["eng"]
                for dd in o["same_eng_deps"]:
                    if ops[dd]["small"] and dd > last_flush.get(e, -1):
                        o["pre_drain"] = True
                        last_flush[e] = i - 1 if i else -1
                        break
                if o["need_inc"]:
                    last_flush[e] = i
        last_of = {}
        for i, o in enumerate(ops):
            last_of[o["eng"] if o["eng"] in self.ENGINES else o["sem"]] = i
        for i in last_of.values():
            ops[i]["need_inc"] = True
        totals_at = [None] * len(ops)
        run_tot = dict(self.cnt)
        for i, o in enumerate(ops):
            totals_at[i] = None
            if o["need_inc"]:
                run_tot[o["sem"]] = run_tot.get(o["sem"], 0) + 1
                o["inc_ord"] = run_tot[o["sem"]]
        pre_tot = {}
        run2 = dict(self.cnt)
        pref = []
        for i, o in enumerate(ops):
            pref.append(dict(run2))
            if o["need_inc"]:
                run2[o["sem"]] = run2.get(o["sem"], 0) + 1
        waits = [[] for _ in ops]
        for i, o in enumerate(ops):
            st = o["stream"]
            need = {}
            if self.bar_vals is not None and st not in self.bar_seen:
                self.bar_seen.add(st)
                for pe, v in self.bar_vals.items():
                    if v > 0:
                        need[pe] = v
            for dd in o["deps"]:
                po = ops[dd]
                assert po["inc_ord"] is not None, (po["tag"], o["tag"])
                if self.sem_policy.get(po["sem"], "ord") == "total":
                    v = pref[i].get(po["sem"], 0) * po["step"]
                    v = max(v, po["inc_ord"] * po["step"])
                else:
                    v = po["inc_ord"] * po["step"]
                if v > need.get(po["sem"], 0):
                    need[po["sem"]] = v
            for pe, v in need.items():
                k = (st, pe)
                if v > self.hw.get(k, 0):
                    self.hw[k] = v
                    waits[i].append((pe, v))
        per_stream = {"t": [], "v": [], "s": [], "g": [], "y": []}
        for i, o in enumerate(ops):
            per_stream[o["stream"]].append(i)
        sems = self.sems
        self.cnt = run_tot
        fin_vals = {nm: self.cnt.get(nm, 0) * (16 if self.sem_policy[nm] != "ord"
                    or nm not in self.ENGINES else 1) for nm in self.sems}
        fin_vals = {nm: self.cnt.get(nm, 0) * (1 if nm in self.ENGINES else 16)
                    for nm in self.sems}

        def run_stream(engh, stream):
            for i in per_stream[stream]:
                o = ops[i]
                for (pe, v) in waits[i]:
                    engh.wait_ge(sems[pe], v)
                if o.get("pre_drain"):
                    engh.drain()
                instr = o["emit"](engh)
                if o["need_inc"]:
                    if o["eng"] in ("v", "s", "g"):
                        engh.drain().then_inc(sems[o["sem"]],
                                                          o["step"])
                    else:
                        assert instr is not None, o["tag"]
                        instr.then_inc(sems[o["sem"]], o["step"])
            if final:
                for pe, v in fin_vals.items():
                    k = (stream, pe)
                    if v > self.hw.get(k, 0) and v > 0:
                        engh.wait_ge(sems[pe], v)

        @block.sync
        def _(eng):
            run_stream(eng, "y")

        @block.tensor
        def _(eng):
            run_stream(eng, "t")

        @block.vector
        def _(eng):
            run_stream(eng, "v")

        @block.scalar
        def _(eng):
            run_stream(eng, "s")

        @block.gpsimd
        def _(eng):
            run_stream(eng, "g")

        self.bar_vals = fin_vals
        self.bar_seen = set()
        self.pending = []
        self.last_writer = {}
        self.readers = {}


def build_nc(debug=False, n_ct=NT_Q):
    nc = bass.Bass("TRN2", target_bir_lowering=False, detect_race_conditions=False)

    d = {}
    for name, shape, dt in [
            ("x_full", [S, H], F32), ("x_own", [HALF, H], F32),
            ("wqT", [H, H], F32), ("wkT", [H, H], F32), ("wvT", [H, H], F32),
            ("bq2", [1, H], F32), ("bk2", [1, H], F32), ("bv2", [1, H], F32),
            ("w1T", [H, FF], F32), ("w2T", [FF, H], F32),
            ("b1r", [4, FF // 4], F32), ("b2r", [4, H // 4], F32),
            ("gamma_bc", [128, H], F32), ("beta_bc", [128, H], F32),
            ("sck_hi", [S, 512], F32),
            ("scq_hi", [HALF, 512], F32),
            ("ident_b", [128, 128], BF16), ("ident_f", [128, 128], F32)]:
        d[name] = nc.dram_tensor(name, shape, dt, kind="ExternalInput").ap()
    out_d = nc.dram_tensor("out", [HALF, H], F32, kind="ExternalOutput").ap()
    dbg = {}
    if debug:
        for nm, sh, dt_ in [("d_ssb", [128, S], F32), ("d_qro", [128, H], BF16),
                            ("d_lp2", [128, 8], F32),
                            ("d_xnC", [128, H], F32), ("d_ln2b", [128, H], F32),
                            ("d_z", [128, 1024], F32), ("d_nxq", [128, H], BF16),
                            ("d_cstats", [128, 64], F32),
                            ("d_xin0", [128, H], F32), ("d_xn0", [128, H], BF16),
                            ("d_kro0", [128, H], BF16), ("d_ps0", [128, 512], F32),
                            ("d_big0", [128, 8 * S], BF16), ("d_big1", [128, NT_K * H], BF16),
                            ("d_knorm2", [128, NT_K], F32), ("d_stats", [128, 64], F32),
                            ("d_partial", [HALF, H], F32), ("d_lparts", [128, 8], F32),
                            ("d_wq_mix", [128, 8 * H], BF16), ("d_hbuf", [128, FF], F32),
                            ("d_xnfmA", [128, H], BF16), ("d_w1q", [128, 8 * FF], BF16)]:
            dbg[nm] = nc.dram_tensor(nm, sh, dt_, kind="ExternalOutput").ap()
    part_d = nc.dram_tensor("partial", [HALF, H], F32).ap()
    w1q_d = nc.dram_tensor("w1q_sp", [128, 8 * FF], BF16).ap()
    w2q_d = nc.dram_tensor("w2q_sp", [128, 32 * H], BF16).ap()
    w1v = d["w1T"].rearrange("(a p) n -> p a n", p=128)
    w2v = d["w2T"].rearrange("(a p) n -> p a n", p=128)

    es = contextlib.ExitStack()
    with es:
        sch = Sched(nc, es)
        for nm, pol in (("ys", "total"), ("yr", "total"), ("bm", "total"),
                        ("gm", "total")):
            sch.dma_sem(nm, pol)

        def sb(stk, name, shape, dt):
            return stk.enter_context(nc.sbuf_tensor(name, shape, dt))

        big0 = sb(es, "big0", [128, 8 * S], BF16)
        big1 = sb(es, "big1", [128, NT_K * H], BF16)
        identb = sb(es, "identb", [128, 128], BF16)
        identf = sb(es, "identf", [128, 128], F32)
        onesf = sb(es, "onesf", [128, 128], F32)
        ones2 = sb(es, "ones2", [2, 128], BF16)
        stats = sb(es, "stats", [128, 64], F32)
        st2 = sb(es, "st2", [128, 16], F32)
        knorm2 = sb(es, "knorm2", [128, NT_K], F32)
        lparts = sb(es, "lparts", [128, 8], F32)
        wsums = sb(es, "wsums", [128, 68], F32)
        rowB = sb(es, "rowB", [1, 128], F32)
        ps_ = [es.enter_context(nc.psum_tensor(f"ps{i}", [128, 512], F32))
               for i in range(4)]
        psT = [es.enter_context(nc.psum_tensor(f"psT{i}", [128, 1024], BF16))
               for i in range(4)]

        C_KMAX2, C_MW1, C_MW2, C_SW1, C_SW2 = 16, 17, 18, 19, 20

        def stc(i):
            return stats[:, i:i + 1]

        y_dma = lambda dst, src, reads, writes, tag="", sem="ys": sch.op(
            "y", lambda e, a=dst, b=src: e.dma_start(out=a, in_=b), reads, writes,
            tag, sem=sem)
        g_dma = lambda dst, src, reads, writes, tag="", sem="gm": sch.op(
            "gd", lambda e, a=dst, b=src: e.dma_start(out=a, in_=b), reads, writes,
            tag, sem=sem)
        v_op = lambda fn, reads, writes, tag="": sch.op("v", fn, reads, writes, tag)
        s_op = lambda fn, reads, writes, tag="": sch.op("s", fn, reads, writes, tag)
        t_op = lambda fn, reads, writes, tag="": sch.op("t", fn, reads, writes, tag)
        g_op = lambda fn, reads, writes, tag="": sch.op("g", fn, reads, writes, tag)

        def emit_ln(xin, xtok, xn, xntok, scratch, sctok, c0, xnsum_col=None,
                    pre=""):
            tk = [f"stat{c0 + j}" for j in range(7)]
            v_op(lambda e: e.reduce_sum(stc(c0), xin[:], axis=AX.X),
                 [xtok], [tk[0]], pre + "ln.sum")
            v_op(lambda e: e.tensor_scalar(out=stc(c0 + 1), in0=stc(c0),
                                           scalar1=-1.0 / H, scalar2=None,
                                           op0=ALU.mult),
                 [tk[0]], [tk[1]], pre + "ln.neg")
            s_op(lambda e: e.activation(out=scratch[:], in_=xin[:], func=AF.Square,
                                        bias=stc(c0 + 1), scale=1.0,
                                        accum_out=stc(c0 + 2)),
                 [xtok, tk[1]], [sctok, tk[2]], pre + "ln.var")
            v_op(lambda e: e.tensor_scalar(out=stc(c0 + 3), in0=stc(c0 + 2),
                                           scalar1=1.0 / H, scalar2=EPS,
                                           op0=ALU.mult, op1=ALU.add),
                 [tk[2]], [tk[3]], pre + "ln.vps")
            v_op(lambda e: e.reciprocal(stc(c0 + 4), stc(c0 + 3)),
                 [tk[3]], [tk[4]], pre + "ln.rin")
            s_op(lambda e: e.activation(out=stc(c0 + 5), in_=stc(c0 + 4),
                                        func=AF.Sqrt),
                 [tk[4]], [tk[5]], pre + "ln.rsd")
            v_op(lambda e: e.tensor_tensor(out=stc(c0 + 6), in0=stc(c0 + 1),
                                           in1=stc(c0 + 5), op=ALU.mult),
                 [tk[1], tk[5]], [tk[6]], pre + "ln.nmr")
            acc = None if xnsum_col is None else stc(xnsum_col)
            if acc is None:
                s_op(lambda e: e.activation(out=xn[:], in_=xin[:], func=AF.Identity,
                                            scale=stc(c0 + 5), bias=stc(c0 + 6)),
                     [xtok, tk[5], tk[6]], [xntok], pre + "ln.xn")
            else:
                s_op(lambda e: e.activation(out=xn[:], in_=xin[:], func=AF.Identity,
                                            scale=stc(c0 + 5), bias=stc(c0 + 6),
                                            accum_out=acc),
                     [xtok, tk[5], tk[6]], [xntok, "xnsum7"], pre + "ln.xn")

        def emit_tr(src, stok, nchunks, dst, dtok, colf, tag, src_colf=None,
                    per_chunk=False, ev_alt=False):
            for j in range(nchunks):
                bi = j % 4
                sc = src_colf(j) if src_colf else j * 128
                t_op(lambda e, bi=bi, sc=sc: e.transpose(
                        psT[bi][:, 0:128], src[:, sc:sc + 128], identb[:]),
                     [stok, "identb"], [f"psT{bi}"], f"{tag}.T{j}")
                c0 = colf(j)
                wtok = f"{dtok}#{c0 // 128}" if per_chunk else dtok
                if ev_alt and j % 2 == 1:
                    v_op(lambda e, bi=bi, c0=c0: e.tensor_copy(
                            dst[:, c0:c0 + 128], psT[bi][:, 0:128]),
                         [f"psT{bi}"], [wtok], f"{tag}.E{j}")
                else:
                    s_op(lambda e, bi=bi, c0=c0: e.copy(dst[:, c0:c0 + 128],
                                                        psT[bi][:, 0:128]),
                         [f"psT{bi}"], [wtok], f"{tag}.E{j}")

        es_ab = contextlib.ExitStack()
        with es_ab:
            wq_mix = sb(es_ab, "wq_mix", [128, 8 * H], BF16)
            bq_rows = sb(es_ab, "bq_rows", [2, H], BF16)

            es_a = contextlib.ExitStack()
            with es_a:
                wk_mix = sb(es_a, "wk_mix", [128, 8 * H], BF16)
                wv_bf = sb(es_a, "wv_bf", [128, 8 * H], BF16)
                xinA = [sb(es_a, f"xinA{i}", [128, H], F32) for i in range(2)]
                schi = sb(es_a, "schi", [128, 512], F32)
                xnA = [sb(es_a, f"xnA{i}", [128, H], BF16)
                       for i in range(2)]
                xnfmA = sb(es_a, "xnfmA", [128, H], BF16)
                kro = sb(es_a, "kro", [128, H], BF16)
                wraw = sb(es_a, "wraw", [128, H], F32)
                brow = wraw[0:1, :]
                bmixf2 = xinA[0][0:1, :]
                brow2 = xnA[0][0:1, :]
                bk_rows = sb(es_a, "bk_rows", [2, H], BF16)
                bv_rows = sb(es_a, "bv_rows", [2, H], BF16)

                blkA = es_a.enter_context(nc.Block("phaseA"))

                y_dma(identb[:], d["ident_b"], [], ["identb"], "ld.idb")
                y_dma(identf[:], d["ident_f"], [], ["identf"], "ld.idf")
                g_op(lambda e: e.memset(onesf[:], 1.0), [], ["onesf"], "ms.1f")
                if debug:
                    g_op(lambda e: e.memset(stats[:], 0.0), [], [], "ms.dbg1")
                    g_op(lambda e: e.memset(st2[:], 0.0), [], [], "ms.dbg2")
                    g_op(lambda e: e.memset(wsums[:], 0.0), [], [], "ms.dbg3")
                    g_op(lambda e: e.memset(lparts[:], 0.0), [], [], "ms.dbg4")
                    g_op(lambda e: e.memset(knorm2[:], 0.0), [], [], "ms.dbg5")
                g_op(lambda e: e.memset(ones2[:], 1.0), [], ["ones2"], "ms.12")

                def emit_wprep(src, dst, dtok, mixed, tag):
                    for dj in range(8):
                        y_dma(wraw[:], src[dj * 128:(dj + 1) * 128, :],
                              ["wraw"], ["wraw"], f"{tag}.ld{dj}", sem="yw")
                        if mixed:
                            v_op(lambda e, dj=dj: e.tensor_tensor(
                                    out=dst[:, dj * H:dj * H + 512],
                                    in0=wraw[:, 0:512], in1=wraw[:, 512:H],
                                    op=ALU.subtract),
                                 ["wraw"], [dtok], f"{tag}.s{dj}")
                            v_op(lambda e, dj=dj: e.tensor_tensor(
                                    out=dst[:, dj * H + 512:(dj + 1) * H],
                                    in0=wraw[:, 0:512], in1=wraw[:, 512:H],
                                    op=ALU.add),
                                 ["wraw"], [dtok], f"{tag}.a{dj}")
                        else:
                            v_op(lambda e, dj=dj: e.tensor_copy(
                                    dst[:, dj * H:(dj + 1) * H], wraw[:]),
                                 ["wraw"], [dtok], f"{tag}.c{dj}")

                def emit_brows(src, rows, rtok, mixed, tag):
                    y_dma(brow, src, ["wraw"], ["wraw"], f"{tag}.ld", sem="yw")
                    if mixed:
                        v_op(lambda e: e.tensor_tensor(out=bmixf2[:, 0:512],
                                                       in0=brow[:, 0:512],
                                                       in1=brow[:, 512:H],
                                                       op=ALU.subtract),
                             ["wraw"], ["xinA0"], f"{tag}.s")
                        v_op(lambda e: e.tensor_tensor(out=bmixf2[:, 512:H],
                                                       in0=brow[:, 0:512],
                                                       in1=brow[:, 512:H],
                                                       op=ALU.add),
                             ["wraw"], ["xinA0"], f"{tag}.a")
                        srcb, srct = bmixf2, "xinA0"
                    else:
                        srcb, srct = brow, "wraw"
                    v_op(lambda e, srcb=srcb: e.tensor_copy(rows[0:1, :], srcb),
                         [srct], [rtok], f"{tag}.hi")
                    v_op(lambda e, srcb=srcb: e.scalar_tensor_tensor(
                            out=brow2, in0=srcb, scalar=1.0,
                            in1=rows[0:1, :], op0=ALU.mult, op1=ALU.subtract),
                         [srct, rtok], ["xnA0"], f"{tag}.lo")
                    g_dma(rows[1:2, :], brow2, ["xnA0"], [rtok], f"{tag}.mv",
                          sem="bm")

                emit_wprep(d["wkT"], wk_mix, "wk_mix", True, "wk")
                emit_wprep(d["wvT"], wv_bf, "wv_bf", False, "wv")
                emit_wprep(d["wqT"], wq_mix, "wq_mix", True, "wq")
                emit_brows(d["bk2"][:], bk_rows, "bk_rows", True, "bk")
                emit_brows(d["bv2"][:], bv_rows, "bv_rows", False, "bv")
                emit_brows(d["bq2"][:], bq_rows, "bq_rows", True, "bq")

                def a_xload(kt):
                    xin, xtok = xinA[kt % 2], f"xinA{kt % 2}"
                    y_dma(xin[:], d["x_full"][kt * 128:(kt + 1) * 128, :],
                          [xtok], [xtok], f"A{kt}.x", sem=f"yx{kt % 2}")

                def a_lnops(kt):
                    xin, xtok = xinA[kt % 2], f"xinA{kt % 2}"
                    emit_ln(xin, xtok, xnA[kt % 2], f"xnA{kt % 2}",
                            wraw[:, 0:H], "wraw", 0, pre=f"A{kt}.")

                def a_xnT(kt):
                    emit_tr(xnA[kt % 2], f"xnA{kt % 2}", 8, xnfmA, "xnfmA",
                            lambda j: j * 128, f"A{kt}.xn", per_chunk=True,
                            ev_alt=True)

                def a_pe(kt):
                    xn, xntok = xnA[kt % 2], f"xnA{kt % 2}"
                    y_dma(schi[:], d["sck_hi"][kt * 128:(kt + 1) * 128, :],
                          ["schi"], ["schi"], f"A{kt}.h", sem="ysh")
                    if debug and kt == 0:
                        g_dma(dbg["d_xin0"], xinA[0][:], ["xinA0"], [], "dbg.xin0")
                        g_dma(dbg["d_xn0"], xn[:], [xntok], [], "dbg.xn0")
                    for wi, (bks, wsb, wtok, btok, brows) in enumerate((
                            ((0, 1), wk_mix, "wk_mix", "bk_rows", bk_rows),
                            ((2, 3), wv_bf, "wv_bf", "bv_rows", bv_rows))):
                        for half, bk_i in enumerate(bks):
                            for dj in range(8):
                                t_op(lambda e, dj=dj, bk_i=bk_i, wsb=wsb, half=half:
                                     e.matmul(ps_[bk_i][:],
                                              xnfmA[:, dj * 128:(dj + 1) * 128],
                                              wsb[:, dj * H + half * 512:
                                                  dj * H + half * 512 + 512],
                                              start=(dj == 0), stop=False),
                                     [f"xnfmA#{dj}", wtok], [f"ps{bk_i}"],
                                     f"A{kt}.mm{wtok}{half}.{dj}")
                            t_op(lambda e, bk_i=bk_i, brows=brows, half=half:
                                 e.matmul(ps_[bk_i][:], ones2[:],
                                          brows[:, half * 512:half * 512 + 512],
                                          start=False, stop=True),
                                 ["ones2", btok], [f"ps{bk_i}"],
                                 f"A{kt}.b{wtok}{half}")
                        if wi == 0:
                            if kt + 2 < NT_K:
                                a_xload(kt + 2)
                            if kt + 1 < NT_K:
                                a_lnops(kt + 1)
                    v_op(lambda e: e.tensor_tensor(out=kro[:, 0:512], in0=ps_[0][:],
                                                   in1=schi[:], op=ALU.mult),
                         ["ps0", "schi"], ["kro"], f"A{kt}.r0")
                    v_op(lambda e: e.tensor_tensor(out=kro[:, 512:H], in0=ps_[1][:],
                                                   in1=schi[:], op=ALU.mult),
                         ["ps1", "schi"], ["kro"], f"A{kt}.r1")
                    v_op(lambda e, kt=kt: e.tensor_copy(
                            big1[:, kt * H:kt * H + 512], ps_[2][:]),
                         ["ps2"], ["big1"], f"A{kt}.v0")
                    v_op(lambda e, kt=kt: e.tensor_copy(
                            big1[:, kt * H + 512:(kt + 1) * H], ps_[3][:]),
                         ["ps3"], ["big1"], f"A{kt}.v1")
                    if debug and kt == 0:
                        g_dma(dbg["d_kro0"], kro[:], ["kro"], [], "dbg.kro0")
                        g_dma(dbg["d_ps0"], None, [], [], "")
                        sch.pending.pop()
                    if kt + 1 < NT_K:
                        a_xnT(kt + 1)
                    emit_tr(kro, "kro", 8, big0, "big0",
                            lambda j, kt=kt: j * S + kt * 128, f"A{kt}.k",
                            ev_alt=True)

                a_xload(0)
                a_xload(1)
                a_lnops(0)
                a_xnT(0)
                for kt in range(NT_K):
                    a_pe(kt)

                sch.finalize_phase(blkA)

            es_b = contextlib.ExitStack()
            with es_b:
                xinB = [sb(es_b, f"xinB{i}", [128, H], F32) for i in range(2)]
                schiB = sb(es_b, "schiB", [128, 512], F32)
                xnB = sb(es_b, "xnB", [128, H], BF16)
                xnfmB = sb(es_b, "xnfmB", [128, H], BF16)
                qro = sb(es_b, "qro", [128, H], BF16)
                qfm = [sb(es_b, f"qfm{i}", [128, H], BF16) for i in range(2)]
                probs = [sb(es_b, f"probs{i}", [128, 512], BF16) for i in range(4)]
                pT = [sb(es_b, f"pT{i}", [128, 128], BF16) for i in range(4)]
                outB = [sb(es_b, "outB0", [128, H], F32)] * 2
                wchd = [sb(es_b, f"wch{i}", [128, 1024], F32) for i in range(2)]
                ssb = sb(es_b, "ssb", [128, S], F32)
                wst = [sb(es_b, "wst0", [128, 1024], BF16)] * 2
                wm1 = sb(es_b, "wm1", [128, 1024], BF16)
                wm2 = sb(es_b, "wm2", [128, 1024], BF16)

                blkB = es_b.enter_context(nc.Block("phaseB"))

                def emit_abs_chunk(src, col, width, tag, p):
                    w, wt = wchd[p], f"wch{p}"
                    g_dma(w[:, 0:width], src, [wt], [wt], f"{tag}.ld",
                          sem=f"gw{p}")
                    v_op(lambda e, w=w, col=col, width=width: e.reduce_sum(
                            wsums[:, col:col + 1], w[:, 0:width], axis=AX.X,
                            apply_absolute_value=True),
                         [wt], [f"ws{col}"], f"{tag}.abs")

                def emit_scales(first_col, n_col, tot_col, denom, out_col, tag):
                    v_op(lambda e, first_col=first_col, n_col=n_col, tot_col=tot_col:
                         e.reduce_sum(st2[:, tot_col:tot_col + 1],
                                      wsums[:, first_col:first_col + n_col],
                                      axis=AX.X),
                         [f"ws{c}" for c in range(first_col, first_col + n_col)],
                         [f"tot{tag}"], f"{tag}.tot")
                    t_op(lambda e, tot_col=tot_col: e.matmul(
                            ps_[2][:, 0:1], onesf[:, 0:128],
                            st2[:, tot_col:tot_col + 1], start=True, stop=True),
                         [f"tot{tag}", "onesf"], ["ps2"], f"{tag}.bc")
                    v_op(lambda e, denom=denom, out_col=out_col: e.tensor_scalar(
                            out=stc(out_col), in0=ps_[2][:, 0:1],
                            scalar1=1.0 / denom, scalar2=EPS,
                            op0=ALU.mult, op1=ALU.max),
                         ["ps2"], [f"mwc{tag}"], f"{tag}.mwc")
                    v_op(lambda e, out_col=out_col: e.reciprocal(
                            stc(out_col + 2), stc(out_col)),
                         [f"mwc{tag}"], [f"sw{tag}"], f"{tag}.sw")

                def emit_wq_load(src, parity, tag):
                    w, wt = wchd[parity], f"wch{parity}"
                    g_dma(w[:, 0:1024], src, [wt], [wt], f"{tag}.ld",
                          sem=f"gw{parity}")

                def emit_wquant_chunk(src, dst_dram, parity, mwtag, tag):
                    w, wt = wchd[parity], f"wch{parity}"
                    ws, wst_t = wst[parity], "wst0"
                    v_op(lambda e, w=w, mwtag=mwtag: e.tensor_scalar(
                            out=wm1[:], in0=w[:, 0:1024],
                            scalar1=stc(C_SW1 if mwtag == "w1" else C_SW2),
                            scalar2=0.5, op0=ALU.mult, op1=ALU.is_ge),
                         [wt, f"sw{mwtag}"], ["wm1"], f"{tag}.m1")
                    v_op(lambda e, w=w, mwtag=mwtag: e.tensor_scalar(
                            out=wm2[:], in0=w[:, 0:1024],
                            scalar1=stc(C_SW1 if mwtag == "w1" else C_SW2),
                            scalar2=-0.5, op0=ALU.mult, op1=ALU.is_le),
                         [wt, f"sw{mwtag}"], ["wm2"], f"{tag}.m2")
                    v_op(lambda e, ws=ws: e.tensor_tensor(
                            out=ws[:], in0=wm2[:], in1=wm1[:],
                            op=ALU.subtract),
                         ["wm1", "wm2"], [wst_t], f"{tag}.nq")
                    g_dma(dst_dram, ws[:], [wst_t],
                          [wst_t], f"{tag}.sp", sem=f"gs{parity}")

                abs_jobs = []
                for a in range(8):
                    for q4 in range(4):
                        abs_jobs.append((w1v[:, a, q4 * 1024:(q4 + 1) * 1024],
                                         a * 4 + q4, 1024, f"w1a{a}.{q4}"))
                for a in range(32):
                    abs_jobs.append((w2v[:, a, :], 32 + a, 1024, f"w2a{a}"))

                quant_jobs = []
                for a in range(8):
                    for q4 in range(4):
                        quant_jobs.append((w1v[:, a, q4 * 1024:(q4 + 1) * 1024],
                                           w1q_d[:, (a * 4 + q4) * 1024:
                                                 (a * 4 + q4 + 1) * 1024],
                                           "w1", f"qw1.{a}.{q4}"))
                for a in range(32):
                    quant_jobs.append((w2v[:, a, :],
                                       w2q_d[:, a * 1024:(a + 1) * 1024],
                                       "w2", f"qw2.{a}"))

                abs_i = [0]
                quant_i = [0]
                quant_ld_i = [0]

                def drip_weight_work(n_abs, n_quant):
                    for _ in range(n_abs):
                        if abs_i[0] < len(abs_jobs):
                            srcv, col, width, tag = abs_jobs[abs_i[0]]
                            emit_abs_chunk(srcv, col, width, tag, abs_i[0] % 2)
                            abs_i[0] += 1
                            if abs_i[0] == 32:
                                emit_scales(0, 32, 1, float(H) * FF, C_MW1, "w1")
                            if abs_i[0] == 64:
                                emit_scales(32, 32, 2, float(H) * FF, C_MW2, "w2")
                    for _ in range(n_quant):
                        if abs_i[0] >= 64 and quant_i[0] < len(quant_jobs):
                            i = quant_i[0]
                            while quant_ld_i[0] <= min(i + 1,
                                                       len(quant_jobs) - 1):
                                srcv, dst, mwtag, tag = \
                                    quant_jobs[quant_ld_i[0]]
                                emit_wq_load(srcv, quant_ld_i[0] % 2, tag)
                                quant_ld_i[0] += 1
                            srcv, dst, mwtag, tag = quant_jobs[i]
                            emit_wquant_chunk(srcv, dst, i % 2, mwtag, tag)
                            quant_i[0] += 1

                def stage1_ln(qt):
                    xin, xtok = xinB[qt % 2], f"xinB{qt % 2}"
                    y_dma(xin[:], d["x_own"][qt * 128:(qt + 1) * 128, :],
                          [xtok], [xtok], f"B{qt}.x", sem=f"yb{qt % 2}")
                    y_dma(schiB[:], d["scq_hi"][qt * 128:(qt + 1) * 128, :],
                          ["schiB"], ["schiB"], f"B{qt}.h", sem="ybh")
                    emit_ln(xin, xtok, xnB, "xnB", qro, "qro", 0, pre=f"B{qt}.")

                def stage1_pe(qt):
                    qf, qftok = qfm[qt % 2], f"qfm{qt % 2}"
                    emit_tr(xnB, "xnB", 8, xnfmB, "xnfmB", lambda j: j * 128,
                            f"B{qt}.xn", per_chunk=True)
                    for half, bk_i in enumerate((0, 1)):
                        for dj in range(8):
                            t_op(lambda e, dj=dj, bk_i=bk_i, half=half: e.matmul(
                                    ps_[bk_i][:], xnfmB[:, dj * 128:(dj + 1) * 128],
                                    wq_mix[:, dj * H + half * 512:
                                           dj * H + half * 512 + 512],
                                    start=(dj == 0), stop=False),
                                 [f"xnfmB#{dj}", "wq_mix"], [f"ps{bk_i}"],
                                 f"B{qt}.q{half}.{dj}")
                        t_op(lambda e, bk_i=bk_i, half=half: e.matmul(
                                ps_[bk_i][:], ones2[:],
                                bq_rows[:, half * 512:half * 512 + 512],
                                start=False, stop=True),
                             ["ones2", "bq_rows"], [f"ps{bk_i}"], f"B{qt}.qb{half}")
                    v_op(lambda e: e.tensor_tensor(out=qro[:, 0:512], in0=ps_[0][:],
                                                   in1=schiB[:], op=ALU.mult),
                         ["ps0", "schiB"], ["qro"], f"B{qt}.r0")
                    v_op(lambda e: e.tensor_tensor(out=qro[:, 512:H], in0=ps_[1][:],
                                                   in1=schiB[:], op=ALU.mult),
                         ["ps1", "schiB"], ["qro"], f"B{qt}.r1")

                def stage1_pe2(qt):
                    qf, qftok = qfm[qt % 2], f"qfm{qt % 2}"
                    emit_tr(qro, "qro", 8, qf, qftok, lambda j: j * 128,
                            f"B{qt}.q", per_chunk=True)

                def emit_scores(qt, ln_cb=None):
                    qf, qftok = qfm[qt % 2], f"qfm{qt % 2}"
                    for kc in range(8):
                        bk_i = kc % 4
                        for ej in range(8):
                            t_op(lambda e, ej=ej, kc=kc, bk_i=bk_i, qf=qf: e.matmul(
                                    ps_[bk_i][:], qf[:, ej * 128:(ej + 1) * 128],
                                    big0[:, ej * S + kc * 512:ej * S + kc * 512 + 512],
                                    start=(ej == 0), stop=(ej == 7)),
                                 [f"{qftok}#{ej}", "big0"], [f"ps{bk_i}"],
                                 f"B{qt}.s{kc}.{ej}")
                        s_op(lambda e, kc=kc, bk_i=bk_i: e.copy(
                                ssb[:, kc * 512:(kc + 1) * 512], ps_[bk_i][:]),
                             [f"ps{bk_i}"], [f"ssb{kc}"], f"B{qt}.sev{kc}")
                        v_op(lambda e, kc=kc: e.reduce_max(
                                st2[:, 8 + kc:9 + kc],
                                ssb[:, kc * 512:(kc + 1) * 512], axis=AX.X),
                             [f"ssb{kc}"], [f"smx{kc}"], f"B{qt}.smx{kc}")
                        if kc == 2 and ln_cb is not None:
                            ln_cb()
                        drip_weight_work(1, 0)

                def stage2(qt):
                    xin, xtok = xinB[qt % 2], f"xinB{qt % 2}"
                    v_op(lambda e: e.reduce_max(stc(10), st2[:, 8:16],
                                                axis=AX.X),
                         [f"smx{kc}" for kc in range(8)], ["qM"],
                         f"B{qt}.rmax")
                    s_op(lambda e: e.activation(out=stc(11), in_=stc(10),
                                                func=AF.Identity,
                                                scale=-INV_SCALE),
                         ["qM"], ["ebias"], f"B{qt}.eb")
                    for kc in range(8):
                        bk_i = kc % 4
                        s_op(lambda e, kc=kc, bk_i=bk_i: e.activation(
                                out=probs[bk_i][:],
                                in_=ssb[:, kc * 512:(kc + 1) * 512], func=AF.Exp,
                                scale=INV_SCALE, bias=stc(11),
                                accum_out=lparts[:, kc:kc + 1]),
                             [f"ssb{kc}", "ebias"], [f"probs{bk_i}", "lparts"],
                             f"B{qt}.e{kc}")
                        for sub in range(4):
                            t_op(lambda e, sub=sub, bk_i=bk_i: e.transpose(
                                    psT[sub][:, 0:128],
                                    probs[bk_i][:, sub * 128:(sub + 1) * 128],
                                    identb[:]),
                                 [f"probs{bk_i}", "identb"], [f"psT{sub}"],
                                 f"B{qt}.pT{kc}.{sub}")
                            s_op(lambda e, sub=sub: e.copy(
                                    pT[sub][:], psT[sub][:, 0:128]),
                                 [f"psT{sub}"], [f"pT{sub}"],
                                 f"B{qt}.pE{kc}.{sub}")
                        for sub in range(4):
                            kt64 = kc * 4 + sub
                            for half, bk2 in enumerate((2, 3)):
                                t_op(lambda e, sub=sub, kt64=kt64, half=half,
                                     bk2=bk2, first=(kt64 == 0):
                                     e.matmul(ps_[bk2][:], pT[sub][:],
                                              big1[:, kt64 * H + half * 512:
                                                   kt64 * H + half * 512 + 512],
                                              start=first, stop=(kt64 == 31)),
                                     [f"pT{sub}", "big1"], [f"ps{bk2}"],
                                     f"B{qt}.pv{kc}.{sub}.{half}")
                        drip_weight_work(0, 1)
                    v_op(lambda e: e.reduce_sum(stc(12), lparts[:, 0:8], axis=AX.X),
                         ["lparts"], ["lsum"], f"B{qt}.l")
                    v_op(lambda e: e.reciprocal(stc(13), stc(12)),
                         ["lsum"], ["rl"], f"B{qt}.rl")
                    ob, obt = outB[0], "outB0"
                    v_op(lambda e, ob=ob, xin=xin: e.scalar_tensor_tensor(
                            out=ob[:, 0:512], in0=ps_[2][:], scalar=stc(13),
                            in1=xin[:, 0:512], op0=ALU.mult, op1=ALU.add),
                         ["ps2", "rl", xtok], [obt], f"B{qt}.o0")
                    v_op(lambda e, ob=ob, xin=xin: e.scalar_tensor_tensor(
                            out=ob[:, 512:H], in0=ps_[3][:], scalar=stc(13),
                            in1=xin[:, 512:H], op0=ALU.mult, op1=ALU.add),
                         ["ps3", "rl", xtok], [obt], f"B{qt}.o1")
                    g_dma(part_d[qt * 128:(qt + 1) * 128, :], ob[:], [obt], [obt],
                          f"B{qt}.st", sem=f"go{qt % 2}")

                stage1_ln(0)
                stage1_pe(0)
                stage1_pe2(0)
                for qt in range(NT_Q):
                    emit_scores(qt, ln_cb=(
                        (lambda q=qt: stage1_ln(q + 1))
                        if qt + 1 < NT_Q else None))
                    if qt + 1 < NT_Q:
                        stage1_pe(qt + 1)
                    stage2(qt)
                    if qt + 1 < NT_Q:
                        stage1_pe2(qt + 1)
                drip_weight_work(64, 64)

                if debug:
                    g_dma(dbg["d_ssb"], ssb[:], ["ssb"], [], "dbgBs")
                    g_dma(dbg["d_qro"], qro[:], ["qro"], [], "dbgBq")
                    g_dma(dbg["d_lp2"], lparts[:], ["lparts"], [], "dbgBl")
                    g_dma(dbg["d_stats"], stats[:], ["ebias"], [], "dbgB0")
                    g_dma(dbg["d_lparts"], lparts[:], ["lparts"], [], "dbgB1")
                    g_dma(dbg["d_w1q"], w1q_d[:], [], [], "dbgB2")
                sch.finalize_phase(blkB)

        es_c = contextlib.ExitStack()
        with es_c:
            gamma = sb(es_c, "gamma", [128, H], F32)
            beta = sb(es_c, "beta", [128, H], F32)
            xinC = [sb(es_c, f"xinC{i}", [128, H], F32) for i in range(3)]
            xpart = xinC
            xnC = sb(es_c, "xnC", [128, H], F32)
            ln2b = sb(es_c, "ln2b", [128, H], F32)
            zbuf = sb(es_c, "zbuf", [128, 1024], F32)
            cibuf = sb(es_c, "cibuf", [128, 1024], I32)
            m1b = sb(es_c, "m1b", [128, 1024], BF16)
            nxq = sb(es_c, "nxq", [128, H], BF16)
            xqfm = sb(es_c, "xqfm", [128, H], BF16)
            hbuf = sb(es_c, "hbuf", [128, FF], F32)
            nhqd = [sb(es_c, f"nhq{i}", [128, 1024], BF16) for i in range(2)]
            hqfm = sb(es_c, "hqfm", [128, 2048], BF16)
            b1rows = sb(es_c, "b1rows", [4, FF], BF16)
            b2rows = sb(es_c, "b2rows", [4, H], BF16)
            blhs1 = sb(es_c, "blhs1", [4, 128], BF16)
            blhs2 = sb(es_c, "blhs2", [4, 128], BF16)
            stg4 = zbuf[0:4, :]
            stg4h = m1b[0:4, :]
            stg4l = nhqd[0][0:4, :]

            blkC = es_c.enter_context(nc.Block("phaseC"))

            for j in range(4):
                y_dma(big0[:, j * 8192:(j + 1) * 8192],
                      w1q_d[:, j * 8192:(j + 1) * 8192],
                      ["big0"], ["big0"], f"C.w1q{j}", sem="yr")
            for j in range(4):
                y_dma(big1[:, j * 8192:(j + 1) * 8192],
                      w2q_d[:, j * 8192:(j + 1) * 8192],
                      ["big1"], ["big1"], f"C.w2q{j}", sem="yr")
            y_dma(gamma[:], d["gamma_bc"], [], ["gamma"], "C.g", sem="yr")
            y_dma(beta[:], d["beta_bc"], [], ["beta"], "C.b", sem="yr")

            def emit_bsplit(src, nq, rows, rtok, tag):
                w = nq
                y_dma(stg4[:, 0:w], src, ["zbuf"], ["zbuf"], f"{tag}.ld", sem="yr")
                v_op(lambda e, w=w: e.tensor_copy(stg4h[:, 0:w], stg4[:, 0:w]),
                     ["zbuf"], ["m1b"], f"{tag}.hi")
                v_op(lambda e, w=w: e.scalar_tensor_tensor(
                        out=stg4l[:, 0:w], in0=stg4[:, 0:w], scalar=1.0,
                        in1=stg4h[:, 0:w], op0=ALU.mult, op1=ALU.subtract),
                     ["zbuf", "m1b"], ["nhq0"], f"{tag}.lo")
                full = 4 * w
                for r, srct, srcb in ((0, "m1b", stg4h), (1, "m1b", stg4h),
                                      (2, "nhq0", stg4l), (3, "nhq0", stg4l)):
                    for p in range(4):
                        g_dma(rows[r:r + 1, p * w:(p + 1) * w],
                              srcb[p:p + 1, 0:w],
                              [srct], [rtok], f"{tag}.mv{r}.{p}")

            emit_bsplit(d["b1r"], FF // 4, b1rows, "b1rows", "b1")
            emit_bsplit(d["b2r"], H // 4, b2rows, "b2rows", "b2")

            CQ = 21

            st2b = sb(es_c, "st2b", [128, 8], BF16)

            def emit_aquant(src, stoks, sx_tok, sx_col, width, off, nxq_out,
                            nxq_tok, tag):
                sl = slice(off, off + width)
                v_op(lambda e: e.scalar_tensor_tensor(
                        out=zbuf[:, 0:width], in0=src[:, sl],
                        scalar=stc(sx_col), in1=src[:, sl],
                        op0=ALU.mult, op1=ALU.bypass),
                     list(stoks) + [sx_tok], ["zbuf"], f"{tag}.z")
                v_op(lambda e: e.tensor_copy(cibuf[:, 0:width], zbuf[:, 0:width]),
                     ["zbuf"], ["cibuf"], f"{tag}.ci")
                v_op(lambda e: e.scalar_tensor_tensor(
                        out=zbuf[:, 0:width], in0=zbuf[:, 0:width], scalar=1.0,
                        in1=cibuf[:, 0:width], op0=ALU.mult, op1=ALU.subtract),
                     ["zbuf", "cibuf"], ["zbuf"], f"{tag}.d")
                v_op(lambda e: e.tensor_scalar(
                        out=m1b[:, 0:width], in0=zbuf[:, 0:width],
                        scalar1=0.5, scalar2=None, op0=ALU.is_ge),
                     ["zbuf"], ["m1b"], f"{tag}.m1")
                v_op(lambda e: e.scalar_tensor_tensor(
                        out=m1b[:, 0:width], in0=zbuf[:, 0:width], scalar=-0.5,
                        in1=m1b[:, 0:width], op0=ALU.is_le, op1=ALU.subtract),
                     ["zbuf", "m1b"], ["m1b"], f"{tag}.mc")
                v_op(lambda e: e.scalar_tensor_tensor(
                        out=nxq_out, in0=cibuf[:, 0:width], scalar=-1.0,
                        in1=m1b[:, 0:width], op0=ALU.mult, op1=ALU.add),
                     ["cibuf", "m1b"], [nxq_tok], f"{tag}.nx")

            def emit_split(src_col, srctok, c0, pre):
                s_op(lambda e: e.copy(st2b[:, c0:c0 + 1], stc(src_col)),
                     [srctok], [f"{pre}hb"], f"{pre}.hb")
                s_op(lambda e: e.copy(st2b[:, c0 + 2:c0 + 3],
                                      st2b[:, c0:c0 + 1]),
                     [f"{pre}hb"], [f"{pre}hb2"], f"{pre}.hb2")
                s_op(lambda e: e.copy(st2[:, 5:6], st2b[:, c0:c0 + 1]),
                     [f"{pre}hb"], [f"{pre}hf"], f"{pre}.hf")
                s_op(lambda e: e.activation(out=st2[:, 6:7], in_=st2[:, 5:6],
                                            func=AF.Identity, scale=-1.0,
                                            bias=stc(src_col)),
                     [f"{pre}hf", srctok], [f"{pre}lf"], f"{pre}.lf")
                s_op(lambda e: e.copy(st2b[:, c0 + 1:c0 + 2], st2[:, 6:7]),
                     [f"{pre}lf"], [f"{pre}lb"], f"{pre}.lb")
                s_op(lambda e: e.copy(st2b[:, c0 + 3:c0 + 4], st2[:, 6:7]),
                     [f"{pre}lf"], [f"{pre}lb2"], f"{pre}.lb2")

            def emit_blhs(c0, pre, rows, rtok, tag):
                t_op(lambda e: e.transpose(psT[0][0:4, 0:128],
                                           st2b[:, c0:c0 + 4], identb[:]),
                     [f"{pre}hb", f"{pre}hb2", f"{pre}lb", f"{pre}lb2",
                      "identb"], ["psT0"], f"{tag}.T")
                s_op(lambda e: e.copy(rows[0:4, :], psT[0][0:4, 0:128]),
                     ["psT0"], [rtok], f"{tag}.E")

            def c_front_vs(tt):
                xin, xtok = xinC[tt % 3], f"xinC{tt % 3}"
                y_dma(xin[:], d["x_own"][tt * 128:(tt + 1) * 128, :],
                      [xtok], [xtok], f"C{tt}.x", sem=f"yc{tt % 3}")
                emit_ln(xin, xtok, xnC, "xnC", zbuf[:, 0:H], "zbuf", 0,
                        xnsum_col=7, pre=f"C{tt}.")
                v_op(lambda e: e.tensor_scalar(out=stc(CQ), in0=stc(7),
                                               scalar1=-1.0 / H, scalar2=None,
                                               op0=ALU.mult),
                     ["xnsum7"], ["nmu2"], f"C{tt}.mu2")
                s_op(lambda e: e.activation(out=zbuf[:, 0:H], in_=xnC[:],
                                            func=AF.Square, bias=stc(CQ),
                                            accum_out=stc(CQ + 1)),
                     ["xnC", "nmu2"], ["zbuf", "v2s"], f"C{tt}.v2")
                v_op(lambda e: e.tensor_scalar(out=stc(CQ + 2), in0=stc(CQ + 1),
                                               scalar1=1.0 / H, scalar2=EPS,
                                               op0=ALU.mult, op1=ALU.add),
                     ["v2s"], ["v2p"], f"C{tt}.vp2")
                v_op(lambda e: e.reciprocal(stc(CQ + 3), stc(CQ + 2)),
                     ["v2p"], ["ri2"], f"C{tt}.ri2")
                s_op(lambda e: e.activation(out=stc(CQ + 4), in_=stc(CQ + 3),
                                            func=AF.Sqrt),
                     ["ri2"], ["rs2"], f"C{tt}.rs2")
                v_op(lambda e: e.tensor_tensor(out=stc(CQ + 5), in0=stc(CQ),
                                               in1=stc(CQ + 4), op=ALU.mult),
                     ["nmu2", "rs2"], ["nm2r"], f"C{tt}.nm2r")
                s_op(lambda e: e.activation(out=ln2b[:], in_=xnC[:], func=AF.Identity,
                                            scale=stc(CQ + 4), bias=stc(CQ + 5)),
                     ["xnC", "rs2", "nm2r"], ["ln2b"], f"C{tt}.l2n")
                v_op(lambda e: e.tensor_tensor(out=ln2b[:], in0=ln2b[:],
                                               in1=gamma[:], op=ALU.mult),
                     ["ln2b", "gamma"], ["ln2b"], f"C{tt}.gam")
                v_op(lambda e: e.tensor_tensor(out=ln2b[:], in0=ln2b[:],
                                               in1=beta[:], op=ALU.add),
                     ["ln2b", "beta"], ["ln2b"], f"C{tt}.bet")
                v_op(lambda e: e.reduce_max(stc(CQ + 6), ln2b[:], axis=AX.X,
                                            apply_absolute_value=True),
                     ["ln2b"], ["mx1"], f"C{tt}.mx1")
                v_op(lambda e: e.tensor_scalar(out=stc(CQ + 7), in0=stc(CQ + 6),
                                               scalar1=EPS, scalar2=None,
                                               op0=ALU.max),
                     ["mx1"], ["den1"], f"C{tt}.den1")
                v_op(lambda e: e.reciprocal(stc(CQ + 8), stc(CQ + 7)),
                     ["den1"], ["rd1"], f"C{tt}.rd1")
                v_op(lambda e: e.tensor_scalar(out=stc(CQ + 9), in0=stc(CQ + 8),
                                               scalar1=127.0, scalar2=None,
                                               op0=ALU.mult),
                     ["rd1"], ["sx1"], f"C{tt}.sx1")
                rqc = 40 + (tt % 2)
                v_op(lambda e, rqc=rqc: e.tensor_tensor(
                        out=stc(rqc), in0=stc(CQ + 7),
                        in1=stc(C_MW1), op=ALU.mult),
                     ["den1"], ["rqt1"], f"C{tt}.rt1")
                v_op(lambda e, rqc=rqc: e.tensor_scalar(
                        out=stc(rqc), in0=stc(rqc),
                        scalar1=1.0 / 127.0, scalar2=None,
                        op0=ALU.mult),
                     ["rqt1"], [f"rq1{tt % 2}"], f"C{tt}.rq1")
                v_op(lambda e, rqc=rqc: e.reciprocal(stc(CQ + 11), stc(rqc)),
                     [f"rq1{tt % 2}"], ["sr1"], f"C{tt}.sr1")

            def c_front_vs2(tt):
                emit_split(CQ + 11, "sr1", 0, "s1")
                emit_aquant(ln2b, ["ln2b"], "sx1", CQ + 9, H, 0, nxq[:], "nxq",
                            f"C{tt}.xq")

            def c_front_pe(tt):
                emit_tr(nxq, "nxq", 8, xqfm, "xqfm", lambda j: j * 128,
                        f"C{tt}.xf", per_chunk=True)

            def c_mid(tt, cb1=None, cb2=None):
                for hc in range(8):
                    bk_i = hc % 4
                    if hc == 0 and cb1 is not None:
                        cb1()
                    if hc == 4 and cb2 is not None:
                        cb2()
                    for dj in range(8):
                        t_op(lambda e, dj=dj, hc=hc, bk_i=bk_i: e.matmul(
                                ps_[bk_i][:], xqfm[:, dj * 128:(dj + 1) * 128],
                                big0[:, dj * FF + hc * 512:dj * FF + hc * 512 + 512],
                                start=(dj == 0), stop=False),
                             [f"xqfm#{dj}", "big0"], [f"ps{bk_i}"],
                             f"C{tt}.m1.{hc}.{dj}")
                    t_op(lambda e, hc=hc, bk_i=bk_i: e.matmul(
                            ps_[bk_i][:], blhs1[:],
                            b1rows[:, hc * 512:hc * 512 + 512],
                            start=False, stop=True),
                         ["blhs1", "b1rows"], [f"ps{bk_i}"], f"C{tt}.bb1.{hc}")
                    s_op(lambda e, hc=hc, bk_i=bk_i, tt=tt: e.activation(
                            out=hbuf[:, hc * 512:(hc + 1) * 512], in_=ps_[bk_i][:],
                            func=AF.Gelu, scale=stc(40 + (tt % 2))),
                         [f"ps{bk_i}", f"rq1{tt % 2}"], [f"hbuf#{hc}"],
                         f"C{tt}.g{hc}")
                    v_op(lambda e, hc=hc: e.reduce_max(
                            st2[:, 8 + hc:9 + hc],
                            hbuf[:, hc * 512:(hc + 1) * 512],
                            axis=AX.X, apply_absolute_value=True),
                         [f"hbuf#{hc}"], [f"hm{hc}"], f"C{tt}.hm{hc}")

            def c_tail(tt):
                v_op(lambda e: e.reduce_max(stc(CQ + 15), st2[:, 8:16], axis=AX.X),
                     [f"hm{h}" for h in range(8)], ["mx2"], f"C{tt}.mx2")
                v_op(lambda e: e.tensor_scalar(out=stc(CQ + 16), in0=stc(CQ + 15),
                                               scalar1=EPS, scalar2=None,
                                               op0=ALU.max),
                     ["mx2"], ["den2"], f"C{tt}.den2")
                v_op(lambda e: e.reciprocal(stc(CQ + 17), stc(CQ + 16)),
                     ["den2"], ["rd2"], f"C{tt}.rd2")
                v_op(lambda e: e.tensor_scalar(out=stc(CQ + 12), in0=stc(CQ + 17),
                                               scalar1=127.0, scalar2=None,
                                               op0=ALU.mult),
                     ["rd2"], ["sx2"], f"C{tt}.sx2")
                v_op(lambda e: e.tensor_tensor(out=stc(CQ + 13), in0=stc(CQ + 16),
                                               in1=stc(C_MW2), op=ALU.mult),
                     ["den2"], ["rqt2"], f"C{tt}.rt2")
                v_op(lambda e: e.tensor_scalar(out=stc(CQ + 13), in0=stc(CQ + 13),
                                               scalar1=1.0 / 127.0, scalar2=None,
                                               op0=ALU.mult),
                     ["rqt2"], ["rq2"], f"C{tt}.rq2")
                v_op(lambda e: e.reciprocal(stc(CQ + 14), stc(CQ + 13)),
                     ["rq2"], ["sr2"], f"C{tt}.sr2")
                emit_split(CQ + 14, "sr2", 4, "s2")

                def emit_aq_q(q):
                    emit_aquant(hbuf, [f"hbuf#{2 * q}", f"hbuf#{2 * q + 1}"],
                                "sx2", CQ + 12, 1024, q * 1024,
                                nhqd[q % 2][:], f"nhq{q % 2}",
                                f"C{tt}.hq{q}")

                def emit_hqT(q):
                    emit_tr(nhqd[q % 2], f"nhq{q % 2}", 8, hqfm, "hqfm",
                            lambda j, q=q: (q % 2) * 1024 + j * 128,
                            f"C{tt}.hf{q}", per_chunk=True)

                def emit_mm2(hf, rng=range(16)):
                    for hj16 in rng:
                        hj = hf * 16 + hj16
                        for half, bk2 in enumerate((2, 3)):
                            t_op(lambda e, hj16=hj16, hj=hj, half=half, bk2=bk2:
                                 e.matmul(ps_[bk2][:],
                                          hqfm[:, hj16 * 128:(hj16 + 1) * 128],
                                          big1[:, hj * H + half * 512:
                                               hj * H + half * 512 + 512],
                                          start=(hj == 0), stop=False),
                                 [f"hqfm#{hj16}", "big1"], [f"ps{bk2}"],
                                 f"C{tt}.m2.{hj}.{half}")

                emit_aq_q(0)
                emit_hqT(0)
                emit_aq_q(1)
                emit_mm2(0, range(8))
                emit_hqT(1)
                emit_aq_q(2)
                emit_mm2(0, range(8, 16))
                emit_hqT(2)
                emit_aq_q(3)
                emit_mm2(1, range(8))
                emit_hqT(3)
                emit_blhs(4, "s2", blhs2, "blhs2", f"C{tt}.b2")
                emit_mm2(1, range(8, 16))
                for half, bk2 in enumerate((2, 3)):
                    t_op(lambda e, half=half, bk2=bk2: e.matmul(
                            ps_[bk2][:], blhs2[:],
                            b2rows[:, half * 512:half * 512 + 512],
                            start=False, stop=True),
                         ["blhs2", "b2rows"], [f"ps{bk2}"], f"C{tt}.bb2.{half}")
                if tt + 1 < NT_Q:
                    emit_blhs(0, "s1", blhs1, "blhs1", f"C{tt + 1}.b1")

            def c_pload(tt):
                y_dma(xpart[tt % 3][:], part_d[tt * 128:(tt + 1) * 128, :],
                      [f"xinC{tt % 3}"], [f"xinC{tt % 3}"], f"C{tt}.p",
                      sem=f"yc{tt % 3}")

            def c_out(tt):
                xp, xptok = xpart[tt % 3], f"xinC{tt % 3}"
                v_op(lambda e, xp=xp: e.scalar_tensor_tensor(
                        out=xp[:, 0:512], in0=ps_[2][:], scalar=stc(CQ + 13),
                        in1=xp[:, 0:512], op0=ALU.mult, op1=ALU.add),
                     ["ps2", "rq2", xptok], [xptok], f"C{tt}.o0")
                v_op(lambda e, xp=xp: e.scalar_tensor_tensor(
                        out=xp[:, 512:H], in0=ps_[3][:], scalar=stc(CQ + 13),
                        in1=xp[:, 512:H], op0=ALU.mult, op1=ALU.add),
                     ["ps3", "rq2", xptok], [xptok], f"C{tt}.o1")
                g_dma(out_d[tt * 128:(tt + 1) * 128, :], xp[:], [xptok],
                      [xptok], f"C{tt}.st", sem=f"gc{tt % 2}")

            c_front_vs(0)
            c_front_vs2(0)
            emit_blhs(0, "s1", blhs1, "blhs1", "C0.b1")
            c_front_pe(0)
            for tt in range(NT_Q):
                if tt > 0:
                    c_out(tt - 1)
                c_mid(tt,
                      cb1=((lambda t=tt: c_front_vs(t + 1))
                           if tt + 1 < NT_Q else None),
                      cb2=((lambda t=tt: c_front_vs2(t + 1))
                           if tt + 1 < NT_Q else None))
                if tt + 1 < NT_Q:
                    c_front_pe(tt + 1)
                c_pload(tt)
                c_tail(tt)
                if debug and tt == 0:
                    g_dma(dbg["d_xnC"], xnC[:], ["xnC"], [], "dbg.xnC")
                    g_dma(dbg["d_ln2b"], ln2b[:], ["ln2b"], [], "dbg.ln2b")
                    g_dma(dbg["d_z"], zbuf[:], ["zbuf"], [], "dbg.z")
                    g_dma(dbg["d_nxq"], nxq[:], ["nxq"], [], "dbg.nxq")
                    g_dma(dbg["d_cstats"], stats[:], ["sr1"], [], "dbg.cst")
            c_out(NT_Q - 1)
            sch.finalize_phase(blkC, final=True)

    return nc


_NC_CACHE = []


def kernel(**inputs):
    X = np.ascontiguousarray(np.asarray(inputs["L0_attn_hidden_states"],
                                        dtype=np.float32))
    Wq = np.asarray(inputs["L0_attn_Wq"], dtype=np.float32)
    Wk = np.asarray(inputs["L0_attn_Wk"], dtype=np.float32)
    Wv = np.asarray(inputs["L0_attn_Wv"], dtype=np.float32)
    bq = np.asarray(inputs["L0_attn_bq"], dtype=np.float32)
    bk = np.asarray(inputs["L0_attn_bk"], dtype=np.float32)
    bv = np.asarray(inputs["L0_attn_bv"], dtype=np.float32)
    w1 = np.asarray(inputs["L0_ffn_w1"], dtype=np.float32)
    b1 = np.asarray(inputs["L0_ffn_b1"], dtype=np.float32)
    w2 = np.asarray(inputs["L0_ffn_w2"], dtype=np.float32)
    b2 = np.asarray(inputs["L0_ffn_b2"], dtype=np.float32)
    gamma = np.asarray(inputs["L0_ffn_gamma"], dtype=np.float32)
    beta = np.asarray(inputs["L0_ffn_beta"], dtype=np.float32)

    perm = np.concatenate([np.arange(0, H, 2), np.arange(1, H, 2)])
    wqT = np.ascontiguousarray(Wq.T[:, perm])
    wkT = np.ascontiguousarray(Wk.T[:, perm])
    wvT = np.ascontiguousarray(Wv.T)
    w1T = np.ascontiguousarray(w1.T)
    w2T = np.ascontiguousarray(w2.T)

    pos = np.arange(S, dtype=np.float32)
    inv = (10000.0 ** (np.arange(0, 512, 2, dtype=np.float64) / 512.0)).astype(np.float32)
    rad = (pos[:, None] / inv).astype(np.float32)
    sc_hi = np.stack([np.sin(rad.astype(np.float64)),
                      np.cos(rad.astype(np.float64))],
                     axis=-1).reshape(S, 512).astype(np.float32)

    identb = np.eye(128, dtype=BFDT)
    identf = np.eye(128, dtype=np.float32)

    in_maps = []
    for c in range(8):
        b, hh = c // 2, c % 2
        m = {
            "x_full": np.ascontiguousarray(X[b]),
            "x_own": np.ascontiguousarray(X[b, hh * HALF:(hh + 1) * HALF]),
            "wqT": wqT, "wkT": wkT, "wvT": wvT,
            "bq2": bq[perm][None, :], "bk2": bk[perm][None, :], "bv2": bv[None, :],
            "w1T": w1T, "w2T": w2T,
            "b1r": np.ascontiguousarray(b1.reshape(4, FF // 4)),
            "b2r": np.ascontiguousarray(b2.reshape(4, H // 4)),
            "gamma_bc": np.ascontiguousarray(np.broadcast_to(gamma, (128, H))),
            "beta_bc": np.ascontiguousarray(np.broadcast_to(beta, (128, H))),
            "sck_hi": sc_hi,
            "scq_hi": np.ascontiguousarray(sc_hi[hh * HALF:(hh + 1) * HALF]),
            "ident_b": identb, "ident_f": identf,
        }
        in_maps.append(m)

    if not _NC_CACHE:
        _NC_CACHE.append(build_nc())
    nc = _NC_CACHE[0]
    res = run_bass_kernel_spmd(nc, in_maps, core_ids=list(range(8)))
    out = np.empty((B, S, H), dtype=np.float32)
    for c in range(8):
        b, hh = c // 2, c % 2
        out[b, hh * HALF:(hh + 1) * HALF] = res.results[c]["out"]
    return out
